# version 04
# baseline (speedup 1.0000x reference)
import math
import contextlib
from types import SimpleNamespace
import numpy as np
import ml_dtypes
import concourse.bass as bass
import concourse.mybir as mybir
from concourse.bass_core_rust import IndirectOffsetOnAxis
from concourse.bass_utils import run_bass_kernel_spmd

F32 = mybir.dt.float32
BF16 = mybir.dt.bfloat16
I32 = mybir.dt.int32
AF = mybir.ActivationFunctionType
ALU = mybir.AluOpType
AX = mybir.AxisListType

W = 8
B = 2
TB = 4096
T = B * TB
D = 2048
TOK = T // W
NEG = -1.0e30
bfdt = ml_dtypes.bfloat16


class Sem:
    def __init__(self, es, nc, name):
        self.h = es.enter_context(nc.semaphore(name))
        self.n = 0

    def inc(self, inst, k):
        inst.then_inc(self.h, k)
        self.n += k
        return self.n


def vw(e, inst):
    e.ev.inc(inst, 1)
    e.nc.vector.wait_ge(e.ev.h, e.ev.n)


_NPFX = [""]


def sbufT(es, nc, name, shape, dtype):
    return es.enter_context(nc.sbuf_tensor(_NPFX[0] + name, shape, dtype))


def psumT(es, nc, name, shape, dtype=F32):
    return es.enter_context(nc.psum_tensor(_NPFX[0] + name, shape, dtype))


def build_masks(layer):
    p = np.arange(128)[:, None]
    f = np.arange(512)[None, :]
    tiles, idx = [], {}

    def add(key, allow):
        idx[key] = len(tiles)
        allow = np.broadcast_to(allow, (128, 512))
        tiles.append(np.where(allow, np.float32(0.0), np.float32(NEG)))

    for o in (0, 128, 256, 384):
        add(("c", o), f <= p + o)
    if layer == 0:
        for o in (1, 129, 257, 385):
            add(("w", o), f >= p + o)
            add(("ws", o), (f >= p + o) | (f < 4))
            add(("wp", o - 512), f >= p + o - 512)
            add(("wsp", o - 512), (f >= p + o - 512) | (f < 4))
        add(("s",), f < 4)
    return np.stack(tiles).astype(np.float32), idx


def chunk_plan(layer, i0b):
    qc = i0b // 512
    if layer == 1:
        return [(jc, None) for jc in range(qc)] + [(qc, ("c", i0b % 512))]
    wlo = i0b - 1023
    wbc = max(0, wlo // 512)
    plan = []
    if wbc > 0:
        plan.append((0, ("s",)))
    for jc in range(wbc, qc + 1):
        if jc == qc:
            plan.append((jc, ("c", i0b % 512)))
            continue
        o = wlo - 512 * jc
        if o >= 1:
            plan.append((jc, ("ws" if jc == 0 else "w", o)))
        elif o > -511:
            plan.append((jc, ("wsp" if jc == 0 else "wp", o)))
        else:
            plan.append((jc, None))
    return plan


def build_attn(layer, debug_qkv=False, debug_attn=False, fz=None):
    NH = 2
    KVD = 128 if layer == 0 else 256
    KT = KVD // 128
    scale = 1.0 / math.sqrt(128 if layer == 0 else 64)
    masks_np, midx = build_masks(layer)
    NM = masks_np.shape[0]
    MAXC = 4 if layer == 0 else 8
    SLAB = max(MAXC, 8) * 512

    pfx = fz["pfx"] if fz else ""
    _NPFX[0] = pfx
    nc = fz["nc"] if fz else bass.Bass("TRN2", target_bir_lowering=False, num_devices=W)
    e = SimpleNamespace()
    e.nc, e.layer, e.scale, e.midx, e.MAXC, e.KT, e.NH = nc, layer, scale, midx, MAXC, KT, NH

    x_own = fz["x_ap"] if fz else nc.dram_tensor("x_own", [D, TOK], F32, kind="ExternalInput").ap()
    wq_h = nc.dram_tensor(pfx + "wq", [D, 256], F32, kind="ExternalInput").ap()
    wk_h = nc.dram_tensor(pfx + "wk", [D, KVD], F32, kind="ExternalInput").ap()
    wv_h = nc.dram_tensor(pfx + "wv", [D, KVD], F32, kind="ExternalInput").ap()
    masks_h = nc.dram_tensor(pfx + "masks", [NM * 128, 512], F32, kind="ExternalInput").ap()
    if layer == 1:
        lam_h = nc.dram_tensor(pfx + "lam", [1, 2], F32, kind="ExternalInput").ap()
    if fz:
        e.y_h = None
        e.ytm_ap = fz["ytm"]
    else:
        e.ytm_ap = None
        e.y_h = nc.dram_tensor("y", [NH * 128, T], BF16, kind="ExternalOutput").ap()
    if debug_qkv:
        xgdb = nc.dram_tensor("xgdb", [W * D, TOK], BF16, kind="ExternalOutput").ap()
        agdb = nc.dram_tensor("agdb", [D, TOK], BF16, kind="ExternalOutput").ap()
        agdb2 = nc.dram_tensor("agdb2", [D, TOK], BF16, kind="ExternalOutput").ap()
        qdb = nc.dram_tensor("qdb", [256, T], BF16, kind="ExternalOutput").ap()
        kdb = nc.dram_tensor("kdb", [KVD, T], BF16, kind="ExternalOutput").ap()
        vdb = nc.dram_tensor("vdb", [T, KVD], BF16, kind="ExternalOutput").ap()

    if debug_attn:
        e.dbg_bd = nc.dram_tensor("dbg_bd", [128, 64], F32, kind="ExternalOutput").ap()
        e.dbg_nrm = nc.dram_tensor("dbg_nrm", [128, 64], F32, kind="ExternalOutput").ap()
        e.dbg_kmb = nc.dram_tensor("dbg_kmb", [128, 4], F32, kind="ExternalOutput").ap()
        e.dbg_p1 = nc.dram_tensor("dbg_p1", [128, 4096], BF16, kind="ExternalOutput").ap()
        e.dbg_acc = nc.dram_tensor("dbg_acc", [128, 16], F32, kind="ExternalOutput").ap()
        e.dbg_sml = nc.dram_tensor("dbg_sml", [128, 8], F32, kind="ExternalOutput").ap()
    e.debug_attn = debug_attn
    if fz:
        agi, xg = fz["agi"], fz["xg"]
    else:
        agi = nc.dram_tensor("agi", [D, TOK], BF16, kind="Internal").ap()
        xg = nc.dram_tensor("xg", [W * D, TOK], BF16, kind="Internal").ap()

    ones_i = nc.inline_tensor(np.ones((128, 128), dtype=bfdt), name=pfx + "ones_c").ap()
    ident_i = nc.inline_tensor(np.eye(128, dtype=bfdt), name=pfx + "ident_c").ap()

    es = contextlib.ExitStack()
    with es:
        wq_sb = sbufT(es, nc, "wq_sb", [128, 16 * 256], BF16)
        wk_sb = sbufT(es, nc, "wk_sb", [128, 16 * KVD], BF16)
        wv_sb = sbufT(es, nc, "wv_sb", [128, 16 * KVD], BF16)
        xk_sb = sbufT(es, nc, "xk_sb", [128, 16 * 512], BF16)
        e.q_sb = sbufT(es, nc, "q_sb", [128, NH * T], BF16)
        e.k_sb = sbufT(es, nc, "k_sb", [128, KT * T], BF16)
        e.v_sb = sbufT(es, nc, "v_sb", [128, (T // 128) * KVD], BF16)
        e.y_sb = sbufT(es, nc, "y_sb", [128, TB], BF16)
        e.p1f = sbufT(es, nc, "p1f", [128, MAXC * 512], F32)
        if layer == 1:
            e.p2f = sbufT(es, nc, "p2f", [128, MAXC * 512], F32)
        e.pn_sb = sbufT(es, nc, "pn_sb", [128, 4096], BF16)
        e.p1_sb = e.pn_sb
        e.pt_sb = sbufT(es, nc, "pt_sb", [128, 2 * 512], BF16)
        e.mask_sb = sbufT(es, nc, "mask_sb", [128, NM * 512], F32)
        e.ones_sb = sbufT(es, nc, "ones_sb", [128, 128], BF16)
        e.id_sb = sbufT(es, nc, "id_sb", [128, 128], BF16)
        e.bd_sb = sbufT(es, nc, "bd_sb", [128, 64], F32)
        e.nrm_sb = sbufT(es, nc, "nrm_sb", [128, 256], F32)
        e.acc_sb = sbufT(es, nc, "acc_sb", [128, 2 * MAXC], F32)
        e.sml_sb = sbufT(es, nc, "sml_sb", [128, 8], F32)
        e.km_sb = sbufT(es, nc, "km_sb", [1, 8], F32)
        e.kmb16 = sbufT(es, nc, "kmb16", [1, 8], BF16)
        e.kmb_sb = sbufT(es, nc, "kmb_sb", [128, 4], F32)
        e.o32_sb = sbufT(es, nc, "o32_sb", [128, 128], F32)
        e.lamb_sb = sbufT(es, nc, "lamb_sb", [128, 2], F32)
        e.fl_sb = sbufT(es, nc, "fl_sb", [128, 2], F32)
        e.ytp_sb = sbufT(es, nc, "ytp_sb", [128, 256], BF16)

        e.ps_a = psumT(es, nc, "ps_a", [128, 512], F32)
        e.ps_b = psumT(es, nc, "ps_b", [128, 512], F32)
        e.ps_c = psumT(es, nc, "ps_c", [128, 512], F32)
        e.ps_d = psumT(es, nc, "ps_d", [128, 512], F32)
        e.ps_p0 = psumT(es, nc, "ps_p0", [128, 512], BF16)
        e.ps_p1 = psumT(es, nc, "ps_p1", [128, 512], BF16)
        e.ps_y = psumT(es, nc, "ps_y", [128, 128], F32)
        e.ps_n = psumT(es, nc, "ps_n", [128, 64], F32)

        if fz:
            sm = fz["sems"]
            ld, cv, ag, xt = sm["ld"], sm["cv"], sm["ag"], sm["xt"]
            e.mm = mm = sm["mm"]; e.ev = ev = sm["ev"]
            e.ex = ex = sm["ex"]; e.so = so = sm["so"]
        else:
            ld = Sem(es, nc, "ld")
            cv = Sem(es, nc, "cv")
            ag = Sem(es, nc, "ag")
            xt = Sem(es, nc, "xt")
            e.mm = mm = Sem(es, nc, "mm")
            e.ev = ev = Sem(es, nc, "ev")
            e.ex = ex = Sem(es, nc, "ex")
            e.so = so = Sem(es, nc, "so")

        sy, gg, ve, sc, te = nc.sync, nc.gpsimd, nc.vector, nc.scalar, nc.tensor

        ld.inc(gg.dma_start(out=wq_sb[:].rearrange("p (c o) -> p c o", c=16), in_=wq_h.rearrange("(c p) o -> p c o", p=128)), 16)
        ld.inc(gg.dma_start(out=wk_sb[:].rearrange("p (c o) -> p c o", c=16), in_=wk_h.rearrange("(c p) o -> p c o", p=128)), 16)
        ld.inc(gg.dma_start(out=wv_sb[:].rearrange("p (c o) -> p c o", c=16), in_=wv_h.rearrange("(c p) o -> p c o", p=128)), 16)
        ld.inc(sy.dma_start(out=e.mask_sb[:].rearrange("p (m f) -> p m f", f=512),
                            in_=masks_h.rearrange("(m p) f -> p m f", p=128)), 16)
        ld.inc(sy.dma_start(out=e.ones_sb[:], in_=ones_i[:, :]), 16)
        ld.inc(sy.dma_start(out=e.id_sb[:], in_=ident_i[:, :]), 16)
        if layer == 1:
            ld.inc(sy.dma_start(out=e.km_sb[0:1, 0:2], in_=lam_h[:, :]), 16)
        cv.inc(gg.dma_start(out=agi[:, :], in_=x_own[:, :]), 16)
        if debug_qkv:
            sy.wait_ge(cv.h, cv.n)
            ag.inc(sy.dma_start(out=agdb[:, :], in_=agi[:, :]), 16)
            gg.wait_ge(ag.h, ag.n)
        gg.wait_ge(cv.h, cv.n)
        ag.inc(gg.collective_compute("AllGather", ALU.bypass,
                                     replica_groups=[list(range(W))],
                                     ins=[agi[:, :]], outs=[xg[:, :]]), 1)
        sy.wait_ge(ag.h, ag.n)

        def xg_win(tt):
            r, cs = tt // 2, 512 * (tt % 2)
            return (xg[r * D:(r + 1) * D, cs:cs + 512]
                    .rearrange("(c p) t -> p c t", p=128))

        for tt in range(16):
            if tt > 0:
                sy.wait_ge(ev.h, ev.n)
            xt.inc(sy.dma_start(out=xk_sb[:].rearrange("p (c t) -> p c t", c=16), in_=xg_win(tt)), 16)
            te.wait_ge(xt.h, xt.n)
            rot = 0
            for dst_sb, wsb, no in ((e.q_sb, wq_sb, 2), (e.k_sb, wk_sb, KT)):
                for oh in range(no):
                    ps = (e.ps_a, e.ps_b)[rot % 2]
                    rot += 1
                    te.wait_ge(ev.h, ev.n)
                    for kc in range(16):
                        inst = te.matmul(ps[:, :],
                                         wsb[:, kc * (no * 128) + oh * 128:
                                             kc * (no * 128) + oh * 128 + 128],
                                         xk_sb[:, kc * 512:(kc + 1) * 512],
                                         start=(kc == 0), stop=(kc == 15))
                    mm.inc(inst, 1)
                    ve.wait_ge(mm.h, mm.n)
                    ev.inc(ve.tensor_copy(dst_sb[:, oh * T + tt * 512:
                                                 oh * T + tt * 512 + 512], ps[:, :]), 1)
            for ts in range(4):
                ps = (e.ps_c, e.ps_d)[ts % 2]
                te.wait_ge(ev.h, ev.n)
                for kc in range(16):
                    inst = te.matmul(ps[:, 0:KVD],
                                     xk_sb[:, kc * 512 + ts * 128:
                                           kc * 512 + ts * 128 + 128],
                                     wv_sb[:, kc * KVD:(kc + 1) * KVD],
                                     start=(kc == 0), stop=(kc == 15))
                mm.inc(inst, 1)
                ve.wait_ge(mm.h, mm.n)
                tg = tt * 4 + ts
                ev.inc(ve.tensor_copy(e.v_sb[:, tg * KVD:(tg + 1) * KVD],
                                      ps[:, 0:KVD]), 1)

        if debug_qkv:
            ev.inc(ve.tensor_copy(e.fl_sb[:, 0:1], e.fl_sb[:, 1:2]), 1)
            gg.wait_ge(ev.h, ev.n)
            so.inc(gg.dma_start(out=xgdb[:, :], in_=xg[:, :]), 16)
            so.inc(gg.dma_start(out=agdb2[:, :], in_=agi[:, :]), 16)
            so.inc(gg.dma_start(out=qdb.rearrange("(h p) t -> p h t", p=128),
                                in_=e.q_sb[:, :].rearrange("p (h t) -> p h t", t=T)), 16)
            so.inc(gg.dma_start(out=kdb.rearrange("(h p) t -> p h t", p=128),
                                in_=e.k_sb[:, :].rearrange("p (h t) -> p h t", t=T)), 16)
            so.inc(gg.dma_start(out=vdb.rearrange("(g p) o -> p g o", p=128),
                                in_=e.v_sb[:, :].rearrange("p (g o) -> p g o", g=T // 128)), 16)
            gg.wait_ge(so.h, so.n)
            return nc

        if layer == 1:
            sc.wait_ge(ld.h, ld.n)
            ex.inc(sc.activation(e.km_sb[0:1, 2:4], e.km_sb[0:1, 0:2], AF.Sigmoid), 1)
            ve.wait_ge(ex.h, ex.n)
            vw(e, ve.memset(e.o32_sb[0:1, :], 1.0))
            te.wait_ge(ev.h, ev.n)
            mm.inc(te.matmul(e.ps_n[:, 0:2], e.o32_sb[0:1, :], e.km_sb[0:1, 2:4],
                             start=True, stop=True), 1)
            ve.wait_ge(mm.h, mm.n)
            vw(e, ve.tensor_copy(e.lamb_sb[:, 0:2], e.ps_n[:, 0:2]))

        for b in range(B):
            for hl in range(NH):
                emit_attn_bh(e, b, hl)

        gg.wait_ge(so.h, so.n)
    return nc


def emit_attn_bh(e, b, hl):
    nc, layer = e.nc, e.layer
    sy, gg, ve, sc, te = nc.sync, nc.gpsimd, nc.vector, nc.scalar, nc.tensor
    mm, ev, ex, so = e.mm, e.ev, e.ex, e.so
    qcol = hl * T + b * TB
    kcol = (hl * T + b * TB) if layer == 1 else b * TB
    halves = ((0,),) if layer == 0 else ((0,), (64,))
    nsets = len(halves)
    MAXC, scale = e.MAXC, e.scale

    sc.wait_ge(mm.h, mm.n)
    ve.wait_ge(so.h, so.n)
    sc.wait_ge(ev.h, ev.n)
    ex.inc(sc.activation(e.p1_sb[:, 0:TB], e.q_sb[:, qcol:qcol + TB], AF.Square), 1)
    te.wait_ge(ex.h, ex.n)
    for si, (base,) in enumerate(halves):
        kw = {} if base == 0 else {"tile_position": (64, 0)}
        npart = 128 // nsets
        for qt in range(32):
            inst = te.matmul(e.ps_n[:, si * 32 + qt:si * 32 + qt + 1],
                             e.p1_sb[base:base + npart, qt * 128:(qt + 1) * 128],
                             e.ones_sb[base:base + npart, 0:1],
                             start=True, stop=True, **kw)
        mm.inc(inst, 1)
    ve.wait_ge(mm.h, mm.n)
    vw(e, ve.tensor_copy(e.nrm_sb[:, 0:64], e.ps_n[:, 0:64]))
    sc.wait_ge(ev.h, ev.n)
    ex.inc(sc.activation(e.p1_sb[:, 0:TB], e.k_sb[:, kcol:kcol + TB], AF.Square), 1)
    te.wait_ge(ex.h, ex.n)
    te.wait_ge(ev.h, ev.n)
    for si, (base,) in enumerate(halves):
        kw = {} if base == 0 else {"tile_position": (64, 0)}
        npart = 128 // nsets
        for jt in range(32):
            inst = te.matmul(e.ps_n[:, si * 32 + jt:si * 32 + jt + 1],
                             e.p1_sb[base:base + npart, jt * 128:(jt + 1) * 128],
                             e.ones_sb[base:base + npart, 0:1],
                             start=True, stop=True, **kw)
        mm.inc(inst, 1)
    ve.wait_ge(mm.h, mm.n)
    vw(e, ve.tensor_copy(e.nrm_sb[:, 64:128], e.ps_n[:, 0:64]))
    for si in range(nsets):
        vw(e, ve.reduce_max(e.sml_sb[:, si:si + 1],
                            e.nrm_sb[:, 64 + si * 32:64 + si * 32 + 32], axis=AX.X))
    vw(e, ve.memset(e.p1_sb[:, 0:128], NEG))
    for si in range(nsets):
        vw(e, ve.tensor_copy(e.p1_sb[:, si * 64:si * 64 + 1],
                             e.sml_sb[:, si:si + 1]))
    te.wait_ge(ev.h, ev.n)
    mm.inc(te.transpose(e.ps_p0[:, 0:128], e.p1_sb[:, 0:128], e.id_sb[:, :]), 1)
    ve.wait_ge(mm.h, mm.n)
    for si in range(nsets):
        vw(e, ve.tensor_copy(e.nrm_sb[si * 64:si * 64 + 1, 128:256],
                             e.ps_p0[si * 64:si * 64 + 1, 0:128]))
        vw(e, ve.reduce_max(e.km_sb[0:1, 4 + si:5 + si],
                            e.nrm_sb[si * 64:si * 64 + 1, 128:256], axis=AX.X))
    sc.wait_ge(ev.h, ev.n)
    ex.inc(sc.activation(e.km_sb[0:1, 6:6 + nsets], e.km_sb[0:1, 4:4 + nsets], AF.Sqrt), 1)
    ve.wait_ge(ex.h, ex.n)
    vw(e, ve.tensor_copy(e.kmb16[0:1, 6:6 + nsets], e.km_sb[0:1, 6:6 + nsets]))
    te.wait_ge(ev.h, ev.n)
    mm.inc(te.matmul(e.ps_n[:, 0:nsets], e.ones_sb[0:1, :], e.kmb16[0:1, 6:6 + nsets],
                     start=True, stop=True), 1)
    ve.wait_ge(mm.h, mm.n)
    vw(e, ve.tensor_copy(e.kmb_sb[:, 0:nsets], e.ps_n[:, 0:nsets]))
    sc.wait_ge(ev.h, ev.n)
    ex.inc(sc.activation(e.nrm_sb[:, 0:64], e.nrm_sb[:, 0:64], AF.Sqrt), 1)
    ve.wait_ge(ex.h, ex.n)
    for si in range(nsets):
        vw(e, ve.tensor_scalar(e.bd_sb[:, si * 32:si * 32 + 32],
                               e.nrm_sb[:, si * 32:si * 32 + 32],
                               e.kmb_sb[:, si:si + 1], -scale * 1.01,
                               ALU.mult, ALU.mult))

    if e.debug_attn and b == 0 and hl == 0:
        ev.inc(ve.tensor_copy(e.fl_sb[:, 0:1], e.fl_sb[:, 1:2]), 1)
        gg.wait_ge(ev.h, ev.n)
        so.inc(gg.dma_start(out=e.dbg_bd[:, :], in_=e.bd_sb[:, :]), 16)
        so.inc(gg.dma_start(out=e.dbg_nrm[:, :], in_=e.nrm_sb[:, 0:64]), 16)
        so.inc(gg.dma_start(out=e.dbg_kmb[:, :], in_=e.kmb_sb[:, :]), 16)
    for qt in range(32):
        i0b = qt * 128
        plan = chunk_plan(layer, i0b)
        nj = len(plan)
        for ci, (jc, mk) in enumerate(plan):
            s1p = (e.ps_a, e.ps_b)[ci % 2]
            s2p = (e.ps_c, e.ps_d)[ci % 2]
            pss = (s1p,) if layer == 0 else (s1p, s2p)
            te.wait_ge(ex.h, ex.n)
            for si, (base,) in enumerate(halves):
                kw = {} if base == 0 else {"tile_position": (64, 0)}
                npart = 128 // nsets
                inst = te.matmul(pss[si][:, :],
                                 e.q_sb[base:base + npart, qcol + i0b:qcol + i0b + 128],
                                 e.k_sb[base:base + npart,
                                        kcol + jc * 512:kcol + jc * 512 + 512],
                                 start=True, stop=True, **kw)
                mm.inc(inst, 1)
            if mk is not None:
                mi = e.midx[mk]
                ve.wait_ge(mm.h, mm.n)
                for ps in pss:
                    ev.inc(ve.tensor_tensor(ps[:, :], ps[:, :],
                                            e.mask_sb[:, mi * 512:(mi + 1) * 512],
                                            ALU.add), 1)
            sc.wait_ge(mm.h, mm.n)
            sc.wait_ge(ev.h, ev.n)
            for si, ps in enumerate(pss):
                slab = (e.p1f, e.p2f if layer == 1 else None)[si]
                ex.inc(sc.activation(slab[:, ci * 512:(ci + 1) * 512], ps[:, :],
                                     AF.Exp,
                                     bias=e.bd_sb[:, si * 32 + qt:si * 32 + qt + 1],
                                     scale=scale,
                                     accum_out=e.acc_sb[:, si * MAXC + ci:
                                                        si * MAXC + ci + 1]), 1)
        ve.wait_ge(ex.h, ex.n)
        vw(e, ve.reduce_sum(e.sml_sb[:, 0:1], e.acc_sb[:, 0:nj], axis=AX.X))
        vw(e, ve.reciprocal(e.sml_sb[:, 1:2], e.sml_sb[:, 0:1]))
        if layer == 0:
            ev.inc(ve.tensor_scalar(e.pn_sb[:, 0:nj * 512], e.p1f[:, 0:nj * 512],
                                    e.sml_sb[:, 1:2], None, ALU.mult), 1)
        else:
            vw(e, ve.reduce_sum(e.sml_sb[:, 2:3], e.acc_sb[:, MAXC:MAXC + nj],
                                axis=AX.X))
            vw(e, ve.reciprocal(e.sml_sb[:, 3:4], e.sml_sb[:, 2:3]))
            vw(e, ve.tensor_tensor(e.sml_sb[:, 4:5], e.sml_sb[:, 0:1],
                                   e.sml_sb[:, 3:4], ALU.mult))
            vw(e, ve.tensor_scalar(e.sml_sb[:, 4:5], e.sml_sb[:, 4:5],
                                   e.lamb_sb[:, hl:hl + 1], -1.0,
                                   ALU.mult, ALU.mult))
            ev.inc(ve.scalar_tensor_tensor(e.p2f[:, 0:nj * 512],
                                           e.p2f[:, 0:nj * 512],
                                           e.sml_sb[:, 4:5],
                                           e.p1f[:, 0:nj * 512],
                                           ALU.mult, ALU.add), 1)
            ve.wait_ge(ev.h, ev.n)
            ev.inc(ve.tensor_scalar(e.pn_sb[:, 0:nj * 512], e.p2f[:, 0:nj * 512],
                                    e.sml_sb[:, 1:2], None, ALU.mult), 1)
        if e.debug_attn and b == 0 and hl == 0 and qt == 2:
            ev.inc(ve.tensor_copy(e.fl_sb[:, 0:1], e.fl_sb[:, 1:2]), 1)
            gg.wait_ge(ev.h, ev.n)
            so.inc(gg.dma_start(out=e.dbg_p1[:, 0:nj * 512], in_=e.p1_sb[:, 0:nj * 512]), 16)
            so.inc(gg.dma_start(out=e.dbg_acc[:, 0:2 * e.MAXC], in_=e.acc_sb[:, :]), 16)
            so.inc(gg.dma_start(out=e.dbg_sml[:, :], in_=e.sml_sb[:, :]), 16)
            gg.wait_ge(so.h, so.n)
        first = True
        for ci, (jc, mk) in enumerate(plan):
            pt_ps = (e.ps_p0, e.ps_p1)[ci % 2]
            pt_slot = e.pt_sb[:, (ci % 2) * 512:(ci % 2) * 512 + 512]
            te.wait_ge(ev.h, ev.n)
            for jt in range(4):
                inst = te.transpose(pt_ps[:, jt * 128:(jt + 1) * 128],
                                    e.pn_sb[:, ci * 512 + jt * 128:
                                            ci * 512 + jt * 128 + 128],
                                    e.id_sb[:, :])
            mm.inc(inst, 1)
            ve.wait_ge(mm.h, mm.n)
            ev.inc(ve.tensor_copy(pt_slot, pt_ps[:, :]), 1)
            te.wait_ge(ev.h, ev.n)
            hofs = hl * 128 if layer == 1 else 0
            for jt in range(4):
                tg = b * 32 + jc * 4 + jt
                inst = te.matmul(e.ps_y[:, :],
                                 e.v_sb[:, tg * (e.KT * 128) + hofs:
                                        tg * (e.KT * 128) + hofs + 128],
                                 pt_slot[:, jt * 128:(jt + 1) * 128],
                                 start=first, stop=(ci == nj - 1 and jt == 3))
                first = False
            mm.inc(inst, 1)
        ve.wait_ge(mm.h, mm.n)
        ev.inc(ve.tensor_copy(e.y_sb[:, i0b:i0b + 128], e.ps_y[:, :]), 1)
    if e.ytm_ap is None:
        ev.inc(ve.tensor_copy(e.fl_sb[:, 0:1], e.fl_sb[:, 1:2]), 1)
        gg.wait_ge(ev.h, ev.n)
        so.inc(gg.dma_start(out=e.y_h[hl * 128:(hl + 1) * 128, b * TB:(b + 1) * TB],
                            in_=e.y_sb[:, :]), 16)
    else:
        for tt in range(32):
            pt_ps = (e.ps_p0, e.ps_p1)[tt % 2]
            slot = (tt % 2) * 128
            te.wait_ge(ev.h, ev.n)
            mm.inc(te.transpose(pt_ps[:, 0:128], e.y_sb[:, tt * 128:(tt + 1) * 128],
                                e.id_sb[:, :]), 1)
            ve.wait_ge(mm.h, mm.n)
            ve.wait_ge(so.h, so.n)
            vw(e, ve.tensor_copy(e.ytp_sb[:, slot:slot + 128], pt_ps[:, 0:128]))
            gg.wait_ge(ev.h, ev.n)
            so.inc(gg.dma_start(
                out=e.ytm_ap[b * TB + tt * 128:b * TB + (tt + 1) * 128,
                             hl * 128:(hl + 1) * 128],
                in_=e.ytp_sb[:, slot:slot + 128]), 16)


def host_attn_inputs(layer, inp):
    x = np.asarray(inp["L0_attn_x"], dtype=np.float32).reshape(T, D)
    masks_np, _ = build_masks(layer)
    masks2 = np.ascontiguousarray(masks_np.reshape(-1, 512))
    maps = []
    for c in range(W):
        m = {"masks": masks2,
             "x_own": np.ascontiguousarray(x[c * TOK:(c + 1) * TOK].T)}
        if layer == 0:
            kv = c // 2
            m["wq"] = np.ascontiguousarray(np.asarray(inp["L0_attn_Wq"])[256 * c:256 * (c + 1)].T)
            m["wk"] = np.ascontiguousarray(np.asarray(inp["L0_attn_Wk"])[128 * kv:128 * (kv + 1)].T)
            m["wv"] = np.ascontiguousarray(np.asarray(inp["L0_attn_Wv"])[128 * kv:128 * (kv + 1)].T)
        else:
            m["wq"] = np.ascontiguousarray(np.asarray(inp["L1_attn_Wq"])[:, 256 * c:256 * (c + 1)])
            m["wk"] = np.ascontiguousarray(np.asarray(inp["L1_attn_Wk"])[:, 256 * c:256 * (c + 1)])
            m["wv"] = np.ascontiguousarray(np.asarray(inp["L1_attn_Wv"])[:, 256 * c:256 * (c + 1)])
            m["lam"] = np.ascontiguousarray(
                np.asarray(inp["L1_attn_lambda_init"])[2 * c:2 * c + 2].reshape(1, 2))
        maps.append(m)
    return maps




def emit_ln(e, out_h):
    nc = e.nc
    sy, gg, ve, sc, te = nc.sync, nc.gpsimd, nc.vector, nc.scalar, nc.tensor
    mm, ev, ex, so = e.mm, e.ev, e.ex, e.so
    vw(e, ve.memset(e.o32_sb[:, :], 1.0))
    vw(e, ve.memset(e.st_sb[0:1, 0:2048], 0.0))
    te.wait_ge(ev.h, ev.n)
    for dt in range(16):
        sc.wait_ge(ev.h, ev.n)
        ex.inc(sc.activation(e.sq1_sb[:, 0:1024], e.xr_sb[:, dt * 1024:(dt + 1) * 1024],
                             AF.Square), 1)
        te.wait_ge(ex.h, ex.n)
        te.wait_ge(ev.h, ev.n)
        for th in range(2):
            inst = te.matmul((e.ps_c, e.ps_d)[th][0:1, 0:512], e.o32_sb[:, 0:1],
                             e.xr_sb[:, dt * 1024 + th * 512:dt * 1024 + th * 512 + 512],
                             start=True, stop=True)
            inst = te.matmul((e.ps_a, e.ps_b)[th][0:1, 0:512], e.o32_sb[:, 0:1],
                             e.sq1_sb[:, th * 512:th * 512 + 512],
                             start=True, stop=True)
        mm.inc(inst, 1)
        ve.wait_ge(mm.h, mm.n)
        vw(e, ve.tensor_tensor(e.st_sb[0:1, 0:512], e.st_sb[0:1, 0:512],
                               e.ps_c[0:1, 0:512], ALU.add))
        vw(e, ve.tensor_tensor(e.st_sb[0:1, 512:1024], e.st_sb[0:1, 512:1024],
                               e.ps_d[0:1, 0:512], ALU.add))
        vw(e, ve.tensor_tensor(e.st_sb[0:1, 1024:1536], e.st_sb[0:1, 1024:1536],
                               e.ps_a[0:1, 0:512], ALU.add))
        vw(e, ve.tensor_tensor(e.st_sb[0:1, 1536:2048], e.st_sb[0:1, 1536:2048],
                               e.ps_b[0:1, 0:512], ALU.add))
    vw(e, ve.tensor_scalar(e.st_sb[0:1, 0:1024], e.st_sb[0:1, 0:1024],
                           1.0 / D, None, ALU.mult))
    vw(e, ve.tensor_scalar(e.st_sb[0:1, 1024:2048], e.st_sb[0:1, 1024:2048],
                           1.0 / D, None, ALU.mult))
    vw(e, ve.tensor_tensor(e.st_sb[0:1, 2048:3072], e.st_sb[0:1, 0:1024],
                           e.st_sb[0:1, 0:1024], ALU.mult))
    vw(e, ve.tensor_tensor(e.st_sb[0:1, 1024:2048], e.st_sb[0:1, 1024:2048],
                           e.st_sb[0:1, 2048:3072], ALU.subtract))
    vw(e, ve.tensor_scalar(e.st_sb[0:1, 1024:2048], e.st_sb[0:1, 1024:2048],
                           1e-5, None, ALU.add))
    sc.wait_ge(ev.h, ev.n)
    ex.inc(sc.activation(e.st_sb[0:1, 2048:3072], e.st_sb[0:1, 1024:2048],
                         AF.Sqrt), 1)
    ve.wait_ge(ex.h, ex.n)
    vw(e, ve.reciprocal(e.st_sb[0:1, 2048:3072], e.st_sb[0:1, 2048:3072]))
    vw(e, ve.tensor_tensor(e.st_sb[0:1, 3072:4096], e.st_sb[0:1, 0:1024],
                           e.st_sb[0:1, 2048:3072], ALU.mult))
    vw(e, ve.tensor_scalar(e.st_sb[0:1, 3072:4096], e.st_sb[0:1, 3072:4096],
                           -1.0, None, ALU.mult))
    vw(e, ve.memset(e.o32_sb[0:1, :], 1.0))
    te.wait_ge(ev.h, ev.n)
    mm.inc(te.matmul(e.ps_ba[:, 0:512], e.o32_sb[0:1, :], e.st_sb[0:1, 2048:2560],
                     start=True, stop=True), 1)
    mm.inc(te.matmul(e.ps_bb[:, 0:512], e.o32_sb[0:1, :], e.st_sb[0:1, 2560:3072],
                     start=True, stop=True), 1)
    mm.inc(te.matmul(e.ps_bc[:, 0:512], e.o32_sb[0:1, :], e.st_sb[0:1, 3072:3584],
                     start=True, stop=True), 1)
    mm.inc(te.matmul(e.ps_bd[:, 0:512], e.o32_sb[0:1, :], e.st_sb[0:1, 3584:4096],
                     start=True, stop=True), 1)
    ve.wait_ge(mm.h, mm.n)
    for dt in range(16):
        ve.wait_ge(so.h, so.n)
        for th in range(2):
            Ab = (e.ps_ba, e.ps_bb)[th]
            Bb = (e.ps_bc, e.ps_bd)[th]
            cols = slice(dt * 1024 + th * 512, dt * 1024 + th * 512 + 512)
            vw(e, ve.tensor_tensor(e.xo_sb[:, th * 512:th * 512 + 512],
                                   e.xr_sb[:, cols], Ab[:, 0:512], ALU.mult))
            vw(e, ve.tensor_tensor(e.xo_sb[:, th * 512:th * 512 + 512],
                                   e.xo_sb[:, th * 512:th * 512 + 512],
                                   Bb[:, 0:512], ALU.add))
        gg.wait_ge(ev.h, ev.n)
        so.inc(gg.dma_start(out=out_h[dt * 128:(dt + 1) * 128, :],
                            in_=e.xo_sb[:, 0:1024]), 16)


def build_o_ln(fz=None):
    pfx = fz["pfx"] if fz else ""
    _NPFX[0] = pfx
    nc = fz["nc"] if fz else bass.Bass("TRN2", target_bir_lowering=False, num_devices=W)
    e = SimpleNamespace()
    e.nc = nc
    if fz:
        y_h = None
        yidx_h = nc.dram_tensor(pfx + "yidx", [128, 64], I32, kind="ExternalInput").ap()
        x_h = fz["x_ap"]
        xn_h = fz["out_ap"]
    else:
        y_h = nc.dram_tensor("y", [D, TOK], BF16, kind="ExternalInput").ap()
        x_h = nc.dram_tensor("x_own", [D, TOK], F32, kind="ExternalInput").ap()
        xn_h = nc.dram_tensor("xn", [D, TOK], F32, kind="ExternalOutput").ap()
    wo_h = nc.dram_tensor(pfx + "wo", [D, D], F32, kind="ExternalInput").ap()
    ones_i = nc.inline_tensor(np.ones((128, 128), dtype=bfdt), name=pfx + "ones_c").ap()
    ident_i = nc.inline_tensor(np.eye(128, dtype=bfdt), name=pfx + "ident_c").ap()

    es = contextlib.ExitStack()
    with es:
        wo_sb = sbufT(es, nc, "wo_sb", [128, 16 * D], BF16)
        yk_sb = sbufT(es, nc, "yk_sb", [128, 16 * TOK], BF16)
        e.xr_sb = sbufT(es, nc, "xr_sb", [128, 16 * TOK], F32)
        e.xo_sb = sbufT(es, nc, "xo_sb", [128, TOK], F32)
        e.xb1_sb = sbufT(es, nc, "xb1_sb", [128, TOK], F32)
        e.sq1_sb = sbufT(es, nc, "sq1_sb", [128, TOK], F32)
        e.st_sb = sbufT(es, nc, "st_sb", [1, 4096], F32)
        e.o32_sb = sbufT(es, nc, "o32_sb", [128, 128], F32)
        e.ones_sb = sbufT(es, nc, "ones_sb", [128, 128], BF16)
        e.fl_sb = sbufT(es, nc, "fl_sb", [128, 2], F32)
        e.ps_a = psumT(es, nc, "ps_a", [128, 512], F32)
        e.ps_b = psumT(es, nc, "ps_b", [128, 512], F32)
        e.ps_c = psumT(es, nc, "ps_c", [128, 512], F32)
        e.ps_d = psumT(es, nc, "ps_d", [128, 512], F32)
        e.ps_ba = psumT(es, nc, "ps_ba", [128, 512], F32)
        e.ps_bb = psumT(es, nc, "ps_bb", [128, 512], F32)
        e.ps_bc = psumT(es, nc, "ps_bc", [128, 512], F32)
        e.ps_bd = psumT(es, nc, "ps_bd", [128, 512], F32)

        if fz:
            sm = fz["sems"]
            ld, ag, xt = sm["ld"], sm["ag"], sm["xt"]
            e.mm = mm = sm["mm"]; e.ev = ev = sm["ev"]
            e.ex = ex = sm["ex"]; e.so = so = sm["so"]
        else:
            ld = Sem(es, nc, "ld")
            e.mm = mm = Sem(es, nc, "mm")
            e.ev = ev = Sem(es, nc, "ev")
            e.ex = ex = Sem(es, nc, "ex")
            e.so = so = Sem(es, nc, "so")
        sy, gg, ve, sc, te = nc.sync, nc.gpsimd, nc.vector, nc.scalar, nc.tensor

        ld.inc(gg.dma_start(out=wo_sb[:].rearrange("p (c o) -> p c o", c=16),
                            in_=wo_h.rearrange("(c p) o -> p c o", p=128)), 16)
        if not fz:
            ld.inc(gg.dma_start(out=yk_sb[:].rearrange("p (c t) -> p c t", c=16),
                                in_=y_h.rearrange("(c p) t -> p c t", p=128)), 16)
        ld.inc(sy.dma_start(out=e.xr_sb[:].rearrange("p (c t) -> p c t", c=16),
                            in_=x_h.rearrange("(c p) t -> p c t", p=128)), 16)
        ld.inc(sy.dma_start(out=e.ones_sb[:], in_=ones_i[:, :]), 16)
        if fz:
            id_sb = sbufT(es, nc, "id_sb", [128, 128], BF16)
            yidx_sb = sbufT(es, nc, "yidx_sb", [128, 64], I32)
            ygb_sb = sbufT(es, nc, "ygb_sb", [128, 256], BF16)
            ps_p0 = e.ps_a[:, 0:64].bitcast(BF16)
            ps_p1 = e.ps_b[:, 0:64].bitcast(BF16)
            ld.inc(sy.dma_start(out=id_sb[:], in_=ident_i[:, :]), 16)
            ld.inc(sy.dma_start(out=yidx_sb[:], in_=yidx_h[:, :]), 16)
            gg.wait_ge(e.so.h, e.so.n)
            ag.inc(gg.collective_compute("AllGather", ALU.bypass,
                                         replica_groups=[list(range(W))],
                                         ins=[fz["ytm"][:, :]],
                                         outs=[fz["ygath"][:, :]]), 1)
            gg.wait_ge(ag.h, ag.n)
            gg.wait_ge(ld.h, ld.n)
            for r in range(W):
                for tt in range(8):
                    col = r * 8 + tt
                    gg.wait_ge(ev.h, ev.n)
                    xt.inc(gg.indirect_dma_start(
                        ygb_sb[:, :], None, fz["ygath"][:, :],
                        IndirectOffsetOnAxis(yidx_sb[:, col:col + 1], 0)), 16)
                    te.wait_ge(xt.h, xt.n)
                    te.wait_ge(ev.h, ev.n)
                    for half in range(2):
                        pt = (ps_p0, ps_p1)[half]
                        mm.inc(te.transpose(pt, ygb_sb[:, half * 128:(half + 1) * 128],
                                            id_sb[:, :]), 1)
                    ve.wait_ge(mm.h, mm.n)
                    for half in range(2):
                        pt = (ps_p0, ps_p1)[half]
                        kc = 2 * r + half
                        ev.inc(ve.tensor_copy(
                            yk_sb[:, kc * TOK + tt * 128:kc * TOK + tt * 128 + 128],
                            pt), 1)

        te.wait_ge(ld.h, ld.n)
        te.wait_ge(ev.h, ev.n)
        for dt in range(16):
            for th in range(2):
                ps = (e.ps_a, e.ps_b)[th]
                te.wait_ge(ev.h, ev.n)
                for kc in range(16):
                    inst = te.matmul(ps[:, :],
                                     wo_sb[:, kc * D + dt * 128:kc * D + dt * 128 + 128],
                                     yk_sb[:, kc * TOK + th * 512:kc * TOK + th * 512 + 512],
                                     start=(kc == 0), stop=(kc == 15))
                mm.inc(inst, 1)
                ve.wait_ge(mm.h, mm.n)
                cols = slice(dt * 1024 + th * 512, dt * 1024 + th * 512 + 512)
                ev.inc(ve.tensor_tensor(e.xr_sb[:, cols], e.xr_sb[:, cols],
                                        ps[:, :], ALU.add), 1)
        emit_ln(e, xn_h)
        gg.wait_ge(so.h, so.n)
    return nc


def build_ffn(layer, debug_ffn=False, fz=None):
    NS = 1024 if layer == 0 else 2048
    NSUB = NS // 1024
    pfx = fz["pfx"] if fz else ""
    _NPFX[0] = pfx
    nc = fz["nc"] if fz else bass.Bass("TRN2", target_bir_lowering=False, num_devices=W)
    e = SimpleNamespace()
    e.nc = nc
    x_h = fz["x_ap"] if fz else nc.dram_tensor("x_own", [D, TOK], F32, kind="ExternalInput").ap()
    if layer == 0:
        wg_h = nc.dram_tensor(pfx + "wg", [D, NS], I32, kind="ExternalInput").ap()
        wu_h = nc.dram_tensor(pfx + "wu", [D, NS], I32, kind="ExternalInput").ap()
        wd_h = nc.dram_tensor(pfx + "wd", [NS, D], I32, kind="ExternalInput").ap()
        gsc_h = nc.dram_tensor(pfx + "gsc", [128, 8], F32, kind="ExternalInput").ap()
        usc_h = nc.dram_tensor(pfx + "usc", [128, 8], F32, kind="ExternalInput").ap()
        dsc_h = nc.dram_tensor(pfx + "dsc", [128, 16], F32, kind="ExternalInput").ap()
    elif fz:
        wg_h, wu_h, wd_h = fz["wg_ap"], fz["wu_ap"], fz["wd_ap"]
    else:
        wg_h = nc.dram_tensor("wg", [D, NS], BF16, kind="ExternalInput").ap()
        wu_h = nc.dram_tensor("wu", [D, NS], BF16, kind="ExternalInput").ap()
        wd_h = nc.dram_tensor("wd", [NS, D], BF16, kind="ExternalInput").ap()
    xn_h = fz["out_ap"] if fz else nc.dram_tensor("xn", [D, TOK], F32, kind="ExternalOutput").ap()
    if debug_ffn:
        drs_h = nc.dram_tensor("drs", [D, TOK], F32, kind="ExternalOutput").ap()
        dit_h = nc.dram_tensor("dit", [128, 8 * 512], BF16, kind="ExternalOutput").ap()
        dpt_h = nc.dram_tensor("dpt", [D, TOK], F32, kind="ExternalOutput").ap()
        dxg_h = nc.dram_tensor("dxg", [W * D, TOK], BF16, kind="ExternalOutput").ap()
    ones_i = nc.inline_tensor(np.ones((128, 128), dtype=bfdt), name=pfx + "ones_c").ap()

    if fz:
        agi, xg, part, rso = fz["agi"], fz["xg"], fz["part"], fz["rso"]
    else:
        agi = nc.dram_tensor("agi", [D, TOK], BF16, kind="Internal").ap()
        xg = nc.dram_tensor("xg", [W * D, TOK], BF16, kind="Internal").ap()
        part = nc.dram_tensor("part", [W * D, TOK], F32, kind="Internal").ap()
        rso = nc.dram_tensor("rso", [D, TOK], F32, kind="Internal").ap()

    es = contextlib.ExitStack()
    with es:
        e.xr_sb = sbufT(es, nc, "xr_sb", [128, 16 * TOK], F32)
        _wgu = e.xr_sb[:, :].bitcast(BF16)
        wg_sb = _wgu[:, 0:16 * 1024]
        wu_sb = _wgu[:, 16 * 1024:32 * 1024]
        wd_sb = sbufT(es, nc, "wd_sb", [128, 8 * D], BF16)
        wi_sb = sbufT(es, nc, "wi_sb", [128, 2 * 2048], I32)
        xk_sb = sbufT(es, nc, "xk_sb", [128, 16 * 512], BF16)
        it_sb = sbufT(es, nc, "it_sb", [128, 8 * 512], BF16)
        t1_sb = sbufT(es, nc, "t1_sb", [128, 512], BF16)
        t2_sb = sbufT(es, nc, "t2_sb", [128, 512], F32)
        po_sb = sbufT(es, nc, "po_sb", [128, 2 * 512], F32)
        e.xo_sb = sbufT(es, nc, "xo_sb", [128, TOK], F32)
        e.xb1_sb = sbufT(es, nc, "xb1_sb", [128, TOK], F32)
        e.sq1_sb = sbufT(es, nc, "sq1_sb", [128, TOK], F32)
        e.st_sb = sbufT(es, nc, "st_sb", [1, 4096], F32)
        e.o32_sb = sbufT(es, nc, "o32_sb", [128, 128], F32)
        e.ones_sb = sbufT(es, nc, "ones_sb", [128, 128], BF16)
        e.fl_sb = sbufT(es, nc, "fl_sb", [128, 2], F32)
        if layer == 0:
            gsc_sb = sbufT(es, nc, "gsc_sb", [128, 8], F32)
            usc_sb = sbufT(es, nc, "usc_sb", [128, 8], F32)
            dsc_sb = sbufT(es, nc, "dsc_sb", [128, 16], F32)

        e.ps_a = psumT(es, nc, "ps_a", [128, 512], F32)
        e.ps_b = psumT(es, nc, "ps_b", [128, 512], F32)
        e.ps_c = psumT(es, nc, "ps_c", [128, 512], F32)
        e.ps_d = psumT(es, nc, "ps_d", [128, 512], F32)
        e.ps_ba = psumT(es, nc, "ps_ba", [128, 512], F32)
        e.ps_bb = psumT(es, nc, "ps_bb", [128, 512], F32)
        e.ps_bc = psumT(es, nc, "ps_bc", [128, 512], F32)
        e.ps_bd = psumT(es, nc, "ps_bd", [128, 512], F32)

        if fz:
            sm = fz["sems"]
            ld, ag, wl, xt = sm["ld"], sm["ag"], sm["wl"], sm["xt"]
            e.mm = mm = sm["mm"]; e.ev = ev = sm["ev"]
            e.ex = ex = sm["ex"]; e.so = so = sm["so"]
        else:
            ld = Sem(es, nc, "ld")
            ag = Sem(es, nc, "ag")
            wl = Sem(es, nc, "wl")
            xt = Sem(es, nc, "xt")
            e.mm = mm = Sem(es, nc, "mm")
            e.ev = ev = Sem(es, nc, "ev")
            e.ex = ex = Sem(es, nc, "ex")
            e.so = so = Sem(es, nc, "so")
        sy, gg, ve, sc, te = nc.sync, nc.gpsimd, nc.vector, nc.scalar, nc.tensor

        ld.inc(gg.dma_start(out=agi[:, :], in_=x_h[:, :]), 16)
        ld.inc(sy.dma_start(out=e.ones_sb[:], in_=ones_i[:, :]), 16)
        if layer == 0:
            ld.inc(sy.dma_start(out=gsc_sb[:], in_=gsc_h[:, :]), 16)
            ld.inc(sy.dma_start(out=usc_sb[:], in_=usc_h[:, :]), 16)
            ld.inc(sy.dma_start(out=dsc_sb[:], in_=dsc_h[:, :]), 16)
        gg.wait_ge(ld.h, ld.n)
        ag.inc(gg.collective_compute("AllGather", ALU.bypass,
                                     replica_groups=[list(range(W))],
                                     ins=[agi[:, :]], outs=[xg[:, :]]), 1)
        sy.wait_ge(ag.h, ag.n)
        gg.wait_ge(ag.h, ag.n)

        def xg_win(tt):
            r, cs = tt // 2, 512 * (tt % 2)
            return (xg[r * D:(r + 1) * D, cs:cs + 512]
                    .rearrange("(c p) t -> p c t", p=128))

        for sub in range(NSUB):
            if layer == 0:
                def deq_load(w_h, dst_sb, nrows, width, iblk):
                    for ib in range(nrows):
                        h = ib % 2
                        if ib >= 2 or iblk > 0:
                            gg.wait_ge(ev.h, ev.n)
                        wl.inc(gg.dma_start(out=wi_sb[:, h * 2048:h * 2048 + width],
                                            in_=w_h[ib * 128:(ib + 1) * 128, :]), 16)
                        ve.wait_ge(wl.h, wl.n)
                        ev.inc(ve.tensor_scalar(dst_sb[:, ib * width:(ib + 1) * width],
                                                wi_sb[:, h * 2048:h * 2048 + width],
                                                -128.0, None, ALU.add), 1)
                deq_load(wg_h, wg_sb, 16, 1024, 0)
                deq_load(wu_h, wu_sb, 16, 1024, 1)
                deq_load(wd_h, wd_sb, 8, 2048, 2)
            else:
                wl.inc(gg.dma_start(out=wg_sb[:].rearrange("p (c n) -> p c n", c=16),
                                    in_=wg_h[:, sub * 1024:(sub + 1) * 1024]
                                    .rearrange("(c p) n -> p c n", p=128)), 16)
                wl.inc(gg.dma_start(out=wu_sb[:].rearrange("p (c n) -> p c n", c=16),
                                    in_=wu_h[:, sub * 1024:(sub + 1) * 1024]
                                    .rearrange("(c p) n -> p c n", p=128)), 16)
                wl.inc(gg.dma_start(out=wd_sb[:].rearrange("p (c n) -> p c n", c=8),
                                    in_=wd_h[sub * 1024:(sub + 1) * 1024, :]
                                    .rearrange("(c p) n -> p c n", p=128)), 16)
            te.wait_ge(wl.h, wl.n)
            te.wait_ge(ev.h, ev.n)

            for tt in range(16):
                sy.wait_ge(mm.h, mm.n)
                xt.inc(sy.dma_start(out=xk_sb[:].rearrange("p (c t) -> p c t", c=16),
                                    in_=xg_win(tt)), 16)
                te.wait_ge(xt.h, xt.n)
                for nch in range(8):
                    psg = (e.ps_a, e.ps_b)[nch % 2]
                    psu = (e.ps_c, e.ps_d)[nch % 2]
                    te.wait_ge(ex.h, ex.n)
                    te.wait_ge(ev.h, ev.n)
                    for kc in range(16):
                        inst = te.matmul(psg[:, :],
                                         wg_sb[:, kc * 1024 + nch * 128:
                                               kc * 1024 + nch * 128 + 128],
                                         xk_sb[:, kc * 512:(kc + 1) * 512],
                                         start=(kc == 0), stop=(kc == 15))
                    for kc in range(16):
                        inst = te.matmul(psu[:, :],
                                         wu_sb[:, kc * 1024 + nch * 128:
                                               kc * 1024 + nch * 128 + 128],
                                         xk_sb[:, kc * 512:(kc + 1) * 512],
                                         start=(kc == 0), stop=(kc == 15))
                    mm.inc(inst, 1)
                    sc.wait_ge(mm.h, mm.n)
                    sc.wait_ge(ev.h, ev.n)
                    if layer == 0:
                        ve.wait_ge(mm.h, mm.n)
                        ex.inc(sc.activation(t1_sb[:, :], psg[:, :], AF.Silu,
                                             scale=gsc_sb[:, sub * 8 + nch:
                                                          sub * 8 + nch + 1]), 1)
                        ve.wait_ge(ex.h, ex.n)
                        ev.inc(ve.scalar_tensor_tensor(
                            it_sb[:, nch * 512:(nch + 1) * 512], psu[:, :],
                            usc_sb[:, sub * 8 + nch:sub * 8 + nch + 1],
                            t1_sb[:, :], ALU.mult, ALU.mult), 1)
                    else:
                        ve.wait_ge(mm.h, mm.n)
                        ex.inc(sc.activation(t1_sb[:, :], psg[:, :], AF.Square), 1)
                        ve.wait_ge(ex.h, ex.n)
                        vw(e, ve.tensor_scalar(t2_sb[:, :], t1_sb[:, :],
                                               0.7978845608028654 * 0.044715,
                                               0.7978845608028654, ALU.mult, ALU.add))
                        vw(e, ve.tensor_tensor(t2_sb[:, :], t2_sb[:, :],
                                               psg[:, :], ALU.mult))
                        sc.wait_ge(ev.h, ev.n)
                        ex.inc(sc.activation(t2_sb[:, :], t2_sb[:, :], AF.Tanh), 1)
                        ve.wait_ge(ex.h, ex.n)
                        vw(e, ve.scalar_tensor_tensor(t2_sb[:, :], t2_sb[:, :], 1.0,
                                                      psg[:, :], ALU.add, ALU.mult))
                        ev.inc(ve.scalar_tensor_tensor(
                            it_sb[:, nch * 512:(nch + 1) * 512], psu[:, :], 0.5,
                            t2_sb[:, :], ALU.mult, ALU.mult), 1)
                if debug_ffn and sub == 0 and tt == 0:
                    ev.inc(ve.tensor_copy(e.fl_sb[:, 0:1], e.fl_sb[:, 1:2]), 1)
                    gg.wait_ge(ev.h, ev.n)
                    so.inc(gg.dma_start(out=dit_h[:, :], in_=it_sb[:, :]), 16)
                    te.wait_ge(so.h, so.n)
                for ht in range(16):
                    pso = (e.ps_ba, e.ps_bb)[ht % 2]
                    te.wait_ge(ev.h, ev.n)
                    for nch in range(8):
                        inst = te.matmul(pso[:, :],
                                         wd_sb[:, nch * D + ht * 128:
                                               nch * D + ht * 128 + 128],
                                         it_sb[:, nch * 512:(nch + 1) * 512],
                                         start=(nch == 0), stop=(nch == 7))
                    mm.inc(inst, 1)
                    ve.wait_ge(mm.h, mm.n)
                    pc = po_sb[:, (ht % 2) * 512:(ht % 2) * 512 + 512]
                    if layer == 0:
                        ev.inc(ve.tensor_scalar(pc, pso[:, :],
                                                dsc_sb[:, ht:ht + 1], None,
                                                ALU.mult), 1)
                    else:
                        ev.inc(ve.tensor_copy(pc, pso[:, :]), 1)
                    ev.inc(ve.tensor_copy(e.fl_sb[:, 0:1], e.fl_sb[:, 1:2]), 1)
                    gg.wait_ge(ev.h, ev.n)
                    r, cs = tt // 2, 512 * (tt % 2)
                    dst = part[r * D + ht * 128:r * D + ht * 128 + 128, cs:cs + 512]
                    if sub == 0:
                        so.inc(gg.dma_start(out=dst, in_=pc), 16)
                    else:
                        so.inc(gg.dma_start(out=dst, in_=pc,
                                            accum_op=ALU.add), 16)
                    ve.wait_ge(so.h, so.n)
        if debug_ffn:
            gg.wait_ge(so.h, so.n)
            so.inc(gg.dma_start(out=dpt_h[:, :], in_=part[0:D, :]), 16)
            so.inc(gg.dma_start(out=dxg_h[:, :], in_=xg[:, :]), 16)
            drs_ok = True
        gg.wait_ge(so.h, so.n)
        ag.inc(gg.collective_compute("ReduceScatter", ALU.add,
                                     replica_groups=[list(range(W))],
                                     ins=[part[:, :]], outs=[rso[:, :]]), 1)
        sy.wait_ge(ag.h, ag.n)
        if debug_ffn:
            gg.wait_ge(ag.h, ag.n)
            so.inc(gg.dma_start(out=drs_h[:, :], in_=rso[:, :]), 16)
        ld.inc(sy.dma_start(out=e.xr_sb[:].rearrange("p (c t) -> p c t", c=16),
                            in_=x_h.rearrange("(c p) t -> p c t", p=128)), 16)
        for dt in range(16):
            if dt > 0:
                sy.wait_ge(ev.h, ev.n)
            ld.inc(sy.dma_start(out=e.xo_sb[:, 0:1024],
                                in_=rso[dt * 128:(dt + 1) * 128, :]), 16)
            ve.wait_ge(ld.h, ld.n)
            ev.inc(ve.tensor_tensor(e.xr_sb[:, dt * 1024:(dt + 1) * 1024],
                                    e.xr_sb[:, dt * 1024:(dt + 1) * 1024],
                                    e.xo_sb[:, 0:1024], ALU.add), 1)
        emit_ln(e, xn_h)
        gg.wait_ge(so.h, so.n)
    return nc


def emit_decode(e, nib, out_bf, tmp):
    ve = e.nc.vector
    m, t, u = tmp
    ve.tensor_single_scalar(m[:, :], nib[:, :], 7, ALU.bitwise_and)
    ve.wait_ge(e.ev.h, e.ev.n)
    vw(e, ve.tensor_single_scalar(t[:, :], m[:, :], 1, ALU.is_equal))
    vw(e, ve.tensor_tensor(u[:, :], m[:, :], t[:, :], ALU.subtract))
    vw(e, ve.tensor_single_scalar(u[:, :], u[:, :], 252, ALU.add))
    vw(e, ve.tensor_single_scalar(t[:, :], m[:, :], 1, ALU.is_ge))
    vw(e, ve.tensor_tensor(u[:, :], u[:, :], t[:, :], ALU.mult))
    vw(e, ve.tensor_single_scalar(u[:, :], u[:, :], 6, ALU.logical_shift_left))
    vw(e, ve.tensor_single_scalar(t[:, :], nib[:, :], 8, ALU.bitwise_and))
    vw(e, ve.tensor_single_scalar(t[:, :], t[:, :], 12, ALU.logical_shift_left))
    vw(e, ve.tensor_tensor(u[:, :], u[:, :], t[:, :], ALU.bitwise_or))
    vw(e, ve.tensor_copy(out_bf[:, :].bitcast(mybir.dt.uint16), u[:, :]))


def build_decode(fz=None):
    pfx = fz["pfx"] if fz else ""
    _NPFX[0] = pfx
    nc = fz["nc"] if fz else bass.Bass("TRN2", target_bir_lowering=False, num_devices=W)
    e = SimpleNamespace()
    e.nc = nc
    ins, outs, scs = {}, {}, {}
    for nm in ("g", "u", "d"):
        ins[nm] = nc.dram_tensor(pfx + nm + "pk", [1024, 2048], I32, kind="ExternalInput").ap()
        scs[nm] = nc.dram_tensor(pfx + nm + "scT", [64, 2048], F32, kind="ExternalInput").ap()
        if fz:
            outs[nm] = fz[nm + "dec"]
        else:
            outs[nm] = nc.dram_tensor(nm + "dec", [2048, 2048], BF16, kind="ExternalOutput").ap()
    G_np = np.zeros((8, 128), dtype=np.float32)
    for a in range(8):
        G_np[a, a * 16:(a + 1) * 16] = 1
    G_i = nc.inline_tensor(G_np, name=pfx + "G_c").ap()

    es = contextlib.ExitStack()
    with es:
        pk_sb = sbufT(es, nc, "pk_sb", [128, 2 * 2048], I32)
        lo_sb = sbufT(es, nc, "lo_sb", [128, 2048], I32)
        hi_sb = sbufT(es, nc, "hi_sb", [128, 2048], I32)
        m_sb = sbufT(es, nc, "m_sb", [128, 2048], I32)
        t_sb = sbufT(es, nc, "t_sb", [128, 2048], I32)
        u_sb = sbufT(es, nc, "u_sb", [128, 2048], I32)
        dlo_sb = sbufT(es, nc, "dlo_sb", [128, 2048], BF16)
        dhi_sb = sbufT(es, nc, "dhi_sb", [128, 2048], BF16)
        olo_sb = sbufT(es, nc, "olo_sb", [128, 2048], BF16)
        ohi_sb = sbufT(es, nc, "ohi_sb", [128, 2048], BF16)
        sc_sb = sbufT(es, nc, "sc_sb", [8, 2 * 2048], F32)
        G_sb = sbufT(es, nc, "G_sb", [8, 128], F32)
        ps_sc = psumT(es, nc, "ps_sc", [128, 512], F32)
        ps_s2 = psumT(es, nc, "ps_s2", [128, 512], F32)

        if fz:
            sm = fz["sems"]
            ld = sm["ld"]
            e.mm = mm = sm["mm"]; e.ev = ev = sm["ev"]; e.so = so = sm["so"]
        else:
            ld = Sem(es, nc, "ld")
            e.mm = mm = Sem(es, nc, "mm")
            e.ev = ev = Sem(es, nc, "ev")
            e.so = so = Sem(es, nc, "so")
        sy, gg, ve, te = nc.sync, nc.gpsimd, nc.vector, nc.tensor

        ld.inc(sy.dma_start(out=G_sb[:], in_=G_i[:, :]), 16)
        for nm in ("g", "u", "d"):
            for rt in range(8):
                h = rt % 2
                sy.wait_ge(ev.h, ev.n)
                ve.wait_ge(so.h, so.n)
                ld.inc(sy.dma_start(out=pk_sb[:, h * 2048:(h + 1) * 2048],
                                    in_=ins[nm][rt * 128:(rt + 1) * 128, :]), 16)
                ld.inc(sy.dma_start(out=sc_sb[0:8, h * 2048:(h + 1) * 2048],
                                    in_=scs[nm][rt * 8:(rt + 1) * 8, :]), 16)
                ve.wait_ge(ld.h, ld.n)
                vw(e, ve.tensor_single_scalar(lo_sb[:, :],
                                              pk_sb[:, h * 2048:(h + 1) * 2048],
                                              15, ALU.bitwise_and))
                vw(e, ve.tensor_single_scalar(hi_sb[:, :],
                                              pk_sb[:, h * 2048:(h + 1) * 2048],
                                              4, ALU.logical_shift_right))
                emit_decode(e, lo_sb, dlo_sb, (m_sb, t_sb, u_sb))
                emit_decode(e, hi_sb, dhi_sb, (m_sb, t_sb, u_sb))
                for cb in range(4):
                    psc = (ps_sc, ps_s2)[cb % 2]
                    te.wait_ge(ev.h, ev.n)
                    mm.inc(te.matmul(psc[:, :], G_sb[0:8, :],
                                     sc_sb[0:8, h * 2048 + cb * 512:
                                           h * 2048 + cb * 512 + 512],
                                     start=True, stop=True), 1)
                    ve.wait_ge(mm.h, mm.n)
                    cbs = slice(cb * 512, (cb + 1) * 512)
                    ev.inc(ve.tensor_tensor(olo_sb[:, cbs], dlo_sb[:, cbs],
                                            psc[:, :], ALU.mult), 1)
                    ev.inc(ve.tensor_tensor(ohi_sb[:, cbs], dhi_sb[:, cbs],
                                            psc[:, :], ALU.mult), 1)
                gg.wait_ge(ev.h, ev.n)
                if nm == "d":
                    so.inc(gg.dma_start(out=outs[nm][(2 * rt) * 128:(2 * rt + 1) * 128, :],
                                        in_=olo_sb[:, :]), 16)
                    so.inc(gg.dma_start(out=outs[nm][(2 * rt + 1) * 128:(2 * rt + 2) * 128, :],
                                        in_=ohi_sb[:, :]), 16)
                else:
                    so.inc(gg.dma_start(out=outs[nm][256 * rt:256 * rt + 256:2, :],
                                        in_=olo_sb[:, :]), 16)
                    so.inc(gg.dma_start(out=outs[nm][256 * rt + 1:256 * rt + 256:2, :],
                                        in_=ohi_sb[:, :]), 16)
                gg.wait_ge(so.h, so.n)
        gg.wait_ge(so.h, so.n)
    return nc


def l1_perm(c):
    perm = np.zeros(2048, dtype=np.int64)
    for ci in range(16):
        for p in range(128):
            perm[ci * 128 + p] = 2048 * c + 256 * (ci // 2) + 2 * p + (ci % 2)
    return perm


def host_decode_inputs(inp):
    gp = np.asarray(inp["L1_ffn_gate_W_packed"])
    up = np.asarray(inp["L1_ffn_up_W_packed"])
    dp = np.asarray(inp["L1_ffn_down_W_packed"])
    gs = np.asarray(inp["L1_ffn_gate_scales"])
    us = np.asarray(inp["L1_ffn_up_scales"])
    ds = np.asarray(inp["L1_ffn_down_scales"])
    gpT = gp.T
    upT = up.T
    dpT = dp.T
    maps = []
    for c in range(W):
        perm = l1_perm(c)
        m = {
            "gpk": np.ascontiguousarray(gpT[:, perm]),
            "upk": np.ascontiguousarray(upT[:, perm]),
            "dpk": np.ascontiguousarray(dpT[1024 * c:1024 * (c + 1), :]),
            "gscT": np.ascontiguousarray(gs[perm, :].T),
            "uscT": np.ascontiguousarray(us[perm, :].T),
            "dscT": np.ascontiguousarray(ds[:, 64 * c:64 * (c + 1)].T),
        }
        maps.append(m)
    return maps


def host_oln_inputs(layer, inp, xcur, ys):
    x = np.asarray(xcur, dtype=np.float32).reshape(T, D)
    wo = np.asarray(inp["L0_attn_Wo"]).T if layer == 0 else np.asarray(inp["L1_attn_Wo"])
    wo = np.ascontiguousarray(wo.astype(np.float32))
    maps = []
    for c in range(W):
        yfm = np.ascontiguousarray(
            np.concatenate([np.asarray(ys[r])[:, c * TOK:(c + 1) * TOK]
                            for r in range(W)], axis=0))
        maps.append({"y": yfm, "wo": wo,
                     "x_own": np.ascontiguousarray(x[c * TOK:(c + 1) * TOK].T)})
    return maps


def host_ffn_inputs(layer, inp, xcur, decoded=None):
    x = np.asarray(xcur, dtype=np.float32).reshape(T, D)
    maps = []
    for c in range(W):
        m = {"x_own": np.ascontiguousarray(x[c * TOK:(c + 1) * TOK].T)}
        if layer == 0:
            s = slice(1024 * c, 1024 * (c + 1))
            m["wg"] = np.ascontiguousarray(np.asarray(inp["L0_ffn_gate_w"])[s].T)
            m["wu"] = np.ascontiguousarray(np.asarray(inp["L0_ffn_up_w"])[s].T)
            m["wd"] = np.ascontiguousarray(np.asarray(inp["L0_ffn_down_w"])[:, s].T)
            m["gsc"] = np.ascontiguousarray(
                np.asarray(inp["L0_ffn_gate_scale"])[s].reshape(8, 128).T)
            m["usc"] = np.ascontiguousarray(
                np.asarray(inp["L0_ffn_up_scale"])[s].reshape(8, 128).T)
            m["dsc"] = np.ascontiguousarray(
                np.asarray(inp["L0_ffn_down_scale"]).reshape(16, 128).T)
        else:
            m["wg"] = decoded[c]["gdec"]
            m["wu"] = decoded[c]["udec"]
            m["wd"] = decoded[c]["ddec"]
        maps.append(m)
    return maps




def build_fused():
    nc = bass.Bass("TRN2", target_bir_lowering=False, num_devices=W)
    es_top = contextlib.ExitStack()
    nc._fused_es = es_top
    sems = {nm: Sem(es_top, nc, nm) for nm in
            ("ld", "cv", "ag", "xt", "wl", "mm", "ev", "ex", "so")}

    x0 = nc.dram_tensor("x_own", [D, TOK], F32, kind="ExternalInput").ap()
    out_h = nc.dram_tensor("xn", [D, TOK], F32, kind="ExternalOutput").ap()
    agi = nc.dram_tensor("agi", [D, TOK], BF16, kind="Internal").ap()
    xg = nc.dram_tensor("xg", [W * D, TOK], BF16, kind="Internal").ap()
    part = nc.dram_tensor("part", [W * D, TOK], F32, kind="Internal").ap()
    rso = nc.dram_tensor("rso", [D, TOK], F32, kind="Internal").ap()
    ytm = nc.dram_tensor("ytm", [T, 256], BF16, kind="Internal").ap()
    ygath = nc.dram_tensor("ygath", [W * T, 256], BF16, kind="Internal").ap()
    x1 = nc.dram_tensor("x1", [D, TOK], F32, kind="Internal").ap()
    x2 = nc.dram_tensor("x2", [D, TOK], F32, kind="Internal").ap()
    x3 = nc.dram_tensor("x3", [D, TOK], F32, kind="Internal").ap()
    gdec = nc.dram_tensor("gdec", [2048, 2048], BF16, kind="Internal").ap()
    udec = nc.dram_tensor("udec", [2048, 2048], BF16, kind="Internal").ap()
    ddec = nc.dram_tensor("ddec", [2048, 2048], BF16, kind="Internal").ap()

    base = dict(nc=nc, sems=sems, agi=agi, xg=xg, part=part, rso=rso,
                ytm=ytm, ygath=ygath)

    build_decode(fz={**base, "pfx": "dc_", "gdec": gdec, "udec": udec, "ddec": ddec})
    nc.all_engine_barrier()
    build_attn(0, fz={**base, "pfx": "a0_", "x_ap": x0})
    nc.all_engine_barrier()
    build_o_ln(fz={**base, "pfx": "o0_", "x_ap": x0, "out_ap": x1})
    nc.all_engine_barrier()
    build_ffn(0, fz={**base, "pfx": "f0_", "x_ap": x1, "out_ap": x2})
    nc.all_engine_barrier()
    build_attn(1, fz={**base, "pfx": "a1_", "x_ap": x2})
    nc.all_engine_barrier()
    build_o_ln(fz={**base, "pfx": "o1_", "x_ap": x2, "out_ap": x3})
    nc.all_engine_barrier()
    build_ffn(1, fz={**base, "pfx": "f1_", "x_ap": x3,
                     "wg_ap": gdec, "wu_ap": udec, "wd_ap": ddec,
                     "out_ap": out_h})
    es_top.close()
    return nc


def host_fused_inputs(inp):
    x = np.asarray(inp["L0_attn_x"], dtype=np.float32).reshape(T, D)
    a0 = host_attn_inputs(0, inp)
    a1 = host_attn_inputs(1, inp)
    dc = host_decode_inputs(inp)
    f0 = host_ffn_inputs(0, inp, x)
    wo0 = np.ascontiguousarray(np.asarray(inp["L0_attn_Wo"]).T.astype(np.float32))
    wo1 = np.ascontiguousarray(np.asarray(inp["L1_attn_Wo"]).astype(np.float32))
    maps = []
    for c in range(W):
        yidx = np.zeros((128, 64), dtype=np.int32)
        for r in range(W):
            for tt in range(8):
                yidx[:, r * 8 + tt] = r * T + c * TOK + tt * 128 + np.arange(128)
        m = {"x_own": np.ascontiguousarray(x[c * TOK:(c + 1) * TOK].T)}
        for k, v in a0[c].items():
            if k != "x_own":
                m["a0_" + k] = v
        for k, v in a1[c].items():
            if k != "x_own":
                m["a1_" + k] = v
        for k, v in dc[c].items():
            m["dc_" + k] = v
        for k, v in f0[c].items():
            if k != "x_own":
                m["f0_" + k] = v
        m["o0_wo"] = wo0
        m["o1_wo"] = wo1
        m["o0_yidx"] = yidx
        m["o1_yidx"] = yidx
        maps.append(m)
    return maps


def kernel_fused(**inputs):
    nc = build_fused()
    maps = host_fused_inputs(inputs)
    res = run_bass_kernel_spmd(nc, maps, core_ids=list(range(W)))
    x4 = _gather_xn(res.results)
    return np.ascontiguousarray(x4.reshape(B, TB, D).astype(np.float32))


def _run(nc, maps):
    res = run_bass_kernel_spmd(nc, maps, core_ids=list(range(W)))
    return res.results


def _gather_xn(res):
    return np.concatenate([np.asarray(res[c]["xn"]).T for c in range(W)], axis=0)


def kernel_multi(**inputs):
    inp = inputs
    x0 = np.asarray(inp["L0_attn_x"], dtype=np.float32)

    res = _run(build_attn(0), host_attn_inputs(0, {**inp, "L0_attn_x": x0}))
    ys = [np.asarray(res[c]["y"]) for c in range(W)]
    res = _run(build_o_ln(), host_oln_inputs(0, inp, x0, ys))
    x1 = _gather_xn(res)

    res = _run(build_ffn(0), host_ffn_inputs(0, inp, x1))
    x2 = _gather_xn(res)

    res = _run(build_attn(1), host_attn_inputs(1, {**inp, "L0_attn_x": x2}))
    ys = [np.asarray(res[c]["y"]) for c in range(W)]
    res = _run(build_o_ln(), host_oln_inputs(1, inp, x2, ys))
    x3 = _gather_xn(res)

    res = _run(build_decode(), host_decode_inputs(inp))
    decoded = [{k: np.asarray(res[c][k]) for k in ("gdec", "udec", "ddec")}
               for c in range(W)]
    res = _run(build_ffn(1), host_ffn_inputs(1, inp, x3, decoded))
    x4 = _gather_xn(res)

    return np.ascontiguousarray(x4.reshape(B, TB, D).astype(np.float32))


def kernel(**inputs):
    return kernel_fused(**inputs)


# version 07
# speedup vs baseline: 1.5449x; 1.5449x over previous
import math
import contextlib
from types import SimpleNamespace
import numpy as np
import ml_dtypes
import concourse.bass as bass
import concourse.mybir as mybir
from concourse.bass_core_rust import IndirectOffsetOnAxis
from concourse.bass_utils import run_bass_kernel_spmd

F32 = mybir.dt.float32
BF16 = mybir.dt.bfloat16
I32 = mybir.dt.int32
AF = mybir.ActivationFunctionType
ALU = mybir.AluOpType
AX = mybir.AxisListType

W = 8
B = 2
TB = 4096
T = B * TB
D = 2048
TOK = T // W
NEG = -1.0e30
bfdt = ml_dtypes.bfloat16


class Sem:
    def __init__(self, es, nc, name):
        self.h = es.enter_context(nc.semaphore(name))
        self.n = 0

    def inc(self, inst, k):
        inst.then_inc(self.h, k)
        self.n += k
        return self.n


def vw(e, inst):
    e.ev.inc(inst, 1)
    e.nc.vector.wait_ge(e.ev.h, e.ev.n)


_NPFX = [""]


def sbufT(es, nc, name, shape, dtype):
    return es.enter_context(nc.sbuf_tensor(_NPFX[0] + name, shape, dtype))


def psumT(es, nc, name, shape, dtype=F32):
    return es.enter_context(nc.psum_tensor(_NPFX[0] + name, shape, dtype))


def build_masks(layer):
    p = np.arange(128)[:, None]
    f = np.arange(512)[None, :]
    tiles, idx = [], {}

    def add(key, allow):
        idx[key] = len(tiles)
        allow = np.broadcast_to(allow, (128, 512))
        tiles.append(np.where(allow, np.float32(0.0), np.float32(NEG)))

    for o in (0, 128, 256, 384):
        add(("c", o), f <= p + o)
    if layer == 0:
        for o in (1, 129, 257, 385):
            add(("w", o), f >= p + o)
            add(("ws", o), (f >= p + o) | (f < 4))
            add(("wp", o - 512), f >= p + o - 512)
            add(("wsp", o - 512), (f >= p + o - 512) | (f < 4))
        add(("s",), f < 4)
    return np.stack(tiles).astype(np.float32), idx


def chunk_plan(layer, i0b):
    qc = i0b // 512
    if layer == 1:
        return [(jc, None) for jc in range(qc)] + [(qc, ("c", i0b % 512))]
    wlo = i0b - 1023
    wbc = max(0, wlo // 512)
    plan = []
    if wbc > 0:
        plan.append((0, ("s",)))
    for jc in range(wbc, qc + 1):
        if jc == qc:
            plan.append((jc, ("c", i0b % 512)))
            continue
        o = wlo - 512 * jc
        if o >= 1:
            plan.append((jc, ("ws" if jc == 0 else "w", o)))
        elif o > -511:
            plan.append((jc, ("wsp" if jc == 0 else "wp", o)))
        else:
            plan.append((jc, None))
    return plan


def build_attn(layer, debug_qkv=False, debug_attn=False, fz=None):
    NH = 2
    KVD = 128 if layer == 0 else 256
    KT = KVD // 128
    scale = 1.0 / math.sqrt(128 if layer == 0 else 64)
    masks_np, midx = build_masks(layer)
    NM = masks_np.shape[0]
    MAXC = 4 if layer == 0 else 8
    SLAB = max(MAXC, 8) * 512

    pfx = fz["pfx"] if fz else ""
    _NPFX[0] = pfx
    nc = fz["nc"] if fz else bass.Bass("TRN2", target_bir_lowering=False, num_devices=W)
    e = SimpleNamespace()
    e.nc, e.layer, e.scale, e.midx, e.MAXC, e.KT, e.NH = nc, layer, scale, midx, MAXC, KT, NH

    x_own = fz["x_ap"] if fz else nc.dram_tensor("x_own", [D, TOK], F32, kind="ExternalInput").ap()
    wq_h = nc.dram_tensor(pfx + "wq", [D, 256], F32, kind="ExternalInput").ap()
    wk_h = nc.dram_tensor(pfx + "wk", [D, KVD], F32, kind="ExternalInput").ap()
    wv_h = nc.dram_tensor(pfx + "wv", [D, KVD], F32, kind="ExternalInput").ap()
    masks_h = nc.dram_tensor(pfx + "masks", [NM * 128, 512], F32, kind="ExternalInput").ap()
    if layer == 1:
        lam_h = nc.dram_tensor(pfx + "lam", [1, 2], F32, kind="ExternalInput").ap()
    if fz:
        e.y_h = None
        e.ytm_ap = fz["ytm"]
    else:
        e.ytm_ap = None
        e.y_h = nc.dram_tensor("y", [NH * 128, T], BF16, kind="ExternalOutput").ap()
    if debug_qkv:
        xgdb = nc.dram_tensor("xgdb", [W * D, TOK], BF16, kind="ExternalOutput").ap()
        agdb = nc.dram_tensor("agdb", [D, TOK], BF16, kind="ExternalOutput").ap()
        agdb2 = nc.dram_tensor("agdb2", [D, TOK], BF16, kind="ExternalOutput").ap()
        qdb = nc.dram_tensor("qdb", [256, T], BF16, kind="ExternalOutput").ap()
        kdb = nc.dram_tensor("kdb", [KVD, T], BF16, kind="ExternalOutput").ap()
        vdb = nc.dram_tensor("vdb", [T, KVD], BF16, kind="ExternalOutput").ap()

    if debug_attn:
        e.dbg_bd = nc.dram_tensor("dbg_bd", [128, 64], F32, kind="ExternalOutput").ap()
        e.dbg_nrm = nc.dram_tensor("dbg_nrm", [128, 64], F32, kind="ExternalOutput").ap()
        e.dbg_kmb = nc.dram_tensor("dbg_kmb", [128, 4], F32, kind="ExternalOutput").ap()
        e.dbg_p1 = nc.dram_tensor("dbg_p1", [128, 4096], BF16, kind="ExternalOutput").ap()
        e.dbg_acc = nc.dram_tensor("dbg_acc", [128, 16], F32, kind="ExternalOutput").ap()
        e.dbg_sml = nc.dram_tensor("dbg_sml", [128, 8], F32, kind="ExternalOutput").ap()
    e.debug_attn = debug_attn
    if fz:
        agi, xg = fz["agi"], fz["xg"]
    else:
        agi = nc.dram_tensor("agi", [D, TOK], BF16, kind="Internal").ap()
        xg = nc.dram_tensor("xg", [W * D, TOK], BF16, kind="Internal").ap()

    ones_i = nc.inline_tensor(np.ones((128, 128), dtype=bfdt), name=pfx + "ones_c").ap()
    ident_i = nc.inline_tensor(np.eye(128, dtype=bfdt), name=pfx + "ident_c").ap()

    es = contextlib.ExitStack()
    with es:
        wq_sb = sbufT(es, nc, "wq_sb", [128, 16 * 256], BF16)
        wk_sb = sbufT(es, nc, "wk_sb", [128, 16 * KVD], BF16)
        wv_sb = sbufT(es, nc, "wv_sb", [128, 16 * KVD], BF16)
        xk_sb = sbufT(es, nc, "xk_sb", [128, 16 * 512], BF16)
        e.q_sb = sbufT(es, nc, "q_sb", [128, NH * T], BF16)
        e.k_sb = sbufT(es, nc, "k_sb", [128, KT * T], BF16)
        e.v_sb = sbufT(es, nc, "v_sb", [128, (T // 128) * KVD], BF16)
        e.y_sb = sbufT(es, nc, "y_sb", [128, TB], BF16)
        e.p1f = sbufT(es, nc, "p1f", [128, MAXC * 512], F32)
        if layer == 1:
            e.p2f = sbufT(es, nc, "p2f", [128, MAXC * 512], F32)
        e.pn_sb = sbufT(es, nc, "pn_sb", [128, 4096], BF16)
        e.p1_sb = e.pn_sb
        e.pt_sb = sbufT(es, nc, "pt_sb", [128, 2 * 512], BF16)
        e.mask_sb = sbufT(es, nc, "mask_sb", [128, NM * 512], F32)
        e.ones_sb = sbufT(es, nc, "ones_sb", [128, 128], BF16)
        e.id_sb = sbufT(es, nc, "id_sb", [128, 128], BF16)
        e.bd_sb = sbufT(es, nc, "bd_sb", [128, 64], F32)
        e.nrm_sb = sbufT(es, nc, "nrm_sb", [128, 256], F32)
        e.acc_sb = sbufT(es, nc, "acc_sb", [128, 2 * MAXC], F32)
        e.sml_sb = sbufT(es, nc, "sml_sb", [128, 8], F32)
        e.km_sb = sbufT(es, nc, "km_sb", [1, 8], F32)
        e.kmb16 = sbufT(es, nc, "kmb16", [1, 8], BF16)
        e.kmb_sb = sbufT(es, nc, "kmb_sb", [128, 4], F32)
        e.o32_sb = sbufT(es, nc, "o32_sb", [128, 128], F32)
        e.lamb_sb = sbufT(es, nc, "lamb_sb", [128, 2], F32)
        e.fl_sb = sbufT(es, nc, "fl_sb", [128, 2], F32)
        e.ytp_sb = sbufT(es, nc, "ytp_sb", [128, 256], BF16)

        e.ps_a = psumT(es, nc, "ps_a", [128, 512], F32)
        e.ps_b = psumT(es, nc, "ps_b", [128, 512], F32)
        e.ps_c = psumT(es, nc, "ps_c", [128, 512], F32)
        e.ps_d = psumT(es, nc, "ps_d", [128, 512], F32)
        e.ps_p0 = psumT(es, nc, "ps_p0", [128, 512], BF16)
        e.ps_p1 = psumT(es, nc, "ps_p1", [128, 512], BF16)
        e.ps_y = psumT(es, nc, "ps_y", [128, 128], F32)
        e.ps_n = psumT(es, nc, "ps_n", [128, 64], F32)

        if fz:
            sm = fz["sems"]
            ld, cv, ag, xt = sm["ld"], sm["cv"], sm["ag"], sm["xt"]
            e.mm = mm = sm["mm"]; e.ev = ev = sm["ev"]
            e.ex = ex = sm["ex"]; e.so = so = sm["so"]
        else:
            ld = Sem(es, nc, "ld")
            cv = Sem(es, nc, "cv")
            ag = Sem(es, nc, "ag")
            xt = Sem(es, nc, "xt")
            e.mm = mm = Sem(es, nc, "mm")
            e.ev = ev = Sem(es, nc, "ev")
            e.ex = ex = Sem(es, nc, "ex")
            e.so = so = Sem(es, nc, "so")

        sy, gg, ve, sc, te = nc.sync, nc.gpsimd, nc.vector, nc.scalar, nc.tensor

        ld.inc(gg.dma_start(out=wq_sb[:].rearrange("p (c o) -> p c o", c=16), in_=wq_h.rearrange("(c p) o -> p c o", p=128)), 16)
        ld.inc(gg.dma_start(out=wk_sb[:].rearrange("p (c o) -> p c o", c=16), in_=wk_h.rearrange("(c p) o -> p c o", p=128)), 16)
        ld.inc(gg.dma_start(out=wv_sb[:].rearrange("p (c o) -> p c o", c=16), in_=wv_h.rearrange("(c p) o -> p c o", p=128)), 16)
        ld.inc(sy.dma_start(out=e.mask_sb[:].rearrange("p (m f) -> p m f", f=512),
                            in_=masks_h.rearrange("(m p) f -> p m f", p=128)), 16)
        ld.inc(sy.dma_start(out=e.ones_sb[:], in_=ones_i[:, :]), 16)
        ld.inc(sy.dma_start(out=e.id_sb[:], in_=ident_i[:, :]), 16)
        if layer == 1:
            ld.inc(sy.dma_start(out=e.km_sb[0:1, 0:2], in_=lam_h[:, :]), 16)
        cv.inc(gg.dma_start(out=agi[:, :], in_=x_own[:, :]), 16)
        if debug_qkv:
            sy.wait_ge(cv.h, cv.n)
            ag.inc(sy.dma_start(out=agdb[:, :], in_=agi[:, :]), 16)
            gg.wait_ge(ag.h, ag.n)
        gg.wait_ge(cv.h, cv.n)
        ag.inc(gg.collective_compute("AllGather", ALU.bypass,
                                     replica_groups=[list(range(W))],
                                     ins=[agi[:, :]], outs=[xg[:, :]]), 1)
        sy.wait_ge(ag.h, ag.n)

        def xg_win(tt):
            r, cs = tt // 2, 512 * (tt % 2)
            return (xg[r * D:(r + 1) * D, cs:cs + 512]
                    .rearrange("(c p) t -> p c t", p=128))

        pj_ev = [0, 0, 0, 0]
        for tt in range(16):
            if tt > 0:
                sy.wait_ge(ev.h, ev.n)
            xt.inc(sy.dma_start(out=xk_sb[:].rearrange("p (c t) -> p c t", c=16), in_=xg_win(tt)), 16)
            te.wait_ge(xt.h, xt.n)
            rot = 0
            for dst_sb, wsb, no in ((e.q_sb, wq_sb, 2), (e.k_sb, wk_sb, KT)):
                for oh in range(no):
                    ps = (e.ps_a, e.ps_b)[rot % 2]
                    te.wait_ge(ev.h, pj_ev[rot % 2])
                    rot += 1
                    for kc in range(16):
                        inst = te.matmul(ps[:, :],
                                         wsb[:, kc * (no * 128) + oh * 128:
                                             kc * (no * 128) + oh * 128 + 128],
                                         xk_sb[:, kc * 512:(kc + 1) * 512],
                                         start=(kc == 0), stop=(kc == 15))
                    mm.inc(inst, 1)
                    ve.wait_ge(mm.h, mm.n)
                    ev.inc(ve.tensor_copy(dst_sb[:, oh * T + tt * 512:
                                                 oh * T + tt * 512 + 512], ps[:, :]), 1)
                    pj_ev[(rot - 1) % 2] = ev.n
            for ts in range(4):
                ps = (e.ps_c, e.ps_d)[ts % 2]
                te.wait_ge(ev.h, pj_ev[2 + ts % 2])
                for kc in range(16):
                    inst = te.matmul(ps[:, 0:KVD],
                                     xk_sb[:, kc * 512 + ts * 128:
                                           kc * 512 + ts * 128 + 128],
                                     wv_sb[:, kc * KVD:(kc + 1) * KVD],
                                     start=(kc == 0), stop=(kc == 15))
                mm.inc(inst, 1)
                ve.wait_ge(mm.h, mm.n)
                tg = tt * 4 + ts
                ev.inc(ve.tensor_copy(e.v_sb[:, tg * KVD:(tg + 1) * KVD],
                                      ps[:, 0:KVD]), 1)
                pj_ev[2 + ts % 2] = ev.n

        if debug_qkv:
            ev.inc(ve.tensor_copy(e.fl_sb[:, 0:1], e.fl_sb[:, 1:2]), 1)
            gg.wait_ge(ev.h, ev.n)
            so.inc(gg.dma_start(out=xgdb[:, :], in_=xg[:, :]), 16)
            so.inc(gg.dma_start(out=agdb2[:, :], in_=agi[:, :]), 16)
            so.inc(gg.dma_start(out=qdb.rearrange("(h p) t -> p h t", p=128),
                                in_=e.q_sb[:, :].rearrange("p (h t) -> p h t", t=T)), 16)
            so.inc(gg.dma_start(out=kdb.rearrange("(h p) t -> p h t", p=128),
                                in_=e.k_sb[:, :].rearrange("p (h t) -> p h t", t=T)), 16)
            so.inc(gg.dma_start(out=vdb.rearrange("(g p) o -> p g o", p=128),
                                in_=e.v_sb[:, :].rearrange("p (g o) -> p g o", g=T // 128)), 16)
            gg.wait_ge(so.h, so.n)
            return nc

        if layer == 1:
            sc.wait_ge(ld.h, ld.n)
            ex.inc(sc.activation(e.km_sb[0:1, 2:4], e.km_sb[0:1, 0:2], AF.Sigmoid), 1)
            ve.wait_ge(ex.h, ex.n)
            vw(e, ve.memset(e.o32_sb[0:1, :], 1.0))
            te.wait_ge(ev.h, ev.n)
            mm.inc(te.matmul(e.ps_n[:, 0:2], e.o32_sb[0:1, :], e.km_sb[0:1, 2:4],
                             start=True, stop=True), 1)
            ve.wait_ge(mm.h, mm.n)
            vw(e, ve.tensor_copy(e.lamb_sb[:, 0:2], e.ps_n[:, 0:2]))

        for b in range(B):
            for hl in range(NH):
                emit_attn_bh(e, b, hl)

        gg.wait_ge(so.h, so.n)
    return nc


def emit_attn_bh(e, b, hl):
    nc, layer = e.nc, e.layer
    sy, gg, ve, sc, te = nc.sync, nc.gpsimd, nc.vector, nc.scalar, nc.tensor
    mm, ev, ex, so = e.mm, e.ev, e.ex, e.so
    qcol = hl * T + b * TB
    kcol = (hl * T + b * TB) if layer == 1 else b * TB
    halves = ((0,),) if layer == 0 else ((0,), (64,))
    nsets = len(halves)
    MAXC, scale = e.MAXC, e.scale

    sc.wait_ge(mm.h, mm.n)
    ve.wait_ge(so.h, so.n)
    sc.wait_ge(ev.h, ev.n)
    ex.inc(sc.activation(e.p1_sb[:, 0:TB], e.q_sb[:, qcol:qcol + TB], AF.Square), 1)
    te.wait_ge(ex.h, ex.n)
    for si, (base,) in enumerate(halves):
        kw = {} if base == 0 else {"tile_position": (64, 0)}
        npart = 128 // nsets
        for qt in range(32):
            inst = te.matmul(e.ps_n[:, si * 32 + qt:si * 32 + qt + 1],
                             e.p1_sb[base:base + npart, qt * 128:(qt + 1) * 128],
                             e.ones_sb[base:base + npart, 0:1],
                             start=True, stop=True, **kw)
        mm.inc(inst, 1)
    ve.wait_ge(mm.h, mm.n)
    vw(e, ve.tensor_copy(e.nrm_sb[:, 0:64], e.ps_n[:, 0:64]))
    sc.wait_ge(ev.h, ev.n)
    ex.inc(sc.activation(e.p1_sb[:, 0:TB], e.k_sb[:, kcol:kcol + TB], AF.Square), 1)
    te.wait_ge(ex.h, ex.n)
    te.wait_ge(ev.h, ev.n)
    for si, (base,) in enumerate(halves):
        kw = {} if base == 0 else {"tile_position": (64, 0)}
        npart = 128 // nsets
        for jt in range(32):
            inst = te.matmul(e.ps_n[:, si * 32 + jt:si * 32 + jt + 1],
                             e.p1_sb[base:base + npart, jt * 128:(jt + 1) * 128],
                             e.ones_sb[base:base + npart, 0:1],
                             start=True, stop=True, **kw)
        mm.inc(inst, 1)
    ve.wait_ge(mm.h, mm.n)
    vw(e, ve.tensor_copy(e.nrm_sb[:, 64:128], e.ps_n[:, 0:64]))
    for si in range(nsets):
        vw(e, ve.reduce_max(e.sml_sb[:, si:si + 1],
                            e.nrm_sb[:, 64 + si * 32:64 + si * 32 + 32], axis=AX.X))
    vw(e, ve.memset(e.p1_sb[:, 0:128], NEG))
    for si in range(nsets):
        vw(e, ve.tensor_copy(e.p1_sb[:, si * 64:si * 64 + 1],
                             e.sml_sb[:, si:si + 1]))
    te.wait_ge(ev.h, ev.n)
    mm.inc(te.transpose(e.ps_p0[:, 0:128], e.p1_sb[:, 0:128], e.id_sb[:, :]), 1)
    ve.wait_ge(mm.h, mm.n)
    for si in range(nsets):
        vw(e, ve.tensor_copy(e.nrm_sb[si * 64:si * 64 + 1, 128:256],
                             e.ps_p0[si * 64:si * 64 + 1, 0:128]))
        vw(e, ve.reduce_max(e.km_sb[0:1, 4 + si:5 + si],
                            e.nrm_sb[si * 64:si * 64 + 1, 128:256], axis=AX.X))
    sc.wait_ge(ev.h, ev.n)
    ex.inc(sc.activation(e.km_sb[0:1, 6:6 + nsets], e.km_sb[0:1, 4:4 + nsets], AF.Sqrt), 1)
    ve.wait_ge(ex.h, ex.n)
    vw(e, ve.tensor_copy(e.kmb16[0:1, 6:6 + nsets], e.km_sb[0:1, 6:6 + nsets]))
    te.wait_ge(ev.h, ev.n)
    mm.inc(te.matmul(e.ps_n[:, 0:nsets], e.ones_sb[0:1, :], e.kmb16[0:1, 6:6 + nsets],
                     start=True, stop=True), 1)
    ve.wait_ge(mm.h, mm.n)
    vw(e, ve.tensor_copy(e.kmb_sb[:, 0:nsets], e.ps_n[:, 0:nsets]))
    sc.wait_ge(ev.h, ev.n)
    ex.inc(sc.activation(e.nrm_sb[:, 0:64], e.nrm_sb[:, 0:64], AF.Sqrt), 1)
    ve.wait_ge(ex.h, ex.n)
    for si in range(nsets):
        vw(e, ve.tensor_scalar(e.bd_sb[:, si * 32:si * 32 + 32],
                               e.nrm_sb[:, si * 32:si * 32 + 32],
                               e.kmb_sb[:, si:si + 1], -scale * 1.01,
                               ALU.mult, ALU.mult))

    if e.debug_attn and b == 0 and hl == 0:
        ev.inc(ve.tensor_copy(e.fl_sb[:, 0:1], e.fl_sb[:, 1:2]), 1)
        gg.wait_ge(ev.h, ev.n)
        so.inc(gg.dma_start(out=e.dbg_bd[:, :], in_=e.bd_sb[:, :]), 16)
        so.inc(gg.dma_start(out=e.dbg_nrm[:, :], in_=e.nrm_sb[:, 0:64]), 16)
        so.inc(gg.dma_start(out=e.dbg_kmb[:, :], in_=e.kmb_sb[:, :]), 16)
    rot_ex = [0, 0]
    ptc_ev = [0, 0]
    pts_mm = [0, 0]
    yso = [0, 0]
    pn_mm = [0]
    for qt in range(32):
        i0b = qt * 128
        plan = chunk_plan(layer, i0b)
        nj = len(plan)
        for ci, (jc, mk) in enumerate(plan):
            s1p = (e.ps_a, e.ps_b)[ci % 2]
            s2p = (e.ps_c, e.ps_d)[ci % 2]
            pss = (s1p,) if layer == 0 else (s1p, s2p)
            te.wait_ge(ex.h, rot_ex[ci % 2])
            for si, (base,) in enumerate(halves):
                kw = {} if base == 0 else {"tile_position": (64, 0)}
                npart = 128 // nsets
                inst = te.matmul(pss[si][:, :],
                                 e.q_sb[base:base + npart, qcol + i0b:qcol + i0b + 128],
                                 e.k_sb[base:base + npart,
                                        kcol + jc * 512:kcol + jc * 512 + 512],
                                 start=True, stop=True, **kw)
                mm.inc(inst, 1)
            if mk is not None:
                mi = e.midx[mk]
                ve.wait_ge(mm.h, mm.n)
                for ps in pss:
                    ev.inc(ve.tensor_tensor(ps[:, :], ps[:, :],
                                            e.mask_sb[:, mi * 512:(mi + 1) * 512],
                                            ALU.add), 1)
            sc.wait_ge(mm.h, mm.n)
            sc.wait_ge(ev.h, ev.n)
            for si, ps in enumerate(pss):
                slab = (e.p1f, e.p2f if layer == 1 else None)[si]
                ex.inc(sc.activation(slab[:, ci * 512:(ci + 1) * 512], ps[:, :],
                                     AF.Exp,
                                     bias=e.bd_sb[:, si * 32 + qt:si * 32 + qt + 1],
                                     scale=scale,
                                     accum_out=e.acc_sb[:, si * MAXC + ci:
                                                        si * MAXC + ci + 1]), 1)
            rot_ex[ci % 2] = ex.n
        ve.wait_ge(ex.h, ex.n)
        ve.wait_ge(mm.h, pn_mm[0])
        vw(e, ve.reduce_sum(e.sml_sb[:, 0:1], e.acc_sb[:, 0:nj], axis=AX.X))
        vw(e, ve.reciprocal(e.sml_sb[:, 1:2], e.sml_sb[:, 0:1]))
        if layer == 0:
            ev.inc(ve.tensor_scalar(e.pn_sb[:, 0:nj * 512], e.p1f[:, 0:nj * 512],
                                    e.sml_sb[:, 1:2], None, ALU.mult), 1)
        else:
            vw(e, ve.reduce_sum(e.sml_sb[:, 2:3], e.acc_sb[:, MAXC:MAXC + nj],
                                axis=AX.X))
            vw(e, ve.reciprocal(e.sml_sb[:, 3:4], e.sml_sb[:, 2:3]))
            vw(e, ve.tensor_tensor(e.sml_sb[:, 4:5], e.sml_sb[:, 0:1],
                                   e.sml_sb[:, 3:4], ALU.mult))
            vw(e, ve.tensor_scalar(e.sml_sb[:, 4:5], e.sml_sb[:, 4:5],
                                   e.lamb_sb[:, hl:hl + 1], -1.0,
                                   ALU.mult, ALU.mult))
            ev.inc(ve.scalar_tensor_tensor(e.p2f[:, 0:nj * 512],
                                           e.p2f[:, 0:nj * 512],
                                           e.sml_sb[:, 4:5],
                                           e.p1f[:, 0:nj * 512],
                                           ALU.mult, ALU.add), 1)
            ve.wait_ge(ev.h, ev.n)
            ev.inc(ve.tensor_scalar(e.pn_sb[:, 0:nj * 512], e.p2f[:, 0:nj * 512],
                                    e.sml_sb[:, 1:2], None, ALU.mult), 1)
        if e.debug_attn and b == 0 and hl == 0 and qt == 2:
            ev.inc(ve.tensor_copy(e.fl_sb[:, 0:1], e.fl_sb[:, 1:2]), 1)
            gg.wait_ge(ev.h, ev.n)
            so.inc(gg.dma_start(out=e.dbg_p1[:, 0:nj * 512], in_=e.p1_sb[:, 0:nj * 512]), 16)
            so.inc(gg.dma_start(out=e.dbg_acc[:, 0:2 * e.MAXC], in_=e.acc_sb[:, :]), 16)
            so.inc(gg.dma_start(out=e.dbg_sml[:, :], in_=e.sml_sb[:, :]), 16)
            gg.wait_ge(so.h, so.n)
        first = True
        tr_mm = [0] * nj
        norm_ev = ev.n

        def emit_tr(cj):
            r1 = cj % 2
            pt_ps1 = (e.ps_p0, e.ps_p1)[r1]
            te.wait_ge(ev.h, max(ptc_ev[r1], norm_ev))
            for jt in range(4):
                inst = te.transpose(pt_ps1[:, jt * 128:(jt + 1) * 128],
                                    e.pn_sb[:, cj * 512 + jt * 128:
                                            cj * 512 + jt * 128 + 128],
                                    e.id_sb[:, :])
            e.mm.inc(inst, 1)
            tr_mm[cj] = e.mm.n

        emit_tr(0)
        for ci, (jc, mk) in enumerate(plan):
            r = ci % 2
            pt_ps = (e.ps_p0, e.ps_p1)[r]
            pt_slot = e.pt_sb[:, r * 512:r * 512 + 512]
            ve.wait_ge(mm.h, max(tr_mm[ci], pts_mm[r]))
            ev.inc(ve.tensor_copy(pt_slot, pt_ps[:, :]), 1)
            ptc_ev[r] = ev.n
            cpc = ev.n
            if ci + 1 < nj:
                emit_tr(ci + 1)
            te.wait_ge(ev.h, cpc)
            hofs = hl * 128 if layer == 1 else 0
            for jt in range(4):
                tg = b * 32 + jc * 4 + jt
                inst = te.matmul(e.ps_y[:, :],
                                 e.v_sb[:, tg * (e.KT * 128) + hofs:
                                        tg * (e.KT * 128) + hofs + 128],
                                 pt_slot[:, jt * 128:(jt + 1) * 128],
                                 start=first, stop=(ci == nj - 1 and jt == 3))
                first = False
            mm.inc(inst, 1)
            pts_mm[r] = mm.n
            pn_mm[0] = max(pn_mm[0], tr_mm[ci])
        ve.wait_ge(mm.h, mm.n)
        ev.inc(ve.tensor_copy(e.y_sb[:, i0b:i0b + 128], e.ps_y[:, :]), 1)
    if e.ytm_ap is None:
        ev.inc(ve.tensor_copy(e.fl_sb[:, 0:1], e.fl_sb[:, 1:2]), 1)
        gg.wait_ge(ev.h, ev.n)
        so.inc(gg.dma_start(out=e.y_h[hl * 128:(hl + 1) * 128, b * TB:(b + 1) * TB],
                            in_=e.y_sb[:, :]), 16)
    else:
        te.wait_ge(ev.h, ev.n)
        for tt in range(32):
            r = tt % 2
            pt_ps = (e.ps_p0, e.ps_p1)[r]
            slot = r * 128
            te.wait_ge(ev.h, ptc_ev[r])
            mm.inc(te.transpose(pt_ps[:, 0:128], e.y_sb[:, tt * 128:(tt + 1) * 128],
                                e.id_sb[:, :]), 1)
            ve.wait_ge(mm.h, mm.n)
            ve.wait_ge(so.h, yso[r])
            vw(e, ve.tensor_copy(e.ytp_sb[:, slot:slot + 128], pt_ps[:, 0:128]))
            ptc_ev[r] = ev.n
            gg.wait_ge(ev.h, ev.n)
            so.inc(gg.dma_start(
                out=e.ytm_ap[b * TB + tt * 128:b * TB + (tt + 1) * 128,
                             hl * 128:(hl + 1) * 128],
                in_=e.ytp_sb[:, slot:slot + 128]), 16)
            yso[r] = so.n


def host_attn_inputs(layer, inp):
    x = np.asarray(inp["L0_attn_x"], dtype=np.float32).reshape(T, D)
    masks_np, _ = build_masks(layer)
    masks2 = np.ascontiguousarray(masks_np.reshape(-1, 512))
    maps = []
    for c in range(W):
        m = {"masks": masks2,
             "x_own": np.ascontiguousarray(x[c * TOK:(c + 1) * TOK].T)}
        if layer == 0:
            kv = c // 2
            m["wq"] = np.ascontiguousarray(np.asarray(inp["L0_attn_Wq"])[256 * c:256 * (c + 1)].T)
            m["wk"] = np.ascontiguousarray(np.asarray(inp["L0_attn_Wk"])[128 * kv:128 * (kv + 1)].T)
            m["wv"] = np.ascontiguousarray(np.asarray(inp["L0_attn_Wv"])[128 * kv:128 * (kv + 1)].T)
        else:
            m["wq"] = np.ascontiguousarray(np.asarray(inp["L1_attn_Wq"])[:, 256 * c:256 * (c + 1)])
            m["wk"] = np.ascontiguousarray(np.asarray(inp["L1_attn_Wk"])[:, 256 * c:256 * (c + 1)])
            m["wv"] = np.ascontiguousarray(np.asarray(inp["L1_attn_Wv"])[:, 256 * c:256 * (c + 1)])
            m["lam"] = np.ascontiguousarray(
                np.asarray(inp["L1_attn_lambda_init"])[2 * c:2 * c + 2].reshape(1, 2))
        maps.append(m)
    return maps




def emit_ln(e, out_h):
    nc = e.nc
    sy, gg, ve, sc, te = nc.sync, nc.gpsimd, nc.vector, nc.scalar, nc.tensor
    mm, ev, ex, so = e.mm, e.ev, e.ex, e.so
    vw(e, ve.memset(e.o32_sb[:, :], 1.0))
    vw(e, ve.memset(e.st_sb[0:1, 0:2048], 0.0))
    te.wait_ge(ev.h, ev.n)
    for dt in range(16):
        sc.wait_ge(ev.h, ev.n)
        ex.inc(sc.activation(e.sq1_sb[:, 0:1024], e.xr_sb[:, dt * 1024:(dt + 1) * 1024],
                             AF.Square), 1)
        te.wait_ge(ex.h, ex.n)
        te.wait_ge(ev.h, ev.n)
        for th in range(2):
            inst = te.matmul((e.ps_c, e.ps_d)[th][0:1, 0:512], e.o32_sb[:, 0:1],
                             e.xr_sb[:, dt * 1024 + th * 512:dt * 1024 + th * 512 + 512],
                             start=True, stop=True)
            inst = te.matmul((e.ps_a, e.ps_b)[th][0:1, 0:512], e.o32_sb[:, 0:1],
                             e.sq1_sb[:, th * 512:th * 512 + 512],
                             start=True, stop=True)
        mm.inc(inst, 1)
        ve.wait_ge(mm.h, mm.n)
        vw(e, ve.tensor_tensor(e.st_sb[0:1, 0:512], e.st_sb[0:1, 0:512],
                               e.ps_c[0:1, 0:512], ALU.add))
        vw(e, ve.tensor_tensor(e.st_sb[0:1, 512:1024], e.st_sb[0:1, 512:1024],
                               e.ps_d[0:1, 0:512], ALU.add))
        vw(e, ve.tensor_tensor(e.st_sb[0:1, 1024:1536], e.st_sb[0:1, 1024:1536],
                               e.ps_a[0:1, 0:512], ALU.add))
        vw(e, ve.tensor_tensor(e.st_sb[0:1, 1536:2048], e.st_sb[0:1, 1536:2048],
                               e.ps_b[0:1, 0:512], ALU.add))
    vw(e, ve.tensor_scalar(e.st_sb[0:1, 0:1024], e.st_sb[0:1, 0:1024],
                           1.0 / D, None, ALU.mult))
    vw(e, ve.tensor_scalar(e.st_sb[0:1, 1024:2048], e.st_sb[0:1, 1024:2048],
                           1.0 / D, None, ALU.mult))
    vw(e, ve.tensor_tensor(e.st_sb[0:1, 2048:3072], e.st_sb[0:1, 0:1024],
                           e.st_sb[0:1, 0:1024], ALU.mult))
    vw(e, ve.tensor_tensor(e.st_sb[0:1, 1024:2048], e.st_sb[0:1, 1024:2048],
                           e.st_sb[0:1, 2048:3072], ALU.subtract))
    vw(e, ve.tensor_scalar(e.st_sb[0:1, 1024:2048], e.st_sb[0:1, 1024:2048],
                           1e-5, None, ALU.add))
    sc.wait_ge(ev.h, ev.n)
    ex.inc(sc.activation(e.st_sb[0:1, 2048:3072], e.st_sb[0:1, 1024:2048],
                         AF.Sqrt), 1)
    ve.wait_ge(ex.h, ex.n)
    vw(e, ve.reciprocal(e.st_sb[0:1, 2048:3072], e.st_sb[0:1, 2048:3072]))
    vw(e, ve.tensor_tensor(e.st_sb[0:1, 3072:4096], e.st_sb[0:1, 0:1024],
                           e.st_sb[0:1, 2048:3072], ALU.mult))
    vw(e, ve.tensor_scalar(e.st_sb[0:1, 3072:4096], e.st_sb[0:1, 3072:4096],
                           -1.0, None, ALU.mult))
    vw(e, ve.memset(e.o32_sb[0:1, :], 1.0))
    te.wait_ge(ev.h, ev.n)
    mm.inc(te.matmul(e.ps_ba[:, 0:512], e.o32_sb[0:1, :], e.st_sb[0:1, 2048:2560],
                     start=True, stop=True), 1)
    mm.inc(te.matmul(e.ps_bb[:, 0:512], e.o32_sb[0:1, :], e.st_sb[0:1, 2560:3072],
                     start=True, stop=True), 1)
    mm.inc(te.matmul(e.ps_bc[:, 0:512], e.o32_sb[0:1, :], e.st_sb[0:1, 3072:3584],
                     start=True, stop=True), 1)
    mm.inc(te.matmul(e.ps_bd[:, 0:512], e.o32_sb[0:1, :], e.st_sb[0:1, 3584:4096],
                     start=True, stop=True), 1)
    ve.wait_ge(mm.h, mm.n)
    for dt in range(16):
        ve.wait_ge(so.h, so.n)
        for th in range(2):
            Ab = (e.ps_ba, e.ps_bb)[th]
            Bb = (e.ps_bc, e.ps_bd)[th]
            cols = slice(dt * 1024 + th * 512, dt * 1024 + th * 512 + 512)
            vw(e, ve.tensor_tensor(e.xo_sb[:, th * 512:th * 512 + 512],
                                   e.xr_sb[:, cols], Ab[:, 0:512], ALU.mult))
            vw(e, ve.tensor_tensor(e.xo_sb[:, th * 512:th * 512 + 512],
                                   e.xo_sb[:, th * 512:th * 512 + 512],
                                   Bb[:, 0:512], ALU.add))
        gg.wait_ge(ev.h, ev.n)
        so.inc(gg.dma_start(out=out_h[dt * 128:(dt + 1) * 128, :],
                            in_=e.xo_sb[:, 0:1024]), 16)


def build_o_ln(fz=None):
    pfx = fz["pfx"] if fz else ""
    _NPFX[0] = pfx
    nc = fz["nc"] if fz else bass.Bass("TRN2", target_bir_lowering=False, num_devices=W)
    e = SimpleNamespace()
    e.nc = nc
    if fz:
        y_h = None
        yidx_h = nc.dram_tensor(pfx + "yidx", [128, 64], I32, kind="ExternalInput").ap()
        x_h = fz["x_ap"]
        xn_h = fz["out_ap"]
    else:
        y_h = nc.dram_tensor("y", [D, TOK], BF16, kind="ExternalInput").ap()
        x_h = nc.dram_tensor("x_own", [D, TOK], F32, kind="ExternalInput").ap()
        xn_h = nc.dram_tensor("xn", [D, TOK], F32, kind="ExternalOutput").ap()
    wo_h = nc.dram_tensor(pfx + "wo", [D, D], F32, kind="ExternalInput").ap()
    ones_i = nc.inline_tensor(np.ones((128, 128), dtype=bfdt), name=pfx + "ones_c").ap()
    ident_i = nc.inline_tensor(np.eye(128, dtype=bfdt), name=pfx + "ident_c").ap()

    es = contextlib.ExitStack()
    with es:
        wo_sb = sbufT(es, nc, "wo_sb", [128, 16 * D], BF16)
        yk_sb = sbufT(es, nc, "yk_sb", [128, 16 * TOK], BF16)
        e.xr_sb = sbufT(es, nc, "xr_sb", [128, 16 * TOK], F32)
        e.xo_sb = sbufT(es, nc, "xo_sb", [128, TOK], F32)
        e.xb1_sb = sbufT(es, nc, "xb1_sb", [128, TOK], F32)
        e.sq1_sb = sbufT(es, nc, "sq1_sb", [128, TOK], F32)
        e.st_sb = sbufT(es, nc, "st_sb", [1, 4096], F32)
        e.o32_sb = sbufT(es, nc, "o32_sb", [128, 128], F32)
        e.ones_sb = sbufT(es, nc, "ones_sb", [128, 128], BF16)
        e.fl_sb = sbufT(es, nc, "fl_sb", [128, 2], F32)
        e.ps_a = psumT(es, nc, "ps_a", [128, 512], F32)
        e.ps_b = psumT(es, nc, "ps_b", [128, 512], F32)
        e.ps_c = psumT(es, nc, "ps_c", [128, 512], F32)
        e.ps_d = psumT(es, nc, "ps_d", [128, 512], F32)
        e.ps_ba = psumT(es, nc, "ps_ba", [128, 512], F32)
        e.ps_bb = psumT(es, nc, "ps_bb", [128, 512], F32)
        e.ps_bc = psumT(es, nc, "ps_bc", [128, 512], F32)
        e.ps_bd = psumT(es, nc, "ps_bd", [128, 512], F32)

        if fz:
            sm = fz["sems"]
            ld, ag, xt = sm["ld"], sm["ag"], sm["xt"]
            e.mm = mm = sm["mm"]; e.ev = ev = sm["ev"]
            e.ex = ex = sm["ex"]; e.so = so = sm["so"]
        else:
            ld = Sem(es, nc, "ld")
            e.mm = mm = Sem(es, nc, "mm")
            e.ev = ev = Sem(es, nc, "ev")
            e.ex = ex = Sem(es, nc, "ex")
            e.so = so = Sem(es, nc, "so")
        sy, gg, ve, sc, te = nc.sync, nc.gpsimd, nc.vector, nc.scalar, nc.tensor

        ld.inc(gg.dma_start(out=wo_sb[:].rearrange("p (c o) -> p c o", c=16),
                            in_=wo_h.rearrange("(c p) o -> p c o", p=128)), 16)
        if not fz:
            ld.inc(gg.dma_start(out=yk_sb[:].rearrange("p (c t) -> p c t", c=16),
                                in_=y_h.rearrange("(c p) t -> p c t", p=128)), 16)
        ld.inc(sy.dma_start(out=e.xr_sb[:].rearrange("p (c t) -> p c t", c=16),
                            in_=x_h.rearrange("(c p) t -> p c t", p=128)), 16)
        ld.inc(sy.dma_start(out=e.ones_sb[:], in_=ones_i[:, :]), 16)
        if fz:
            id_sb = sbufT(es, nc, "id_sb", [128, 128], BF16)
            yidx_sb = sbufT(es, nc, "yidx_sb", [128, 64], I32)
            ygb_sb = sbufT(es, nc, "ygb_sb", [128, 256], BF16)
            ps_p0 = e.ps_a[:, 0:64].bitcast(BF16)
            ps_p1 = e.ps_b[:, 0:64].bitcast(BF16)
            ld.inc(sy.dma_start(out=id_sb[:], in_=ident_i[:, :]), 16)
            ld.inc(sy.dma_start(out=yidx_sb[:], in_=yidx_h[:, :]), 16)
            gg.wait_ge(e.so.h, e.so.n)
            ag.inc(gg.collective_compute("AllGather", ALU.bypass,
                                         replica_groups=[list(range(W))],
                                         ins=[fz["ytm"][:, :]],
                                         outs=[fz["ygath"][:, :]]), 1)
            gg.wait_ge(ag.h, ag.n)
            gg.wait_ge(ld.h, ld.n)
            for r in range(W):
                for tt in range(8):
                    col = r * 8 + tt
                    gg.wait_ge(ev.h, ev.n)
                    xt.inc(gg.indirect_dma_start(
                        ygb_sb[:, :], None, fz["ygath"][:, :],
                        IndirectOffsetOnAxis(yidx_sb[:, col:col + 1], 0)), 16)
                    te.wait_ge(xt.h, xt.n)
                    te.wait_ge(ev.h, ev.n)
                    for half in range(2):
                        pt = (ps_p0, ps_p1)[half]
                        mm.inc(te.transpose(pt, ygb_sb[:, half * 128:(half + 1) * 128],
                                            id_sb[:, :]), 1)
                    ve.wait_ge(mm.h, mm.n)
                    for half in range(2):
                        pt = (ps_p0, ps_p1)[half]
                        kc = 2 * r + half
                        ev.inc(ve.tensor_copy(
                            yk_sb[:, kc * TOK + tt * 128:kc * TOK + tt * 128 + 128],
                            pt), 1)

        te.wait_ge(ld.h, ld.n)
        te.wait_ge(ev.h, ev.n)
        for dt in range(16):
            for th in range(2):
                ps = (e.ps_a, e.ps_b)[th]
                te.wait_ge(ev.h, ev.n)
                for kc in range(16):
                    inst = te.matmul(ps[:, :],
                                     wo_sb[:, kc * D + dt * 128:kc * D + dt * 128 + 128],
                                     yk_sb[:, kc * TOK + th * 512:kc * TOK + th * 512 + 512],
                                     start=(kc == 0), stop=(kc == 15))
                mm.inc(inst, 1)
                ve.wait_ge(mm.h, mm.n)
                cols = slice(dt * 1024 + th * 512, dt * 1024 + th * 512 + 512)
                ev.inc(ve.tensor_tensor(e.xr_sb[:, cols], e.xr_sb[:, cols],
                                        ps[:, :], ALU.add), 1)
        emit_ln(e, xn_h)
        gg.wait_ge(so.h, so.n)
    return nc


def build_ffn(layer, debug_ffn=False, fz=None):
    NS = 1024 if layer == 0 else 2048
    NSUB = NS // 1024
    pfx = fz["pfx"] if fz else ""
    _NPFX[0] = pfx
    nc = fz["nc"] if fz else bass.Bass("TRN2", target_bir_lowering=False, num_devices=W)
    e = SimpleNamespace()
    e.nc = nc
    x_h = fz["x_ap"] if fz else nc.dram_tensor("x_own", [D, TOK], F32, kind="ExternalInput").ap()
    if layer == 0:
        wg_h = nc.dram_tensor(pfx + "wg", [D, NS], I32, kind="ExternalInput").ap()
        wu_h = nc.dram_tensor(pfx + "wu", [D, NS], I32, kind="ExternalInput").ap()
        wd_h = nc.dram_tensor(pfx + "wd", [NS, D], I32, kind="ExternalInput").ap()
        gsc_h = nc.dram_tensor(pfx + "gsc", [128, 8], F32, kind="ExternalInput").ap()
        usc_h = nc.dram_tensor(pfx + "usc", [128, 8], F32, kind="ExternalInput").ap()
        dsc_h = nc.dram_tensor(pfx + "dsc", [128, 16], F32, kind="ExternalInput").ap()
    elif fz:
        wg_h, wu_h, wd_h = fz["wg_ap"], fz["wu_ap"], fz["wd_ap"]
    else:
        wg_h = nc.dram_tensor("wg", [D, NS], BF16, kind="ExternalInput").ap()
        wu_h = nc.dram_tensor("wu", [D, NS], BF16, kind="ExternalInput").ap()
        wd_h = nc.dram_tensor("wd", [NS, D], BF16, kind="ExternalInput").ap()
    xn_h = fz["out_ap"] if fz else nc.dram_tensor("xn", [D, TOK], F32, kind="ExternalOutput").ap()
    if debug_ffn:
        drs_h = nc.dram_tensor("drs", [D, TOK], F32, kind="ExternalOutput").ap()
        dit_h = nc.dram_tensor("dit", [128, 8 * 512], BF16, kind="ExternalOutput").ap()
        dpt_h = nc.dram_tensor("dpt", [D, TOK], F32, kind="ExternalOutput").ap()
        dxg_h = nc.dram_tensor("dxg", [W * D, TOK], BF16, kind="ExternalOutput").ap()
    ones_i = nc.inline_tensor(np.ones((128, 128), dtype=bfdt), name=pfx + "ones_c").ap()

    if fz:
        agi, xg, part, rso = fz["agi"], fz["xg"], fz["part"], fz["rso"]
    else:
        agi = nc.dram_tensor("agi", [D, TOK], BF16, kind="Internal").ap()
        xg = nc.dram_tensor("xg", [W * D, TOK], BF16, kind="Internal").ap()
        part = nc.dram_tensor("part", [W * D, TOK], F32, kind="Internal").ap()
        rso = nc.dram_tensor("rso", [D, TOK], F32, kind="Internal").ap()

    es = contextlib.ExitStack()
    with es:
        e.xr_sb = sbufT(es, nc, "xr_sb", [128, 16 * TOK], F32)
        _wgu = e.xr_sb[:, :].bitcast(BF16)
        wg_sb = _wgu[:, 0:16 * 1024]
        wu_sb = _wgu[:, 16 * 1024:32 * 1024]
        wd_sb = sbufT(es, nc, "wd_sb", [128, 8 * D], BF16)
        wi_sb = sbufT(es, nc, "wi_sb", [128, 2 * 2048], I32)
        xk_sb = sbufT(es, nc, "xk_sb", [128, 2 * 16 * 512], BF16)
        it_sb = sbufT(es, nc, "it_sb", [128, 8 * 512], BF16)
        t1_sb = sbufT(es, nc, "t1_sb", [128, 2 * 512], BF16)
        t2_sb = sbufT(es, nc, "t2_sb", [128, 2 * 512], F32)
        po_sb = sbufT(es, nc, "po_sb", [128, 4 * 512], F32)
        e.xo_sb = sbufT(es, nc, "xo_sb", [128, TOK], F32)
        e.xb1_sb = sbufT(es, nc, "xb1_sb", [128, TOK], F32)
        e.sq1_sb = sbufT(es, nc, "sq1_sb", [128, TOK], F32)
        e.st_sb = sbufT(es, nc, "st_sb", [1, 4096], F32)
        e.o32_sb = sbufT(es, nc, "o32_sb", [128, 128], F32)
        e.ones_sb = sbufT(es, nc, "ones_sb", [128, 128], BF16)
        e.fl_sb = sbufT(es, nc, "fl_sb", [128, 2], F32)
        if layer == 0:
            gsc_sb = sbufT(es, nc, "gsc_sb", [128, 8], F32)
            usc_sb = sbufT(es, nc, "usc_sb", [128, 8], F32)
            dsc_sb = sbufT(es, nc, "dsc_sb", [128, 16], F32)

        e.ps_a = psumT(es, nc, "ps_a", [128, 512], F32)
        e.ps_b = psumT(es, nc, "ps_b", [128, 512], F32)
        e.ps_c = psumT(es, nc, "ps_c", [128, 512], F32)
        e.ps_d = psumT(es, nc, "ps_d", [128, 512], F32)
        e.ps_ba = psumT(es, nc, "ps_ba", [128, 512], F32)
        e.ps_bb = psumT(es, nc, "ps_bb", [128, 512], F32)
        e.ps_bc = psumT(es, nc, "ps_bc", [128, 512], F32)
        e.ps_bd = psumT(es, nc, "ps_bd", [128, 512], F32)

        if fz:
            sm = fz["sems"]
            ld, ag, wl, xt = sm["ld"], sm["ag"], sm["wl"], sm["xt"]
            e.mm = mm = sm["mm"]; e.ev = ev = sm["ev"]
            e.ex = ex = sm["ex"]; e.so = so = sm["so"]
        else:
            ld = Sem(es, nc, "ld")
            ag = Sem(es, nc, "ag")
            wl = Sem(es, nc, "wl")
            xt = Sem(es, nc, "xt")
            e.mm = mm = Sem(es, nc, "mm")
            e.ev = ev = Sem(es, nc, "ev")
            e.ex = ex = Sem(es, nc, "ex")
            e.so = so = Sem(es, nc, "so")
        sy, gg, ve, sc, te = nc.sync, nc.gpsimd, nc.vector, nc.scalar, nc.tensor

        ld.inc(gg.dma_start(out=agi[:, :], in_=x_h[:, :]), 16)
        ld.inc(sy.dma_start(out=e.ones_sb[:], in_=ones_i[:, :]), 16)
        if layer == 0:
            ld.inc(sy.dma_start(out=gsc_sb[:], in_=gsc_h[:, :]), 16)
            ld.inc(sy.dma_start(out=usc_sb[:], in_=usc_h[:, :]), 16)
            ld.inc(sy.dma_start(out=dsc_sb[:], in_=dsc_h[:, :]), 16)
        gg.wait_ge(ld.h, ld.n)
        ag.inc(gg.collective_compute("AllGather", ALU.bypass,
                                     replica_groups=[list(range(W))],
                                     ins=[agi[:, :]], outs=[xg[:, :]]), 1)
        sy.wait_ge(ag.h, ag.n)
        gg.wait_ge(ag.h, ag.n)

        def xg_win(tt):
            r, cs = tt // 2, 512 * (tt % 2)
            return (xg[r * D:(r + 1) * D, cs:cs + 512]
                    .rearrange("(c p) t -> p c t", p=128))

        xk_mm = [0, 0]
        xt_at = [0, 0]
        g_ex = [0, 0]
        g_ev = [0, 0]
        t_ev = [0, 0]
        o_ev = [0, 0]
        po_so = [0, 0, 0, 0]
        it_mm = [0]
        for sub in range(NSUB):
            if sub > 0:
                gg.wait_ge(mm.h, mm.n)
                ve.wait_ge(mm.h, mm.n)
            if layer == 0:
                def deq_load(w_h, dst_sb, nrows, width, iblk):
                    for ib in range(nrows):
                        h = ib % 2
                        if ib >= 2 or iblk > 0:
                            gg.wait_ge(ev.h, ev.n)
                        wl.inc(gg.dma_start(out=wi_sb[:, h * 2048:h * 2048 + width],
                                            in_=w_h[ib * 128:(ib + 1) * 128, :]), 16)
                        ve.wait_ge(wl.h, wl.n)
                        ev.inc(ve.tensor_scalar(dst_sb[:, ib * width:(ib + 1) * width],
                                                wi_sb[:, h * 2048:h * 2048 + width],
                                                -128.0, None, ALU.add), 1)
                deq_load(wg_h, wg_sb, 16, 1024, 0)
                deq_load(wu_h, wu_sb, 16, 1024, 1)
                deq_load(wd_h, wd_sb, 8, 2048, 2)
            else:
                wl.inc(gg.dma_start(out=wg_sb[:].rearrange("p (c n) -> p c n", c=16),
                                    in_=wg_h[:, sub * 1024:(sub + 1) * 1024]
                                    .rearrange("(c p) n -> p c n", p=128)), 16)
                wl.inc(gg.dma_start(out=wu_sb[:].rearrange("p (c n) -> p c n", c=16),
                                    in_=wu_h[:, sub * 1024:(sub + 1) * 1024]
                                    .rearrange("(c p) n -> p c n", p=128)), 16)
                wl.inc(gg.dma_start(out=wd_sb[:].rearrange("p (c n) -> p c n", c=8),
                                    in_=wd_h[sub * 1024:(sub + 1) * 1024, :]
                                    .rearrange("(c p) n -> p c n", p=128)), 16)
            te.wait_ge(wl.h, wl.n)
            te.wait_ge(ev.h, ev.n)

            for tt in range(16):
                xslot = tt % 2
                sy.wait_ge(mm.h, xk_mm[xslot])
                xt.inc(sy.dma_start(out=xk_sb[:, xslot * 8192:(xslot + 1) * 8192]
                                    .rearrange("p (c t) -> p c t", c=16),
                                    in_=xg_win(tt)), 16)
                xt_at[xslot] = xt.n
                xkv = xk_sb[:, xslot * 8192:(xslot + 1) * 8192]
                te.wait_ge(xt.h, xt_at[xslot])
                for nch in range(8):
                    r = nch % 2
                    psg = (e.ps_a, e.ps_b)[r]
                    psu = (e.ps_c, e.ps_d)[r]
                    te.wait_ge(ex.h, g_ex[r])
                    te.wait_ge(ev.h, g_ev[r])
                    for kc in range(16):
                        inst = te.matmul(psg[:, :],
                                         wg_sb[:, kc * 1024 + nch * 128:
                                               kc * 1024 + nch * 128 + 128],
                                         xkv[:, kc * 512:(kc + 1) * 512],
                                         start=(kc == 0), stop=(kc == 15))
                    for kc in range(16):
                        inst = te.matmul(psu[:, :],
                                         wu_sb[:, kc * 1024 + nch * 128:
                                               kc * 1024 + nch * 128 + 128],
                                         xkv[:, kc * 512:(kc + 1) * 512],
                                         start=(kc == 0), stop=(kc == 15))
                    mm.inc(inst, 1)
                    sc.wait_ge(mm.h, mm.n)
                    sc.wait_ge(ev.h, t_ev[r])
                    t1v = t1_sb[:, r * 512:(r + 1) * 512]
                    t2v = t2_sb[:, r * 512:(r + 1) * 512]
                    ve.wait_ge(mm.h, it_mm[0])
                    if layer == 0:
                        ex.inc(sc.activation(t1v, psg[:, :], AF.Silu,
                                             scale=gsc_sb[:, sub * 8 + nch:
                                                          sub * 8 + nch + 1]), 1)
                        ve.wait_ge(ex.h, ex.n)
                        ev.inc(ve.scalar_tensor_tensor(
                            it_sb[:, nch * 512:(nch + 1) * 512], psu[:, :],
                            usc_sb[:, sub * 8 + nch:sub * 8 + nch + 1],
                            t1v, ALU.mult, ALU.mult), 1)
                    else:
                        ex.inc(sc.activation(t1v, psg[:, :], AF.Square), 1)
                        ve.wait_ge(ex.h, ex.n)
                        vw(e, ve.tensor_scalar(t2v, t1v,
                                               0.7978845608028654 * 0.044715,
                                               0.7978845608028654, ALU.mult, ALU.add))
                        vw(e, ve.tensor_tensor(t2v, t2v, psg[:, :], ALU.mult))
                        sc.wait_ge(ev.h, ev.n)
                        ex.inc(sc.activation(t2v, t2v, AF.Tanh), 1)
                        ve.wait_ge(ex.h, ex.n)
                        vw(e, ve.scalar_tensor_tensor(t2v, t2v, 1.0,
                                                      psg[:, :], ALU.add, ALU.mult))
                        ev.inc(ve.scalar_tensor_tensor(
                            it_sb[:, nch * 512:(nch + 1) * 512], psu[:, :], 0.5,
                            t2v, ALU.mult, ALU.mult), 1)
                    g_ex[r] = ex.n
                    g_ev[r] = ev.n
                    t_ev[r] = ev.n
                for ht in range(16):
                    r = ht % 2
                    pso = (e.ps_ba, e.ps_bb)[r]
                    te.wait_ge(ev.h, o_ev[r])
                    if ht == 0:
                        te.wait_ge(ev.h, ev.n)
                    for nch in range(8):
                        inst = te.matmul(pso[:, :],
                                         wd_sb[:, nch * D + ht * 128:
                                               nch * D + ht * 128 + 128],
                                         it_sb[:, nch * 512:(nch + 1) * 512],
                                         start=(nch == 0), stop=(nch == 7))
                    mm.inc(inst, 1)
                    ve.wait_ge(mm.h, mm.n)
                    ps4 = ht % 4
                    ve.wait_ge(so.h, po_so[ps4])
                    pc = po_sb[:, ps4 * 512:ps4 * 512 + 512]
                    if layer == 0:
                        ev.inc(ve.tensor_scalar(pc, pso[:, :],
                                                dsc_sb[:, ht:ht + 1], None,
                                                ALU.mult), 1)
                    else:
                        ev.inc(ve.tensor_copy(pc, pso[:, :]), 1)
                    o_ev[r] = ev.n
                    gg.wait_ge(ev.h, ev.n)
                    rr, cs = tt // 2, 512 * (tt % 2)
                    dst = part[rr * D + ht * 128:rr * D + ht * 128 + 128, cs:cs + 512]
                    if sub == 0:
                        so.inc(gg.dma_start(out=dst, in_=pc), 16)
                    else:
                        so.inc(gg.dma_start(out=dst, in_=pc,
                                            accum_op=ALU.add), 16)
                    po_so[ps4] = so.n
                xk_mm[xslot] = mm.n
                it_mm[0] = mm.n
        gg.wait_ge(so.h, so.n)
        ag.inc(gg.collective_compute("ReduceScatter", ALU.add,
                                     replica_groups=[list(range(W))],
                                     ins=[part[:, :]], outs=[rso[:, :]]), 1)
        sy.wait_ge(ag.h, ag.n)
        if debug_ffn:
            gg.wait_ge(ag.h, ag.n)
            so.inc(gg.dma_start(out=drs_h[:, :], in_=rso[:, :]), 16)
        ld.inc(sy.dma_start(out=e.xr_sb[:].rearrange("p (c t) -> p c t", c=16),
                            in_=x_h.rearrange("(c p) t -> p c t", p=128)), 16)
        for dt in range(16):
            if dt > 0:
                sy.wait_ge(ev.h, ev.n)
            ld.inc(sy.dma_start(out=e.xo_sb[:, 0:1024],
                                in_=rso[dt * 128:(dt + 1) * 128, :]), 16)
            ve.wait_ge(ld.h, ld.n)
            ev.inc(ve.tensor_tensor(e.xr_sb[:, dt * 1024:(dt + 1) * 1024],
                                    e.xr_sb[:, dt * 1024:(dt + 1) * 1024],
                                    e.xo_sb[:, 0:1024], ALU.add), 1)
        emit_ln(e, xn_h)
        gg.wait_ge(so.h, so.n)
    return nc


def emit_decode(e, nib, out_bf, tmp):
    ve = e.nc.vector
    m, t, u = tmp
    ve.tensor_single_scalar(m[:, :], nib[:, :], 7, ALU.bitwise_and)
    ve.wait_ge(e.ev.h, e.ev.n)
    vw(e, ve.tensor_single_scalar(t[:, :], m[:, :], 1, ALU.is_equal))
    vw(e, ve.tensor_tensor(u[:, :], m[:, :], t[:, :], ALU.subtract))
    vw(e, ve.tensor_single_scalar(u[:, :], u[:, :], 252, ALU.add))
    vw(e, ve.tensor_single_scalar(t[:, :], m[:, :], 1, ALU.is_ge))
    vw(e, ve.tensor_tensor(u[:, :], u[:, :], t[:, :], ALU.mult))
    vw(e, ve.tensor_single_scalar(u[:, :], u[:, :], 6, ALU.logical_shift_left))
    vw(e, ve.tensor_single_scalar(t[:, :], nib[:, :], 8, ALU.bitwise_and))
    vw(e, ve.tensor_single_scalar(t[:, :], t[:, :], 12, ALU.logical_shift_left))
    vw(e, ve.tensor_tensor(u[:, :], u[:, :], t[:, :], ALU.bitwise_or))
    vw(e, ve.tensor_copy(out_bf[:, :].bitcast(mybir.dt.uint16), u[:, :]))


def build_decode(fz=None):
    pfx = fz["pfx"] if fz else ""
    _NPFX[0] = pfx
    nc = fz["nc"] if fz else bass.Bass("TRN2", target_bir_lowering=False, num_devices=W)
    e = SimpleNamespace()
    e.nc = nc
    ins, outs, scs = {}, {}, {}
    for nm in ("g", "u", "d"):
        ins[nm] = nc.dram_tensor(pfx + nm + "pk", [1024, 2048], I32, kind="ExternalInput").ap()
        scs[nm] = nc.dram_tensor(pfx + nm + "scT", [64, 2048], F32, kind="ExternalInput").ap()
        if fz:
            outs[nm] = fz[nm + "dec"]
        else:
            outs[nm] = nc.dram_tensor(nm + "dec", [2048, 2048], BF16, kind="ExternalOutput").ap()
    G_np = np.zeros((8, 128), dtype=np.float32)
    for a in range(8):
        G_np[a, a * 16:(a + 1) * 16] = 1
    G_i = nc.inline_tensor(G_np, name=pfx + "G_c").ap()

    es = contextlib.ExitStack()
    with es:
        pk_sb = sbufT(es, nc, "pk_sb", [128, 2 * 2048], I32)
        lo_sb = sbufT(es, nc, "lo_sb", [128, 2048], I32)
        hi_sb = sbufT(es, nc, "hi_sb", [128, 2048], I32)
        m_sb = sbufT(es, nc, "m_sb", [128, 2048], I32)
        t_sb = sbufT(es, nc, "t_sb", [128, 2048], I32)
        u_sb = sbufT(es, nc, "u_sb", [128, 2048], I32)
        dlo_sb = sbufT(es, nc, "dlo_sb", [128, 2048], BF16)
        dhi_sb = sbufT(es, nc, "dhi_sb", [128, 2048], BF16)
        olo_sb = sbufT(es, nc, "olo_sb", [128, 2048], BF16)
        ohi_sb = sbufT(es, nc, "ohi_sb", [128, 2048], BF16)
        sc_sb = sbufT(es, nc, "sc_sb", [8, 2 * 2048], F32)
        G_sb = sbufT(es, nc, "G_sb", [8, 128], F32)
        ps_sc = psumT(es, nc, "ps_sc", [128, 512], F32)
        ps_s2 = psumT(es, nc, "ps_s2", [128, 512], F32)

        if fz:
            sm = fz["sems"]
            ld = sm["ld"]
            e.mm = mm = sm["mm"]; e.ev = ev = sm["ev"]; e.so = so = sm["so"]
        else:
            ld = Sem(es, nc, "ld")
            e.mm = mm = Sem(es, nc, "mm")
            e.ev = ev = Sem(es, nc, "ev")
            e.so = so = Sem(es, nc, "so")
        sy, gg, ve, te = nc.sync, nc.gpsimd, nc.vector, nc.tensor

        ld.inc(sy.dma_start(out=G_sb[:], in_=G_i[:, :]), 16)
        for nm in ("g", "u", "d"):
            for rt in range(8):
                h = rt % 2
                sy.wait_ge(ev.h, ev.n)
                ve.wait_ge(so.h, so.n)
                ld.inc(sy.dma_start(out=pk_sb[:, h * 2048:(h + 1) * 2048],
                                    in_=ins[nm][rt * 128:(rt + 1) * 128, :]), 16)
                ld.inc(sy.dma_start(out=sc_sb[0:8, h * 2048:(h + 1) * 2048],
                                    in_=scs[nm][rt * 8:(rt + 1) * 8, :]), 16)
                ve.wait_ge(ld.h, ld.n)
                vw(e, ve.tensor_single_scalar(lo_sb[:, :],
                                              pk_sb[:, h * 2048:(h + 1) * 2048],
                                              15, ALU.bitwise_and))
                vw(e, ve.tensor_single_scalar(hi_sb[:, :],
                                              pk_sb[:, h * 2048:(h + 1) * 2048],
                                              4, ALU.logical_shift_right))
                emit_decode(e, lo_sb, dlo_sb, (m_sb, t_sb, u_sb))
                emit_decode(e, hi_sb, dhi_sb, (m_sb, t_sb, u_sb))
                for cb in range(4):
                    psc = (ps_sc, ps_s2)[cb % 2]
                    te.wait_ge(ev.h, ev.n)
                    mm.inc(te.matmul(psc[:, :], G_sb[0:8, :],
                                     sc_sb[0:8, h * 2048 + cb * 512:
                                           h * 2048 + cb * 512 + 512],
                                     start=True, stop=True), 1)
                    ve.wait_ge(mm.h, mm.n)
                    cbs = slice(cb * 512, (cb + 1) * 512)
                    ev.inc(ve.tensor_tensor(olo_sb[:, cbs], dlo_sb[:, cbs],
                                            psc[:, :], ALU.mult), 1)
                    ev.inc(ve.tensor_tensor(ohi_sb[:, cbs], dhi_sb[:, cbs],
                                            psc[:, :], ALU.mult), 1)
                gg.wait_ge(ev.h, ev.n)
                if nm == "d":
                    so.inc(gg.dma_start(out=outs[nm][(2 * rt) * 128:(2 * rt + 1) * 128, :],
                                        in_=olo_sb[:, :]), 16)
                    so.inc(gg.dma_start(out=outs[nm][(2 * rt + 1) * 128:(2 * rt + 2) * 128, :],
                                        in_=ohi_sb[:, :]), 16)
                else:
                    so.inc(gg.dma_start(out=outs[nm][256 * rt:256 * rt + 256:2, :],
                                        in_=olo_sb[:, :]), 16)
                    so.inc(gg.dma_start(out=outs[nm][256 * rt + 1:256 * rt + 256:2, :],
                                        in_=ohi_sb[:, :]), 16)
                gg.wait_ge(so.h, so.n)
        gg.wait_ge(so.h, so.n)
    return nc


def l1_perm(c):
    perm = np.zeros(2048, dtype=np.int64)
    for ci in range(16):
        for p in range(128):
            perm[ci * 128 + p] = 2048 * c + 256 * (ci // 2) + 2 * p + (ci % 2)
    return perm


def host_decode_inputs(inp):
    gp = np.asarray(inp["L1_ffn_gate_W_packed"])
    up = np.asarray(inp["L1_ffn_up_W_packed"])
    dp = np.asarray(inp["L1_ffn_down_W_packed"])
    gs = np.asarray(inp["L1_ffn_gate_scales"])
    us = np.asarray(inp["L1_ffn_up_scales"])
    ds = np.asarray(inp["L1_ffn_down_scales"])
    gpT = gp.T
    upT = up.T
    dpT = dp.T
    maps = []
    for c in range(W):
        perm = l1_perm(c)
        m = {
            "gpk": np.ascontiguousarray(gpT[:, perm]),
            "upk": np.ascontiguousarray(upT[:, perm]),
            "dpk": np.ascontiguousarray(dpT[1024 * c:1024 * (c + 1), :]),
            "gscT": np.ascontiguousarray(gs[perm, :].T),
            "uscT": np.ascontiguousarray(us[perm, :].T),
            "dscT": np.ascontiguousarray(ds[:, 64 * c:64 * (c + 1)].T),
        }
        maps.append(m)
    return maps


def host_oln_inputs(layer, inp, xcur, ys):
    x = np.asarray(xcur, dtype=np.float32).reshape(T, D)
    wo = np.asarray(inp["L0_attn_Wo"]).T if layer == 0 else np.asarray(inp["L1_attn_Wo"])
    wo = np.ascontiguousarray(wo.astype(np.float32))
    maps = []
    for c in range(W):
        yfm = np.ascontiguousarray(
            np.concatenate([np.asarray(ys[r])[:, c * TOK:(c + 1) * TOK]
                            for r in range(W)], axis=0))
        maps.append({"y": yfm, "wo": wo,
                     "x_own": np.ascontiguousarray(x[c * TOK:(c + 1) * TOK].T)})
    return maps


def host_ffn_inputs(layer, inp, xcur, decoded=None):
    x = np.asarray(xcur, dtype=np.float32).reshape(T, D)
    maps = []
    for c in range(W):
        m = {"x_own": np.ascontiguousarray(x[c * TOK:(c + 1) * TOK].T)}
        if layer == 0:
            s = slice(1024 * c, 1024 * (c + 1))
            m["wg"] = np.ascontiguousarray(np.asarray(inp["L0_ffn_gate_w"])[s].T)
            m["wu"] = np.ascontiguousarray(np.asarray(inp["L0_ffn_up_w"])[s].T)
            m["wd"] = np.ascontiguousarray(np.asarray(inp["L0_ffn_down_w"])[:, s].T)
            m["gsc"] = np.ascontiguousarray(
                np.asarray(inp["L0_ffn_gate_scale"])[s].reshape(8, 128).T)
            m["usc"] = np.ascontiguousarray(
                np.asarray(inp["L0_ffn_up_scale"])[s].reshape(8, 128).T)
            m["dsc"] = np.ascontiguousarray(
                np.asarray(inp["L0_ffn_down_scale"]).reshape(16, 128).T)
        else:
            m["wg"] = decoded[c]["gdec"]
            m["wu"] = decoded[c]["udec"]
            m["wd"] = decoded[c]["ddec"]
        maps.append(m)
    return maps




def build_fused():
    nc = bass.Bass("TRN2", target_bir_lowering=False, num_devices=W)
    es_top = contextlib.ExitStack()
    nc._fused_es = es_top
    sems = {nm: Sem(es_top, nc, nm) for nm in
            ("ld", "cv", "ag", "xt", "wl", "mm", "ev", "ex", "so")}

    x0 = nc.dram_tensor("x_own", [D, TOK], F32, kind="ExternalInput").ap()
    out_h = nc.dram_tensor("xn", [D, TOK], F32, kind="ExternalOutput").ap()
    agi = nc.dram_tensor("agi", [D, TOK], BF16, kind="Internal").ap()
    xg = nc.dram_tensor("xg", [W * D, TOK], BF16, kind="Internal").ap()
    part = nc.dram_tensor("part", [W * D, TOK], F32, kind="Internal").ap()
    rso = nc.dram_tensor("rso", [D, TOK], F32, kind="Internal").ap()
    ytm = nc.dram_tensor("ytm", [T, 256], BF16, kind="Internal").ap()
    ygath = nc.dram_tensor("ygath", [W * T, 256], BF16, kind="Internal").ap()
    x1 = nc.dram_tensor("x1", [D, TOK], F32, kind="Internal").ap()
    x2 = nc.dram_tensor("x2", [D, TOK], F32, kind="Internal").ap()
    x3 = nc.dram_tensor("x3", [D, TOK], F32, kind="Internal").ap()
    gdec = nc.dram_tensor("gdec", [2048, 2048], BF16, kind="Internal").ap()
    udec = nc.dram_tensor("udec", [2048, 2048], BF16, kind="Internal").ap()
    ddec = nc.dram_tensor("ddec", [2048, 2048], BF16, kind="Internal").ap()

    base = dict(nc=nc, sems=sems, agi=agi, xg=xg, part=part, rso=rso,
                ytm=ytm, ygath=ygath)

    with nc.named_scope("dec"):
        build_decode(fz={**base, "pfx": "dc_", "gdec": gdec, "udec": udec, "ddec": ddec})
        nc.all_engine_barrier()
    with nc.named_scope("a0"):
        build_attn(0, fz={**base, "pfx": "a0_", "x_ap": x0})
        nc.all_engine_barrier()
    with nc.named_scope("o0"):
        build_o_ln(fz={**base, "pfx": "o0_", "x_ap": x0, "out_ap": x1})
        nc.all_engine_barrier()
    with nc.named_scope("f0"):
        build_ffn(0, fz={**base, "pfx": "f0_", "x_ap": x1, "out_ap": x2})
        nc.all_engine_barrier()
    with nc.named_scope("a1"):
        build_attn(1, fz={**base, "pfx": "a1_", "x_ap": x2})
        nc.all_engine_barrier()
    with nc.named_scope("o1"):
        build_o_ln(fz={**base, "pfx": "o1_", "x_ap": x2, "out_ap": x3})
        nc.all_engine_barrier()
    with nc.named_scope("f1"):
        build_ffn(1, fz={**base, "pfx": "f1_", "x_ap": x3,
                         "wg_ap": gdec, "wu_ap": udec, "wd_ap": ddec,
                         "out_ap": out_h})
    es_top.close()
    return nc


def host_fused_inputs(inp):
    x = np.asarray(inp["L0_attn_x"], dtype=np.float32).reshape(T, D)
    a0 = host_attn_inputs(0, inp)
    a1 = host_attn_inputs(1, inp)
    dc = host_decode_inputs(inp)
    f0 = host_ffn_inputs(0, inp, x)
    wo0 = np.ascontiguousarray(np.asarray(inp["L0_attn_Wo"]).T.astype(np.float32))
    wo1 = np.ascontiguousarray(np.asarray(inp["L1_attn_Wo"]).astype(np.float32))
    maps = []
    for c in range(W):
        yidx = np.zeros((128, 64), dtype=np.int32)
        for r in range(W):
            for tt in range(8):
                yidx[:, r * 8 + tt] = r * T + c * TOK + tt * 128 + np.arange(128)
        m = {"x_own": np.ascontiguousarray(x[c * TOK:(c + 1) * TOK].T)}
        for k, v in a0[c].items():
            if k != "x_own":
                m["a0_" + k] = v
        for k, v in a1[c].items():
            if k != "x_own":
                m["a1_" + k] = v
        for k, v in dc[c].items():
            m["dc_" + k] = v
        for k, v in f0[c].items():
            if k != "x_own":
                m["f0_" + k] = v
        m["o0_wo"] = wo0
        m["o1_wo"] = wo1
        m["o0_yidx"] = yidx
        m["o1_yidx"] = yidx
        maps.append(m)
    return maps


def kernel_fused(**inputs):
    nc = build_fused()
    maps = host_fused_inputs(inputs)
    res = run_bass_kernel_spmd(nc, maps, core_ids=list(range(W)))
    x4 = _gather_xn(res.results)
    return np.ascontiguousarray(x4.reshape(B, TB, D).astype(np.float32))


def _run(nc, maps):
    res = run_bass_kernel_spmd(nc, maps, core_ids=list(range(W)))
    return res.results


def _gather_xn(res):
    return np.concatenate([np.asarray(res[c]["xn"]).T for c in range(W)], axis=0)


def kernel_multi(**inputs):
    inp = inputs
    x0 = np.asarray(inp["L0_attn_x"], dtype=np.float32)

    res = _run(build_attn(0), host_attn_inputs(0, {**inp, "L0_attn_x": x0}))
    ys = [np.asarray(res[c]["y"]) for c in range(W)]
    res = _run(build_o_ln(), host_oln_inputs(0, inp, x0, ys))
    x1 = _gather_xn(res)

    res = _run(build_ffn(0), host_ffn_inputs(0, inp, x1))
    x2 = _gather_xn(res)

    res = _run(build_attn(1), host_attn_inputs(1, {**inp, "L0_attn_x": x2}))
    ys = [np.asarray(res[c]["y"]) for c in range(W)]
    res = _run(build_o_ln(), host_oln_inputs(1, inp, x2, ys))
    x3 = _gather_xn(res)

    res = _run(build_decode(), host_decode_inputs(inp))
    decoded = [{k: np.asarray(res[c][k]) for k in ("gdec", "udec", "ddec")}
               for c in range(W)]
    res = _run(build_ffn(1), host_ffn_inputs(1, inp, x3, decoded))
    x4 = _gather_xn(res)

    return np.ascontiguousarray(x4.reshape(B, TB, D).astype(np.float32))


def kernel(**inputs):
    return kernel_fused(**inputs)


# version 09
# speedup vs baseline: 1.5535x; 1.0056x over previous
import math
import contextlib
from types import SimpleNamespace
import numpy as np
import ml_dtypes
import concourse.bass as bass
import concourse.mybir as mybir
from concourse.bass_core_rust import IndirectOffsetOnAxis
from concourse.bass_utils import run_bass_kernel_spmd

F32 = mybir.dt.float32
BF16 = mybir.dt.bfloat16
I32 = mybir.dt.int32
AF = mybir.ActivationFunctionType
ALU = mybir.AluOpType
AX = mybir.AxisListType

W = 8
B = 2
TB = 4096
T = B * TB
D = 2048
TOK = T // W
NEG = -1.0e30
bfdt = ml_dtypes.bfloat16


class Sem:
    def __init__(self, es, nc, name):
        self.h = es.enter_context(nc.semaphore(name))
        self.n = 0

    def inc(self, inst, k):
        inst.then_inc(self.h, k)
        self.n += k
        return self.n


def vw(e, inst):
    e.ev.inc(inst, 1)
    e.nc.vector.wait_ge(e.ev.h, e.ev.n)


_NPFX = [""]


def sbufT(es, nc, name, shape, dtype):
    return es.enter_context(nc.sbuf_tensor(_NPFX[0] + name, shape, dtype))


def psumT(es, nc, name, shape, dtype=F32):
    return es.enter_context(nc.psum_tensor(_NPFX[0] + name, shape, dtype))


def build_masks(layer):
    p = np.arange(128)[:, None]
    f = np.arange(512)[None, :]
    tiles, idx = [], {}

    def add(key, allow):
        idx[key] = len(tiles)
        allow = np.broadcast_to(allow, (128, 512))
        tiles.append(np.where(allow, np.float32(0.0), np.float32(NEG)))

    for o in (0, 128, 256, 384):
        add(("c", o), f <= p + o)
    if layer == 0:
        for o in (1, 129, 257, 385):
            add(("w", o), f >= p + o)
            add(("ws", o), (f >= p + o) | (f < 4))
            add(("wp", o - 512), f >= p + o - 512)
            add(("wsp", o - 512), (f >= p + o - 512) | (f < 4))
        add(("s",), f < 4)
    return np.stack(tiles).astype(np.float32), idx


def chunk_plan(layer, i0b):
    qc = i0b // 512
    if layer == 1:
        return [(jc, None) for jc in range(qc)] + [(qc, ("c", i0b % 512))]
    wlo = i0b - 1023
    wbc = max(0, wlo // 512)
    plan = []
    if wbc > 0:
        plan.append((0, ("s",)))
    for jc in range(wbc, qc + 1):
        if jc == qc:
            plan.append((jc, ("c", i0b % 512)))
            continue
        o = wlo - 512 * jc
        if o >= 1:
            plan.append((jc, ("ws" if jc == 0 else "w", o)))
        elif o > -511:
            plan.append((jc, ("wsp" if jc == 0 else "wp", o)))
        else:
            plan.append((jc, None))
    return plan


def build_attn(layer, debug_qkv=False, debug_attn=False, fz=None):
    NH = 2
    KVD = 128 if layer == 0 else 256
    KT = KVD // 128
    scale = 1.0 / math.sqrt(128 if layer == 0 else 64)
    masks_np, midx = build_masks(layer)
    NM = masks_np.shape[0]
    MAXC = 4 if layer == 0 else 8
    SLAB = max(MAXC, 8) * 512

    pfx = fz["pfx"] if fz else ""
    _NPFX[0] = pfx
    nc = fz["nc"] if fz else bass.Bass("TRN2", target_bir_lowering=False, num_devices=W)
    e = SimpleNamespace()
    e.nc, e.layer, e.scale, e.midx, e.MAXC, e.KT, e.NH = nc, layer, scale, midx, MAXC, KT, NH

    x_own = fz["x_ap"] if fz else nc.dram_tensor("x_own", [D, TOK], F32, kind="ExternalInput").ap()
    wq_h = nc.dram_tensor(pfx + "wq", [D, 256], F32, kind="ExternalInput").ap()
    wk_h = nc.dram_tensor(pfx + "wk", [D, KVD], F32, kind="ExternalInput").ap()
    wv_h = nc.dram_tensor(pfx + "wv", [D, KVD], F32, kind="ExternalInput").ap()
    masks_h = nc.dram_tensor(pfx + "masks", [NM * 128, 512], F32, kind="ExternalInput").ap()
    if layer == 1:
        lam_h = nc.dram_tensor(pfx + "lam", [1, 2], F32, kind="ExternalInput").ap()
    if fz:
        e.y_h = None
        e.ytm_ap = fz["ytm"]
    else:
        e.ytm_ap = None
        e.y_h = nc.dram_tensor("y", [NH * 128, T], BF16, kind="ExternalOutput").ap()
    if debug_qkv:
        xgdb = nc.dram_tensor("xgdb", [W * D, TOK], BF16, kind="ExternalOutput").ap()
        agdb = nc.dram_tensor("agdb", [D, TOK], BF16, kind="ExternalOutput").ap()
        agdb2 = nc.dram_tensor("agdb2", [D, TOK], BF16, kind="ExternalOutput").ap()
        qdb = nc.dram_tensor("qdb", [256, T], BF16, kind="ExternalOutput").ap()
        kdb = nc.dram_tensor("kdb", [KVD, T], BF16, kind="ExternalOutput").ap()
        vdb = nc.dram_tensor("vdb", [T, KVD], BF16, kind="ExternalOutput").ap()

    if debug_attn:
        e.dbg_bd = nc.dram_tensor("dbg_bd", [128, 64], F32, kind="ExternalOutput").ap()
        e.dbg_nrm = nc.dram_tensor("dbg_nrm", [128, 64], F32, kind="ExternalOutput").ap()
        e.dbg_kmb = nc.dram_tensor("dbg_kmb", [128, 4], F32, kind="ExternalOutput").ap()
        e.dbg_p1 = nc.dram_tensor("dbg_p1", [128, 4096], BF16, kind="ExternalOutput").ap()
        e.dbg_acc = nc.dram_tensor("dbg_acc", [128, 16], F32, kind="ExternalOutput").ap()
        e.dbg_sml = nc.dram_tensor("dbg_sml", [128, 8], F32, kind="ExternalOutput").ap()
    e.debug_attn = debug_attn
    if fz:
        agi, xg = fz["agi"], fz["xg"]
    else:
        agi = nc.dram_tensor("agi", [D, TOK], BF16, kind="Internal").ap()
        xg = nc.dram_tensor("xg", [W * D, TOK], BF16, kind="Internal").ap()

    ones_i = nc.inline_tensor(np.ones((128, 128), dtype=bfdt), name=pfx + "ones_c").ap()
    ident_i = nc.inline_tensor(np.eye(128, dtype=bfdt), name=pfx + "ident_c").ap()

    es = contextlib.ExitStack()
    with es:
        wq_sb = sbufT(es, nc, "wq_sb", [128, 16 * 256], BF16)
        wk_sb = sbufT(es, nc, "wk_sb", [128, 16 * KVD], BF16)
        wv_sb = sbufT(es, nc, "wv_sb", [128, 16 * KVD], BF16)
        xk_sb = sbufT(es, nc, "xk_sb", [128, 16 * 512], BF16)
        e.q_sb = sbufT(es, nc, "q_sb", [128, NH * T], BF16)
        e.k_sb = sbufT(es, nc, "k_sb", [128, KT * T], BF16)
        e.v_sb = sbufT(es, nc, "v_sb", [128, (T // 128) * KVD], BF16)
        e.y_sb = sbufT(es, nc, "y_sb", [128, TB], BF16)
        e.p1f = sbufT(es, nc, "p1f", [128, MAXC * 512], F32)
        if layer == 1:
            e.p2f = sbufT(es, nc, "p2f", [128, MAXC * 512], F32)
        e.pn_sb = sbufT(es, nc, "pn_sb", [128, 4096], BF16)
        e.p1_sb = e.pn_sb
        e.pt_sb = sbufT(es, nc, "pt_sb", [128, 2 * 512], BF16)
        e.mask_sb = sbufT(es, nc, "mask_sb", [128, NM * 512], F32)
        e.ones_sb = sbufT(es, nc, "ones_sb", [128, 128], BF16)
        e.id_sb = sbufT(es, nc, "id_sb", [128, 128], BF16)
        e.bd_sb = sbufT(es, nc, "bd_sb", [128, 64], F32)
        e.nrm_sb = sbufT(es, nc, "nrm_sb", [128, 256], F32)
        e.acc_sb = sbufT(es, nc, "acc_sb", [128, 2 * MAXC], F32)
        e.sml_sb = sbufT(es, nc, "sml_sb", [128, 8], F32)
        e.km_sb = sbufT(es, nc, "km_sb", [1, 8], F32)
        e.kmb16 = sbufT(es, nc, "kmb16", [1, 8], BF16)
        e.kmb_sb = sbufT(es, nc, "kmb_sb", [128, 4], F32)
        e.o32_sb = sbufT(es, nc, "o32_sb", [128, 128], F32)
        e.lamb_sb = sbufT(es, nc, "lamb_sb", [128, 2], F32)
        e.fl_sb = sbufT(es, nc, "fl_sb", [128, 2], F32)
        e.ytp_sb = sbufT(es, nc, "ytp_sb", [128, 256], BF16)

        e.ps_a = psumT(es, nc, "ps_a", [128, 512], F32)
        e.ps_b = psumT(es, nc, "ps_b", [128, 512], F32)
        e.ps_c = psumT(es, nc, "ps_c", [128, 512], F32)
        e.ps_d = psumT(es, nc, "ps_d", [128, 512], F32)
        e.ps_p0 = psumT(es, nc, "ps_p0", [128, 512], BF16)
        e.ps_p1 = psumT(es, nc, "ps_p1", [128, 512], BF16)
        e.ps_y = psumT(es, nc, "ps_y", [128, 128], F32)
        e.ps_n = psumT(es, nc, "ps_n", [128, 64], F32)

        if fz:
            sm = fz["sems"]
            ld, cv, ag, xt = sm["ld"], sm["cv"], sm["ag"], sm["xt"]
            e.mm = mm = sm["mm"]; e.ev = ev = sm["ev"]
            e.ex = ex = sm["ex"]; e.so = so = sm["so"]
        else:
            ld = Sem(es, nc, "ld")
            cv = Sem(es, nc, "cv")
            ag = Sem(es, nc, "ag")
            xt = Sem(es, nc, "xt")
            e.mm = mm = Sem(es, nc, "mm")
            e.ev = ev = Sem(es, nc, "ev")
            e.ex = ex = Sem(es, nc, "ex")
            e.so = so = Sem(es, nc, "so")

        sy, gg, ve, sc, te = nc.sync, nc.gpsimd, nc.vector, nc.scalar, nc.tensor

        ld.inc(gg.dma_start(out=wq_sb[:].rearrange("p (c o) -> p c o", c=16), in_=wq_h.rearrange("(c p) o -> p c o", p=128)), 16)
        ld.inc(gg.dma_start(out=wk_sb[:].rearrange("p (c o) -> p c o", c=16), in_=wk_h.rearrange("(c p) o -> p c o", p=128)), 16)
        ld.inc(gg.dma_start(out=wv_sb[:].rearrange("p (c o) -> p c o", c=16), in_=wv_h.rearrange("(c p) o -> p c o", p=128)), 16)
        ld.inc(sy.dma_start(out=e.mask_sb[:].rearrange("p (m f) -> p m f", f=512),
                            in_=masks_h.rearrange("(m p) f -> p m f", p=128)), 16)
        ld.inc(sy.dma_start(out=e.ones_sb[:], in_=ones_i[:, :]), 16)
        ld.inc(sy.dma_start(out=e.id_sb[:], in_=ident_i[:, :]), 16)
        if layer == 1:
            ld.inc(sy.dma_start(out=e.km_sb[0:1, 0:2], in_=lam_h[:, :]), 16)
        cv.inc(gg.dma_start(out=agi[:, :], in_=x_own[:, :]), 16)
        if debug_qkv:
            sy.wait_ge(cv.h, cv.n)
            ag.inc(sy.dma_start(out=agdb[:, :], in_=agi[:, :]), 16)
            gg.wait_ge(ag.h, ag.n)
        gg.wait_ge(cv.h, cv.n)
        ag.inc(gg.collective_compute("AllGather", ALU.bypass,
                                     replica_groups=[list(range(W))],
                                     ins=[agi[:, :]], outs=[xg[:, :]]), 1)
        sy.wait_ge(ag.h, ag.n)

        def xg_win(tt):
            r, cs = tt // 2, 512 * (tt % 2)
            return (xg[r * D:(r + 1) * D, cs:cs + 512]
                    .rearrange("(c p) t -> p c t", p=128))

        pj_ev = [0, 0, 0, 0]
        for tt in range(16):
            if tt > 0:
                sy.wait_ge(ev.h, ev.n)
            xt.inc(sy.dma_start(out=xk_sb[:].rearrange("p (c t) -> p c t", c=16), in_=xg_win(tt)), 16)
            te.wait_ge(xt.h, xt.n)
            rot = 0
            for dst_sb, wsb, no in ((e.q_sb, wq_sb, 2), (e.k_sb, wk_sb, KT)):
                for oh in range(no):
                    ps = (e.ps_a, e.ps_b)[rot % 2]
                    te.wait_ge(ev.h, pj_ev[rot % 2])
                    rot += 1
                    for kc in range(16):
                        inst = te.matmul(ps[:, :],
                                         wsb[:, kc * (no * 128) + oh * 128:
                                             kc * (no * 128) + oh * 128 + 128],
                                         xk_sb[:, kc * 512:(kc + 1) * 512],
                                         start=(kc == 0), stop=(kc == 15))
                    mm.inc(inst, 1)
                    ve.wait_ge(mm.h, mm.n)
                    ev.inc(ve.tensor_copy(dst_sb[:, oh * T + tt * 512:
                                                 oh * T + tt * 512 + 512], ps[:, :]), 1)
                    pj_ev[(rot - 1) % 2] = ev.n
            for ts in range(4):
                ps = (e.ps_c, e.ps_d)[ts % 2]
                te.wait_ge(ev.h, pj_ev[2 + ts % 2])
                for kc in range(16):
                    inst = te.matmul(ps[:, 0:KVD],
                                     xk_sb[:, kc * 512 + ts * 128:
                                           kc * 512 + ts * 128 + 128],
                                     wv_sb[:, kc * KVD:(kc + 1) * KVD],
                                     start=(kc == 0), stop=(kc == 15))
                mm.inc(inst, 1)
                ve.wait_ge(mm.h, mm.n)
                tg = tt * 4 + ts
                ev.inc(ve.tensor_copy(e.v_sb[:, tg * KVD:(tg + 1) * KVD],
                                      ps[:, 0:KVD]), 1)
                pj_ev[2 + ts % 2] = ev.n

        if debug_qkv:
            ev.inc(ve.tensor_copy(e.fl_sb[:, 0:1], e.fl_sb[:, 1:2]), 1)
            gg.wait_ge(ev.h, ev.n)
            so.inc(gg.dma_start(out=xgdb[:, :], in_=xg[:, :]), 16)
            so.inc(gg.dma_start(out=agdb2[:, :], in_=agi[:, :]), 16)
            so.inc(gg.dma_start(out=qdb.rearrange("(h p) t -> p h t", p=128),
                                in_=e.q_sb[:, :].rearrange("p (h t) -> p h t", t=T)), 16)
            so.inc(gg.dma_start(out=kdb.rearrange("(h p) t -> p h t", p=128),
                                in_=e.k_sb[:, :].rearrange("p (h t) -> p h t", t=T)), 16)
            so.inc(gg.dma_start(out=vdb.rearrange("(g p) o -> p g o", p=128),
                                in_=e.v_sb[:, :].rearrange("p (g o) -> p g o", g=T // 128)), 16)
            gg.wait_ge(so.h, so.n)
            return nc

        if layer == 1:
            sc.wait_ge(ld.h, ld.n)
            ex.inc(sc.activation(e.km_sb[0:1, 2:4], e.km_sb[0:1, 0:2], AF.Sigmoid), 1)
            ve.wait_ge(ex.h, ex.n)
            vw(e, ve.memset(e.o32_sb[0:1, :], 1.0))
            te.wait_ge(ev.h, ev.n)
            mm.inc(te.matmul(e.ps_n[:, 0:2], e.o32_sb[0:1, :], e.km_sb[0:1, 2:4],
                             start=True, stop=True), 1)
            ve.wait_ge(mm.h, mm.n)
            vw(e, ve.tensor_copy(e.lamb_sb[:, 0:2], e.ps_n[:, 0:2]))

        for b in range(B):
            for hl in range(NH):
                emit_attn_bh(e, b, hl)

        gg.wait_ge(so.h, so.n)
    return nc


def emit_attn_bh(e, b, hl):
    nc, layer = e.nc, e.layer
    sy, gg, ve, sc, te = nc.sync, nc.gpsimd, nc.vector, nc.scalar, nc.tensor
    mm, ev, ex, so = e.mm, e.ev, e.ex, e.so
    qcol = hl * T + b * TB
    kcol = (hl * T + b * TB) if layer == 1 else b * TB
    halves = ((0,),) if layer == 0 else ((0,), (64,))
    nsets = len(halves)
    MAXC, scale = e.MAXC, e.scale

    sc.wait_ge(mm.h, mm.n)
    ve.wait_ge(so.h, so.n)
    sc.wait_ge(ev.h, ev.n)
    ex.inc(sc.activation(e.p1_sb[:, 0:TB], e.q_sb[:, qcol:qcol + TB], AF.Square), 1)
    te.wait_ge(ex.h, ex.n)
    for si, (base,) in enumerate(halves):
        kw = {} if base == 0 else {"tile_position": (64, 0)}
        npart = 128 // nsets
        for qt in range(32):
            inst = te.matmul(e.ps_n[:, si * 32 + qt:si * 32 + qt + 1],
                             e.p1_sb[base:base + npart, qt * 128:(qt + 1) * 128],
                             e.ones_sb[base:base + npart, 0:1],
                             start=True, stop=True, **kw)
        mm.inc(inst, 1)
    ve.wait_ge(mm.h, mm.n)
    vw(e, ve.tensor_copy(e.nrm_sb[:, 0:64], e.ps_n[:, 0:64]))
    sc.wait_ge(ev.h, ev.n)
    ex.inc(sc.activation(e.p1_sb[:, 0:TB], e.k_sb[:, kcol:kcol + TB], AF.Square), 1)
    te.wait_ge(ex.h, ex.n)
    te.wait_ge(ev.h, ev.n)
    for si, (base,) in enumerate(halves):
        kw = {} if base == 0 else {"tile_position": (64, 0)}
        npart = 128 // nsets
        for jt in range(32):
            inst = te.matmul(e.ps_n[:, si * 32 + jt:si * 32 + jt + 1],
                             e.p1_sb[base:base + npart, jt * 128:(jt + 1) * 128],
                             e.ones_sb[base:base + npart, 0:1],
                             start=True, stop=True, **kw)
        mm.inc(inst, 1)
    ve.wait_ge(mm.h, mm.n)
    vw(e, ve.tensor_copy(e.nrm_sb[:, 64:128], e.ps_n[:, 0:64]))
    for si in range(nsets):
        vw(e, ve.reduce_max(e.sml_sb[:, si:si + 1],
                            e.nrm_sb[:, 64 + si * 32:64 + si * 32 + 32], axis=AX.X))
    vw(e, ve.memset(e.p1_sb[:, 0:128], NEG))
    for si in range(nsets):
        vw(e, ve.tensor_copy(e.p1_sb[:, si * 64:si * 64 + 1],
                             e.sml_sb[:, si:si + 1]))
    te.wait_ge(ev.h, ev.n)
    mm.inc(te.transpose(e.ps_p0[:, 0:128], e.p1_sb[:, 0:128], e.id_sb[:, :]), 1)
    ve.wait_ge(mm.h, mm.n)
    for si in range(nsets):
        vw(e, ve.tensor_copy(e.nrm_sb[si * 64:si * 64 + 1, 128:256],
                             e.ps_p0[si * 64:si * 64 + 1, 0:128]))
        vw(e, ve.reduce_max(e.km_sb[0:1, 4 + si:5 + si],
                            e.nrm_sb[si * 64:si * 64 + 1, 128:256], axis=AX.X))
    sc.wait_ge(ev.h, ev.n)
    ex.inc(sc.activation(e.km_sb[0:1, 6:6 + nsets], e.km_sb[0:1, 4:4 + nsets], AF.Sqrt), 1)
    ve.wait_ge(ex.h, ex.n)
    vw(e, ve.tensor_copy(e.kmb16[0:1, 6:6 + nsets], e.km_sb[0:1, 6:6 + nsets]))
    te.wait_ge(ev.h, ev.n)
    mm.inc(te.matmul(e.ps_n[:, 0:nsets], e.ones_sb[0:1, :], e.kmb16[0:1, 6:6 + nsets],
                     start=True, stop=True), 1)
    ve.wait_ge(mm.h, mm.n)
    vw(e, ve.tensor_copy(e.kmb_sb[:, 0:nsets], e.ps_n[:, 0:nsets]))
    sc.wait_ge(ev.h, ev.n)
    ex.inc(sc.activation(e.nrm_sb[:, 0:64], e.nrm_sb[:, 0:64], AF.Sqrt), 1)
    ve.wait_ge(ex.h, ex.n)
    for si in range(nsets):
        vw(e, ve.tensor_scalar(e.bd_sb[:, si * 32:si * 32 + 32],
                               e.nrm_sb[:, si * 32:si * 32 + 32],
                               e.kmb_sb[:, si:si + 1], -scale * 1.01,
                               ALU.mult, ALU.mult))

    if e.debug_attn and b == 0 and hl == 0:
        ev.inc(ve.tensor_copy(e.fl_sb[:, 0:1], e.fl_sb[:, 1:2]), 1)
        gg.wait_ge(ev.h, ev.n)
        so.inc(gg.dma_start(out=e.dbg_bd[:, :], in_=e.bd_sb[:, :]), 16)
        so.inc(gg.dma_start(out=e.dbg_nrm[:, :], in_=e.nrm_sb[:, 0:64]), 16)
        so.inc(gg.dma_start(out=e.dbg_kmb[:, :], in_=e.kmb_sb[:, :]), 16)
    rot_ex = [0, 0]
    ptc_ev = [0, 0]
    pts_mm = [0, 0]
    yso = [0, 0]
    pn_mm = [0]
    for qt in range(32):
        i0b = qt * 128
        plan = chunk_plan(layer, i0b)
        nj = len(plan)
        for ci, (jc, mk) in enumerate(plan):
            s1p = (e.ps_a, e.ps_b)[ci % 2]
            s2p = (e.ps_c, e.ps_d)[ci % 2]
            pss = (s1p,) if layer == 0 else (s1p, s2p)
            te.wait_ge(ex.h, rot_ex[ci % 2])
            for si, (base,) in enumerate(halves):
                kw = {} if base == 0 else {"tile_position": (64, 0)}
                npart = 128 // nsets
                inst = te.matmul(pss[si][:, :],
                                 e.q_sb[base:base + npart, qcol + i0b:qcol + i0b + 128],
                                 e.k_sb[base:base + npart,
                                        kcol + jc * 512:kcol + jc * 512 + 512],
                                 start=True, stop=True, **kw)
                mm.inc(inst, 1)
            if mk is not None:
                mi = e.midx[mk]
                ve.wait_ge(mm.h, mm.n)
                for ps in pss:
                    ev.inc(ve.tensor_tensor(ps[:, :], ps[:, :],
                                            e.mask_sb[:, mi * 512:(mi + 1) * 512],
                                            ALU.add), 1)
            sc.wait_ge(mm.h, mm.n)
            sc.wait_ge(ev.h, ev.n)
            for si, ps in enumerate(pss):
                slab = (e.p1f, e.p2f if layer == 1 else None)[si]
                ex.inc(sc.activation(slab[:, ci * 512:(ci + 1) * 512], ps[:, :],
                                     AF.Exp,
                                     bias=e.bd_sb[:, si * 32 + qt:si * 32 + qt + 1],
                                     scale=scale,
                                     accum_out=e.acc_sb[:, si * MAXC + ci:
                                                        si * MAXC + ci + 1]), 1)
            rot_ex[ci % 2] = ex.n
        ve.wait_ge(ex.h, ex.n)
        ve.wait_ge(mm.h, pn_mm[0])
        vw(e, ve.reduce_sum(e.sml_sb[:, 0:1], e.acc_sb[:, 0:nj], axis=AX.X))
        vw(e, ve.reciprocal(e.sml_sb[:, 1:2], e.sml_sb[:, 0:1]))
        if layer == 0:
            ev.inc(ve.tensor_scalar(e.pn_sb[:, 0:nj * 512], e.p1f[:, 0:nj * 512],
                                    e.sml_sb[:, 1:2], None, ALU.mult), 1)
        else:
            vw(e, ve.reduce_sum(e.sml_sb[:, 2:3], e.acc_sb[:, MAXC:MAXC + nj],
                                axis=AX.X))
            vw(e, ve.reciprocal(e.sml_sb[:, 3:4], e.sml_sb[:, 2:3]))
            vw(e, ve.tensor_tensor(e.sml_sb[:, 4:5], e.sml_sb[:, 0:1],
                                   e.sml_sb[:, 3:4], ALU.mult))
            vw(e, ve.tensor_scalar(e.sml_sb[:, 4:5], e.sml_sb[:, 4:5],
                                   e.lamb_sb[:, hl:hl + 1], -1.0,
                                   ALU.mult, ALU.mult))
            ev.inc(ve.scalar_tensor_tensor(e.p2f[:, 0:nj * 512],
                                           e.p2f[:, 0:nj * 512],
                                           e.sml_sb[:, 4:5],
                                           e.p1f[:, 0:nj * 512],
                                           ALU.mult, ALU.add), 1)
            ve.wait_ge(ev.h, ev.n)
            ev.inc(ve.tensor_scalar(e.pn_sb[:, 0:nj * 512], e.p2f[:, 0:nj * 512],
                                    e.sml_sb[:, 1:2], None, ALU.mult), 1)
        if e.debug_attn and b == 0 and hl == 0 and qt == 2:
            ev.inc(ve.tensor_copy(e.fl_sb[:, 0:1], e.fl_sb[:, 1:2]), 1)
            gg.wait_ge(ev.h, ev.n)
            so.inc(gg.dma_start(out=e.dbg_p1[:, 0:nj * 512], in_=e.p1_sb[:, 0:nj * 512]), 16)
            so.inc(gg.dma_start(out=e.dbg_acc[:, 0:2 * e.MAXC], in_=e.acc_sb[:, :]), 16)
            so.inc(gg.dma_start(out=e.dbg_sml[:, :], in_=e.sml_sb[:, :]), 16)
            gg.wait_ge(so.h, so.n)
        first = True
        tr_mm = [0] * nj
        norm_ev = ev.n

        def emit_tr(cj):
            r1 = cj % 2
            pt_ps1 = (e.ps_p0, e.ps_p1)[r1]
            te.wait_ge(ev.h, max(ptc_ev[r1], norm_ev))
            for jt in range(4):
                inst = te.transpose(pt_ps1[:, jt * 128:(jt + 1) * 128],
                                    e.pn_sb[:, cj * 512 + jt * 128:
                                            cj * 512 + jt * 128 + 128],
                                    e.id_sb[:, :])
            e.mm.inc(inst, 1)
            tr_mm[cj] = e.mm.n

        emit_tr(0)
        for ci, (jc, mk) in enumerate(plan):
            r = ci % 2
            pt_ps = (e.ps_p0, e.ps_p1)[r]
            pt_slot = e.pt_sb[:, r * 512:r * 512 + 512]
            ve.wait_ge(mm.h, max(tr_mm[ci], pts_mm[r]))
            ev.inc(ve.tensor_copy(pt_slot, pt_ps[:, :]), 1)
            ptc_ev[r] = ev.n
            cpc = ev.n
            if ci + 1 < nj:
                emit_tr(ci + 1)
            te.wait_ge(ev.h, cpc)
            hofs = hl * 128 if layer == 1 else 0
            for jt in range(4):
                tg = b * 32 + jc * 4 + jt
                inst = te.matmul(e.ps_y[:, :],
                                 e.v_sb[:, tg * (e.KT * 128) + hofs:
                                        tg * (e.KT * 128) + hofs + 128],
                                 pt_slot[:, jt * 128:(jt + 1) * 128],
                                 start=first, stop=(ci == nj - 1 and jt == 3))
                first = False
            mm.inc(inst, 1)
            pts_mm[r] = mm.n
            pn_mm[0] = max(pn_mm[0], tr_mm[ci])
        ve.wait_ge(mm.h, mm.n)
        ev.inc(ve.tensor_copy(e.y_sb[:, i0b:i0b + 128], e.ps_y[:, :]), 1)
    if e.ytm_ap is None:
        ev.inc(ve.tensor_copy(e.fl_sb[:, 0:1], e.fl_sb[:, 1:2]), 1)
        gg.wait_ge(ev.h, ev.n)
        so.inc(gg.dma_start(out=e.y_h[hl * 128:(hl + 1) * 128, b * TB:(b + 1) * TB],
                            in_=e.y_sb[:, :]), 16)
    else:
        te.wait_ge(ev.h, ev.n)
        for tt in range(32):
            r = tt % 2
            pt_ps = (e.ps_p0, e.ps_p1)[r]
            slot = r * 128
            te.wait_ge(ev.h, ptc_ev[r])
            mm.inc(te.transpose(pt_ps[:, 0:128], e.y_sb[:, tt * 128:(tt + 1) * 128],
                                e.id_sb[:, :]), 1)
            ve.wait_ge(mm.h, mm.n)
            ve.wait_ge(so.h, yso[r])
            vw(e, ve.tensor_copy(e.ytp_sb[:, slot:slot + 128], pt_ps[:, 0:128]))
            ptc_ev[r] = ev.n
            gg.wait_ge(ev.h, ev.n)
            so.inc(gg.dma_start(
                out=e.ytm_ap[b * TB + tt * 128:b * TB + (tt + 1) * 128,
                             hl * 128:(hl + 1) * 128],
                in_=e.ytp_sb[:, slot:slot + 128]), 16)
            yso[r] = so.n


def host_attn_inputs(layer, inp):
    x = np.asarray(inp["L0_attn_x"], dtype=np.float32).reshape(T, D)
    masks_np, _ = build_masks(layer)
    masks2 = np.ascontiguousarray(masks_np.reshape(-1, 512))
    maps = []
    for c in range(W):
        m = {"masks": masks2,
             "x_own": np.ascontiguousarray(x[c * TOK:(c + 1) * TOK].T)}
        if layer == 0:
            kv = c // 2
            m["wq"] = np.ascontiguousarray(np.asarray(inp["L0_attn_Wq"])[256 * c:256 * (c + 1)].T)
            m["wk"] = np.ascontiguousarray(np.asarray(inp["L0_attn_Wk"])[128 * kv:128 * (kv + 1)].T)
            m["wv"] = np.ascontiguousarray(np.asarray(inp["L0_attn_Wv"])[128 * kv:128 * (kv + 1)].T)
        else:
            m["wq"] = np.ascontiguousarray(np.asarray(inp["L1_attn_Wq"])[:, 256 * c:256 * (c + 1)])
            m["wk"] = np.ascontiguousarray(np.asarray(inp["L1_attn_Wk"])[:, 256 * c:256 * (c + 1)])
            m["wv"] = np.ascontiguousarray(np.asarray(inp["L1_attn_Wv"])[:, 256 * c:256 * (c + 1)])
            m["lam"] = np.ascontiguousarray(
                np.asarray(inp["L1_attn_lambda_init"])[2 * c:2 * c + 2].reshape(1, 2))
        maps.append(m)
    return maps




def emit_ln(e, out_h):
    nc = e.nc
    sy, gg, ve, sc, te = nc.sync, nc.gpsimd, nc.vector, nc.scalar, nc.tensor
    mm, ev, ex, so = e.mm, e.ev, e.ex, e.so
    vw(e, ve.memset(e.o32_sb[:, :], 1.0))
    vw(e, ve.memset(e.st_sb[0:1, 0:2048], 0.0))
    te.wait_ge(ev.h, ev.n)
    sc.wait_ge(ev.h, ev.n)
    sq_mm = [0, 0]
    st_ev = [0, 0]
    for dt in range(16):
        hh = dt % 2
        sc.wait_ge(mm.h, sq_mm[hh])
        ex.inc(sc.activation(e.sq1_sb[:, hh * 1024:hh * 1024 + 1024],
                             e.xr_sb[:, dt * 1024:(dt + 1) * 1024],
                             AF.Square), 1)
        te.wait_ge(ex.h, ex.n)
        te.wait_ge(ev.h, st_ev[hh])
        pset = ((e.ps_c, e.ps_d, e.ps_a, e.ps_b),
                (e.ps_bc, e.ps_bd, e.ps_ba, e.ps_bb))[hh]
        for th in range(2):
            inst = te.matmul(pset[th][0:1, 0:512], e.o32_sb[:, 0:1],
                             e.xr_sb[:, dt * 1024 + th * 512:dt * 1024 + th * 512 + 512],
                             start=True, stop=True)
            inst = te.matmul(pset[2 + th][0:1, 0:512], e.o32_sb[:, 0:1],
                             e.sq1_sb[:, hh * 1024 + th * 512:hh * 1024 + th * 512 + 512],
                             start=True, stop=True)
        mm.inc(inst, 1)
        sq_mm[hh] = mm.n
        ve.wait_ge(mm.h, mm.n)
        vw(e, ve.tensor_tensor(e.st_sb[0:1, 0:512], e.st_sb[0:1, 0:512],
                               pset[0][0:1, 0:512], ALU.add))
        vw(e, ve.tensor_tensor(e.st_sb[0:1, 512:1024], e.st_sb[0:1, 512:1024],
                               pset[1][0:1, 0:512], ALU.add))
        vw(e, ve.tensor_tensor(e.st_sb[0:1, 1024:1536], e.st_sb[0:1, 1024:1536],
                               pset[2][0:1, 0:512], ALU.add))
        vw(e, ve.tensor_tensor(e.st_sb[0:1, 1536:2048], e.st_sb[0:1, 1536:2048],
                               pset[3][0:1, 0:512], ALU.add))
        st_ev[hh] = ev.n
    vw(e, ve.tensor_scalar(e.st_sb[0:1, 0:1024], e.st_sb[0:1, 0:1024],
                           1.0 / D, None, ALU.mult))
    vw(e, ve.tensor_scalar(e.st_sb[0:1, 1024:2048], e.st_sb[0:1, 1024:2048],
                           1.0 / D, None, ALU.mult))
    vw(e, ve.tensor_tensor(e.st_sb[0:1, 2048:3072], e.st_sb[0:1, 0:1024],
                           e.st_sb[0:1, 0:1024], ALU.mult))
    vw(e, ve.tensor_tensor(e.st_sb[0:1, 1024:2048], e.st_sb[0:1, 1024:2048],
                           e.st_sb[0:1, 2048:3072], ALU.subtract))
    vw(e, ve.tensor_scalar(e.st_sb[0:1, 1024:2048], e.st_sb[0:1, 1024:2048],
                           1e-5, None, ALU.add))
    sc.wait_ge(ev.h, ev.n)
    ex.inc(sc.activation(e.st_sb[0:1, 2048:3072], e.st_sb[0:1, 1024:2048],
                         AF.Sqrt), 1)
    ve.wait_ge(ex.h, ex.n)
    vw(e, ve.reciprocal(e.st_sb[0:1, 2048:3072], e.st_sb[0:1, 2048:3072]))
    vw(e, ve.tensor_tensor(e.st_sb[0:1, 3072:4096], e.st_sb[0:1, 0:1024],
                           e.st_sb[0:1, 2048:3072], ALU.mult))
    vw(e, ve.tensor_scalar(e.st_sb[0:1, 3072:4096], e.st_sb[0:1, 3072:4096],
                           -1.0, None, ALU.mult))
    vw(e, ve.memset(e.o32_sb[0:1, :], 1.0))
    te.wait_ge(ev.h, ev.n)
    mm.inc(te.matmul(e.ps_ba[:, 0:512], e.o32_sb[0:1, :], e.st_sb[0:1, 2048:2560],
                     start=True, stop=True), 1)
    mm.inc(te.matmul(e.ps_bb[:, 0:512], e.o32_sb[0:1, :], e.st_sb[0:1, 2560:3072],
                     start=True, stop=True), 1)
    mm.inc(te.matmul(e.ps_bc[:, 0:512], e.o32_sb[0:1, :], e.st_sb[0:1, 3072:3584],
                     start=True, stop=True), 1)
    mm.inc(te.matmul(e.ps_bd[:, 0:512], e.o32_sb[0:1, :], e.st_sb[0:1, 3584:4096],
                     start=True, stop=True), 1)
    ve.wait_ge(mm.h, mm.n)
    xo_so = [0, 0]
    for dt in range(16):
        hh = dt % 2
        ve.wait_ge(so.h, xo_so[hh])
        for th in range(2):
            Ab = (e.ps_ba, e.ps_bb)[th]
            cols = slice(dt * 1024 + th * 512, dt * 1024 + th * 512 + 512)
            xoc = slice(hh * 1024 + th * 512, hh * 1024 + th * 512 + 512)
            ve.tensor_tensor(e.xo_sb[:, xoc], e.xr_sb[:, cols],
                             Ab[:, 0:512], ALU.mult)
        for th in range(2):
            Bb = (e.ps_bc, e.ps_bd)[th]
            xoc = slice(hh * 1024 + th * 512, hh * 1024 + th * 512 + 512)
            ev.inc(ve.tensor_tensor(e.xo_sb[:, xoc], e.xo_sb[:, xoc],
                                    Bb[:, 0:512], ALU.add), 1)
        gg.wait_ge(ev.h, ev.n)
        so.inc(gg.dma_start(out=out_h[dt * 128:(dt + 1) * 128, :],
                            in_=e.xo_sb[:, hh * 1024:hh * 1024 + 1024]), 16)
        xo_so[hh] = so.n


def build_o_ln(fz=None):
    pfx = fz["pfx"] if fz else ""
    _NPFX[0] = pfx
    nc = fz["nc"] if fz else bass.Bass("TRN2", target_bir_lowering=False, num_devices=W)
    e = SimpleNamespace()
    e.nc = nc
    if fz:
        y_h = None
        yidx_h = nc.dram_tensor(pfx + "yidx", [128, 64], I32, kind="ExternalInput").ap()
        x_h = fz["x_ap"]
        xn_h = fz["out_ap"]
    else:
        y_h = nc.dram_tensor("y", [D, TOK], BF16, kind="ExternalInput").ap()
        x_h = nc.dram_tensor("x_own", [D, TOK], F32, kind="ExternalInput").ap()
        xn_h = nc.dram_tensor("xn", [D, TOK], F32, kind="ExternalOutput").ap()
    wo_h = nc.dram_tensor(pfx + "wo", [D, D], F32, kind="ExternalInput").ap()
    ones_i = nc.inline_tensor(np.ones((128, 128), dtype=bfdt), name=pfx + "ones_c").ap()
    ident_i = nc.inline_tensor(np.eye(128, dtype=bfdt), name=pfx + "ident_c").ap()

    es = contextlib.ExitStack()
    with es:
        wo_sb = sbufT(es, nc, "wo_sb", [128, 16 * D], BF16)
        yk_sb = sbufT(es, nc, "yk_sb", [128, 16 * TOK], BF16)
        e.xr_sb = sbufT(es, nc, "xr_sb", [128, 16 * TOK], F32)
        e.xo_sb = sbufT(es, nc, "xo_sb", [128, 2 * TOK], F32)
        e.xb1_sb = sbufT(es, nc, "xb1_sb", [128, TOK], F32)
        e.sq1_sb = sbufT(es, nc, "sq1_sb", [128, 2 * TOK], F32)
        e.st_sb = sbufT(es, nc, "st_sb", [1, 4096], F32)
        e.o32_sb = sbufT(es, nc, "o32_sb", [128, 128], F32)
        e.ones_sb = sbufT(es, nc, "ones_sb", [128, 128], BF16)
        e.fl_sb = sbufT(es, nc, "fl_sb", [128, 2], F32)
        e.ps_a = psumT(es, nc, "ps_a", [128, 512], F32)
        e.ps_b = psumT(es, nc, "ps_b", [128, 512], F32)
        e.ps_c = psumT(es, nc, "ps_c", [128, 512], F32)
        e.ps_d = psumT(es, nc, "ps_d", [128, 512], F32)
        e.ps_ba = psumT(es, nc, "ps_ba", [128, 512], F32)
        e.ps_bb = psumT(es, nc, "ps_bb", [128, 512], F32)
        e.ps_bc = psumT(es, nc, "ps_bc", [128, 512], F32)
        e.ps_bd = psumT(es, nc, "ps_bd", [128, 512], F32)

        if fz:
            sm = fz["sems"]
            ld, ag, xt = sm["ld"], sm["ag"], sm["xt"]
            e.mm = mm = sm["mm"]; e.ev = ev = sm["ev"]
            e.ex = ex = sm["ex"]; e.so = so = sm["so"]
        else:
            ld = Sem(es, nc, "ld")
            e.mm = mm = Sem(es, nc, "mm")
            e.ev = ev = Sem(es, nc, "ev")
            e.ex = ex = Sem(es, nc, "ex")
            e.so = so = Sem(es, nc, "so")
        sy, gg, ve, sc, te = nc.sync, nc.gpsimd, nc.vector, nc.scalar, nc.tensor

        ld.inc(gg.dma_start(out=wo_sb[:].rearrange("p (c o) -> p c o", c=16),
                            in_=wo_h.rearrange("(c p) o -> p c o", p=128)), 16)
        if not fz:
            ld.inc(gg.dma_start(out=yk_sb[:].rearrange("p (c t) -> p c t", c=16),
                                in_=y_h.rearrange("(c p) t -> p c t", p=128)), 16)
        ld.inc(sy.dma_start(out=e.xr_sb[:].rearrange("p (c t) -> p c t", c=16),
                            in_=x_h.rearrange("(c p) t -> p c t", p=128)), 16)
        ld.inc(sy.dma_start(out=e.ones_sb[:], in_=ones_i[:, :]), 16)
        if fz:
            id_sb = sbufT(es, nc, "id_sb", [128, 128], BF16)
            yidx_sb = sbufT(es, nc, "yidx_sb", [128, 64], I32)
            ygb_sb = sbufT(es, nc, "ygb_sb", [128, 256], BF16)
            ps_p0 = e.ps_a[:, 0:64].bitcast(BF16)
            ps_p1 = e.ps_b[:, 0:64].bitcast(BF16)
            ld.inc(sy.dma_start(out=id_sb[:], in_=ident_i[:, :]), 16)
            ld.inc(sy.dma_start(out=yidx_sb[:], in_=yidx_h[:, :]), 16)
            gg.wait_ge(e.so.h, e.so.n)
            ag.inc(gg.collective_compute("AllGather", ALU.bypass,
                                         replica_groups=[list(range(W))],
                                         ins=[fz["ytm"][:, :]],
                                         outs=[fz["ygath"][:, :]]), 1)
            gg.wait_ge(ag.h, ag.n)
            gg.wait_ge(ld.h, ld.n)
            for r in range(W):
                for tt in range(8):
                    col = r * 8 + tt
                    gg.wait_ge(ev.h, ev.n)
                    xt.inc(gg.indirect_dma_start(
                        ygb_sb[:, :], None, fz["ygath"][:, :],
                        IndirectOffsetOnAxis(yidx_sb[:, col:col + 1], 0)), 16)
                    te.wait_ge(xt.h, xt.n)
                    te.wait_ge(ev.h, ev.n)
                    for half in range(2):
                        pt = (ps_p0, ps_p1)[half]
                        mm.inc(te.transpose(pt, ygb_sb[:, half * 128:(half + 1) * 128],
                                            id_sb[:, :]), 1)
                    ve.wait_ge(mm.h, mm.n)
                    for half in range(2):
                        pt = (ps_p0, ps_p1)[half]
                        kc = 2 * r + half
                        ev.inc(ve.tensor_copy(
                            yk_sb[:, kc * TOK + tt * 128:kc * TOK + tt * 128 + 128],
                            pt), 1)

        te.wait_ge(ld.h, ld.n)
        te.wait_ge(ev.h, ev.n)
        for dt in range(16):
            for th in range(2):
                ps = (e.ps_a, e.ps_b)[th]
                te.wait_ge(ev.h, ev.n)
                for kc in range(16):
                    inst = te.matmul(ps[:, :],
                                     wo_sb[:, kc * D + dt * 128:kc * D + dt * 128 + 128],
                                     yk_sb[:, kc * TOK + th * 512:kc * TOK + th * 512 + 512],
                                     start=(kc == 0), stop=(kc == 15))
                mm.inc(inst, 1)
                ve.wait_ge(mm.h, mm.n)
                cols = slice(dt * 1024 + th * 512, dt * 1024 + th * 512 + 512)
                ev.inc(ve.tensor_tensor(e.xr_sb[:, cols], e.xr_sb[:, cols],
                                        ps[:, :], ALU.add), 1)
        emit_ln(e, xn_h)
        gg.wait_ge(so.h, so.n)
    return nc


def build_ffn(layer, debug_ffn=False, fz=None):
    NS = 1024 if layer == 0 else 2048
    NSUB = NS // 1024
    pfx = fz["pfx"] if fz else ""
    _NPFX[0] = pfx
    nc = fz["nc"] if fz else bass.Bass("TRN2", target_bir_lowering=False, num_devices=W)
    e = SimpleNamespace()
    e.nc = nc
    x_h = fz["x_ap"] if fz else nc.dram_tensor("x_own", [D, TOK], F32, kind="ExternalInput").ap()
    if layer == 0:
        wg_h = nc.dram_tensor(pfx + "wg", [D, NS], I32, kind="ExternalInput").ap()
        wu_h = nc.dram_tensor(pfx + "wu", [D, NS], I32, kind="ExternalInput").ap()
        wd_h = nc.dram_tensor(pfx + "wd", [NS, D], I32, kind="ExternalInput").ap()
        gsc_h = nc.dram_tensor(pfx + "gsc", [128, 8], F32, kind="ExternalInput").ap()
        usc_h = nc.dram_tensor(pfx + "usc", [128, 8], F32, kind="ExternalInput").ap()
        dsc_h = nc.dram_tensor(pfx + "dsc", [128, 16], F32, kind="ExternalInput").ap()
    elif fz:
        wg_h, wu_h, wd_h = fz["wg_ap"], fz["wu_ap"], fz["wd_ap"]
    else:
        wg_h = nc.dram_tensor("wg", [D, NS], BF16, kind="ExternalInput").ap()
        wu_h = nc.dram_tensor("wu", [D, NS], BF16, kind="ExternalInput").ap()
        wd_h = nc.dram_tensor("wd", [NS, D], BF16, kind="ExternalInput").ap()
    xn_h = fz["out_ap"] if fz else nc.dram_tensor("xn", [D, TOK], F32, kind="ExternalOutput").ap()
    if debug_ffn:
        drs_h = nc.dram_tensor("drs", [D, TOK], F32, kind="ExternalOutput").ap()
        dit_h = nc.dram_tensor("dit", [128, 8 * 512], BF16, kind="ExternalOutput").ap()
        dpt_h = nc.dram_tensor("dpt", [D, TOK], F32, kind="ExternalOutput").ap()
        dxg_h = nc.dram_tensor("dxg", [W * D, TOK], BF16, kind="ExternalOutput").ap()
    ones_i = nc.inline_tensor(np.ones((128, 128), dtype=bfdt), name=pfx + "ones_c").ap()

    if fz:
        agi, xg, part, rso = fz["agi"], fz["xg"], fz["part"], fz["rso"]
    else:
        agi = nc.dram_tensor("agi", [D, TOK], BF16, kind="Internal").ap()
        xg = nc.dram_tensor("xg", [W * D, TOK], BF16, kind="Internal").ap()
        part = nc.dram_tensor("part", [W * D, TOK], F32, kind="Internal").ap()
        rso = nc.dram_tensor("rso", [D, TOK], F32, kind="Internal").ap()

    es = contextlib.ExitStack()
    with es:
        e.xr_sb = sbufT(es, nc, "xr_sb", [128, 16 * TOK], F32)
        _wgu = e.xr_sb[:, :].bitcast(BF16)
        wg_sb = _wgu[:, 0:16 * 1024]
        wu_sb = _wgu[:, 16 * 1024:32 * 1024]
        wd_sb = sbufT(es, nc, "wd_sb", [128, 8 * D], BF16)
        wi_sb = sbufT(es, nc, "wi_sb", [128, 2 * 2048], I32)
        xk_sb = sbufT(es, nc, "xk_sb", [128, 2 * 16 * 512], BF16)
        it_sb = sbufT(es, nc, "it_sb", [128, 8 * 512], BF16)
        t1_sb = sbufT(es, nc, "t1_sb", [128, 2 * 512], BF16)
        t2_sb = sbufT(es, nc, "t2_sb", [128, 2 * 512], F32)
        po_sb = sbufT(es, nc, "po_sb", [128, 4 * 512], F32)
        e.xo_sb = sbufT(es, nc, "xo_sb", [128, 2 * TOK], F32)
        e.xb1_sb = sbufT(es, nc, "xb1_sb", [128, TOK], F32)
        e.sq1_sb = sbufT(es, nc, "sq1_sb", [128, 2 * TOK], F32)
        e.st_sb = sbufT(es, nc, "st_sb", [1, 4096], F32)
        e.o32_sb = sbufT(es, nc, "o32_sb", [128, 128], F32)
        e.ones_sb = sbufT(es, nc, "ones_sb", [128, 128], BF16)
        e.fl_sb = sbufT(es, nc, "fl_sb", [128, 2], F32)
        if layer == 0:
            gsc_sb = sbufT(es, nc, "gsc_sb", [128, 8], F32)
            usc_sb = sbufT(es, nc, "usc_sb", [128, 8], F32)
            dsc_sb = sbufT(es, nc, "dsc_sb", [128, 16], F32)

        e.ps_a = psumT(es, nc, "ps_a", [128, 512], F32)
        e.ps_b = psumT(es, nc, "ps_b", [128, 512], F32)
        e.ps_c = psumT(es, nc, "ps_c", [128, 512], F32)
        e.ps_d = psumT(es, nc, "ps_d", [128, 512], F32)
        e.ps_ba = psumT(es, nc, "ps_ba", [128, 512], F32)
        e.ps_bb = psumT(es, nc, "ps_bb", [128, 512], F32)
        e.ps_bc = psumT(es, nc, "ps_bc", [128, 512], F32)
        e.ps_bd = psumT(es, nc, "ps_bd", [128, 512], F32)

        if fz:
            sm = fz["sems"]
            ld, ag, wl, xt = sm["ld"], sm["ag"], sm["wl"], sm["xt"]
            e.mm = mm = sm["mm"]; e.ev = ev = sm["ev"]
            e.ex = ex = sm["ex"]; e.so = so = sm["so"]
        else:
            ld = Sem(es, nc, "ld")
            ag = Sem(es, nc, "ag")
            wl = Sem(es, nc, "wl")
            xt = Sem(es, nc, "xt")
            e.mm = mm = Sem(es, nc, "mm")
            e.ev = ev = Sem(es, nc, "ev")
            e.ex = ex = Sem(es, nc, "ex")
            e.so = so = Sem(es, nc, "so")
        sy, gg, ve, sc, te = nc.sync, nc.gpsimd, nc.vector, nc.scalar, nc.tensor

        ld.inc(gg.dma_start(out=agi[:, :], in_=x_h[:, :]), 16)
        ld.inc(sy.dma_start(out=e.ones_sb[:], in_=ones_i[:, :]), 16)
        if layer == 0:
            ld.inc(sy.dma_start(out=gsc_sb[:], in_=gsc_h[:, :]), 16)
            ld.inc(sy.dma_start(out=usc_sb[:], in_=usc_h[:, :]), 16)
            ld.inc(sy.dma_start(out=dsc_sb[:], in_=dsc_h[:, :]), 16)
        gg.wait_ge(ld.h, ld.n)
        ag.inc(gg.collective_compute("AllGather", ALU.bypass,
                                     replica_groups=[list(range(W))],
                                     ins=[agi[:, :]], outs=[xg[:, :]]), 1)
        sy.wait_ge(ag.h, ag.n)
        gg.wait_ge(ag.h, ag.n)

        def xg_win(tt):
            r, cs = tt // 2, 512 * (tt % 2)
            return (xg[r * D:(r + 1) * D, cs:cs + 512]
                    .rearrange("(c p) t -> p c t", p=128))

        xk_mm = [0, 0]
        xt_at = [0, 0]
        g_ex = [0, 0]
        g_ev = [0, 0]
        t_ev = [0, 0]
        o_ev = [0, 0]
        po_so = [0, 0, 0, 0]
        it_mm = [0]
        for sub in range(NSUB):
            if sub > 0:
                gg.wait_ge(mm.h, mm.n)
                ve.wait_ge(mm.h, mm.n)
            if layer == 0:
                def deq_load(w_h, dst_sb, nrows, width, iblk):
                    for ib in range(nrows):
                        h = ib % 2
                        if ib >= 2 or iblk > 0:
                            gg.wait_ge(ev.h, ev.n)
                        wl.inc(gg.dma_start(out=wi_sb[:, h * 2048:h * 2048 + width],
                                            in_=w_h[ib * 128:(ib + 1) * 128, :]), 16)
                        ve.wait_ge(wl.h, wl.n)
                        ev.inc(ve.tensor_scalar(dst_sb[:, ib * width:(ib + 1) * width],
                                                wi_sb[:, h * 2048:h * 2048 + width],
                                                -128.0, None, ALU.add), 1)
                deq_load(wg_h, wg_sb, 16, 1024, 0)
                deq_load(wu_h, wu_sb, 16, 1024, 1)
                deq_load(wd_h, wd_sb, 8, 2048, 2)
            else:
                wl.inc(gg.dma_start(out=wg_sb[:].rearrange("p (c n) -> p c n", c=16),
                                    in_=wg_h[:, sub * 1024:(sub + 1) * 1024]
                                    .rearrange("(c p) n -> p c n", p=128)), 16)
                wl.inc(gg.dma_start(out=wu_sb[:].rearrange("p (c n) -> p c n", c=16),
                                    in_=wu_h[:, sub * 1024:(sub + 1) * 1024]
                                    .rearrange("(c p) n -> p c n", p=128)), 16)
                wl.inc(gg.dma_start(out=wd_sb[:].rearrange("p (c n) -> p c n", c=8),
                                    in_=wd_h[sub * 1024:(sub + 1) * 1024, :]
                                    .rearrange("(c p) n -> p c n", p=128)), 16)
            te.wait_ge(wl.h, wl.n)
            te.wait_ge(ev.h, ev.n)

            for tt in range(16):
                xslot = tt % 2
                sy.wait_ge(mm.h, xk_mm[xslot])
                xt.inc(sy.dma_start(out=xk_sb[:, xslot * 8192:(xslot + 1) * 8192]
                                    .rearrange("p (c t) -> p c t", c=16),
                                    in_=xg_win(tt)), 16)
                xt_at[xslot] = xt.n
                xkv = xk_sb[:, xslot * 8192:(xslot + 1) * 8192]
                te.wait_ge(xt.h, xt_at[xslot])
                for nch in range(8):
                    r = nch % 2
                    psg = (e.ps_a, e.ps_b)[r]
                    psu = (e.ps_c, e.ps_d)[r]
                    te.wait_ge(ex.h, g_ex[r])
                    te.wait_ge(ev.h, g_ev[r])
                    for kc in range(16):
                        inst = te.matmul(psg[:, :],
                                         wg_sb[:, kc * 1024 + nch * 128:
                                               kc * 1024 + nch * 128 + 128],
                                         xkv[:, kc * 512:(kc + 1) * 512],
                                         start=(kc == 0), stop=(kc == 15))
                    for kc in range(16):
                        inst = te.matmul(psu[:, :],
                                         wu_sb[:, kc * 1024 + nch * 128:
                                               kc * 1024 + nch * 128 + 128],
                                         xkv[:, kc * 512:(kc + 1) * 512],
                                         start=(kc == 0), stop=(kc == 15))
                    mm.inc(inst, 1)
                    sc.wait_ge(mm.h, mm.n)
                    sc.wait_ge(ev.h, t_ev[r])
                    t1v = t1_sb[:, r * 512:(r + 1) * 512]
                    t2v = t2_sb[:, r * 512:(r + 1) * 512]
                    ve.wait_ge(mm.h, it_mm[0])
                    if layer == 0:
                        ex.inc(sc.activation(t1v, psg[:, :], AF.Silu,
                                             scale=gsc_sb[:, sub * 8 + nch:
                                                          sub * 8 + nch + 1]), 1)
                        ve.wait_ge(ex.h, ex.n)
                        ev.inc(ve.scalar_tensor_tensor(
                            it_sb[:, nch * 512:(nch + 1) * 512], psu[:, :],
                            usc_sb[:, sub * 8 + nch:sub * 8 + nch + 1],
                            t1v, ALU.mult, ALU.mult), 1)
                    else:
                        ex.inc(sc.activation(t1v, psg[:, :], AF.Square), 1)
                        ve.wait_ge(ex.h, ex.n)
                        vw(e, ve.tensor_scalar(t2v, t1v,
                                               0.7978845608028654 * 0.044715,
                                               0.7978845608028654, ALU.mult, ALU.add))
                        vw(e, ve.tensor_tensor(t2v, t2v, psg[:, :], ALU.mult))
                        sc.wait_ge(ev.h, ev.n)
                        ex.inc(sc.activation(t2v, t2v, AF.Tanh), 1)
                        ve.wait_ge(ex.h, ex.n)
                        vw(e, ve.scalar_tensor_tensor(t2v, t2v, 1.0,
                                                      psg[:, :], ALU.add, ALU.mult))
                        ev.inc(ve.scalar_tensor_tensor(
                            it_sb[:, nch * 512:(nch + 1) * 512], psu[:, :], 0.5,
                            t2v, ALU.mult, ALU.mult), 1)
                    g_ex[r] = ex.n
                    g_ev[r] = ev.n
                    t_ev[r] = ev.n
                for ht in range(16):
                    r = ht % 2
                    pso = (e.ps_ba, e.ps_bb)[r]
                    te.wait_ge(ev.h, o_ev[r])
                    if ht == 0:
                        te.wait_ge(ev.h, ev.n)
                    for nch in range(8):
                        inst = te.matmul(pso[:, :],
                                         wd_sb[:, nch * D + ht * 128:
                                               nch * D + ht * 128 + 128],
                                         it_sb[:, nch * 512:(nch + 1) * 512],
                                         start=(nch == 0), stop=(nch == 7))
                    mm.inc(inst, 1)
                    ve.wait_ge(mm.h, mm.n)
                    ps4 = ht % 4
                    ve.wait_ge(so.h, po_so[ps4])
                    pc = po_sb[:, ps4 * 512:ps4 * 512 + 512]
                    if layer == 0:
                        ev.inc(ve.tensor_scalar(pc, pso[:, :],
                                                dsc_sb[:, ht:ht + 1], None,
                                                ALU.mult), 1)
                    else:
                        ev.inc(ve.tensor_copy(pc, pso[:, :]), 1)
                    o_ev[r] = ev.n
                    gg.wait_ge(ev.h, ev.n)
                    rr, cs = tt // 2, 512 * (tt % 2)
                    dst = part[rr * D + ht * 128:rr * D + ht * 128 + 128, cs:cs + 512]
                    if sub == 0:
                        so.inc(gg.dma_start(out=dst, in_=pc), 16)
                    else:
                        so.inc(gg.dma_start(out=dst, in_=pc,
                                            accum_op=ALU.add), 16)
                    po_so[ps4] = so.n
                xk_mm[xslot] = mm.n
                it_mm[0] = mm.n
        gg.wait_ge(so.h, so.n)
        ag.inc(gg.collective_compute("ReduceScatter", ALU.add,
                                     replica_groups=[list(range(W))],
                                     ins=[part[:, :]], outs=[rso[:, :]]), 1)
        sy.wait_ge(ag.h, ag.n)
        if debug_ffn:
            gg.wait_ge(ag.h, ag.n)
            so.inc(gg.dma_start(out=drs_h[:, :], in_=rso[:, :]), 16)
        ld.inc(sy.dma_start(out=e.xr_sb[:].rearrange("p (c t) -> p c t", c=16),
                            in_=x_h.rearrange("(c p) t -> p c t", p=128)), 16)
        for dt in range(16):
            if dt > 0:
                sy.wait_ge(ev.h, ev.n)
            ld.inc(sy.dma_start(out=e.xo_sb[:, 0:1024],
                                in_=rso[dt * 128:(dt + 1) * 128, :]), 16)
            ve.wait_ge(ld.h, ld.n)
            ev.inc(ve.tensor_tensor(e.xr_sb[:, dt * 1024:(dt + 1) * 1024],
                                    e.xr_sb[:, dt * 1024:(dt + 1) * 1024],
                                    e.xo_sb[:, 0:1024], ALU.add), 1)
        emit_ln(e, xn_h)
        gg.wait_ge(so.h, so.n)
    return nc


def emit_decode(e, pk, dlo, dhi, tmp6):
    ve = e.nc.vector
    a, b, u, a2, b2, u2 = tmp6
    ev = e.ev
    ve.tensor_single_scalar(a[:, :], pk[:, :], 7, ALU.bitwise_and)
    ve.tensor_scalar(a2[:, :], pk[:, :], 4, 7, ALU.logical_shift_right,
                     ALU.bitwise_and)
    ve.tensor_scalar(b[:, :], pk[:, :], 8, 12, ALU.bitwise_and,
                     ALU.logical_shift_left)
    ve.tensor_scalar(b2[:, :], pk[:, :], 128, 8, ALU.bitwise_and,
                     ALU.logical_shift_left)
    ve.tensor_single_scalar(a[:, :], a[:, :], 252, ALU.add)
    ve.tensor_single_scalar(a2[:, :], a2[:, :], 252, ALU.add)
    ve.tensor_single_scalar(u[:, :], a[:, :], 253, ALU.is_equal)
    ve.tensor_single_scalar(u2[:, :], a2[:, :], 253, ALU.is_equal)
    ve.tensor_tensor(u[:, :], a[:, :], u[:, :], ALU.subtract)
    ve.tensor_tensor(u2[:, :], a2[:, :], u2[:, :], ALU.subtract)
    ve.tensor_scalar(a[:, :], a[:, :], 253, 64, ALU.is_ge, ALU.mult)
    ve.tensor_scalar(a2[:, :], a2[:, :], 253, 64, ALU.is_ge, ALU.mult)
    ve.tensor_tensor(u[:, :], u[:, :], a[:, :], ALU.mult)
    ve.tensor_tensor(u2[:, :], u2[:, :], a2[:, :], ALU.mult)
    ve.tensor_tensor(u[:, :], u[:, :], b[:, :], ALU.bitwise_or)
    ve.tensor_tensor(u2[:, :], u2[:, :], b2[:, :], ALU.bitwise_or)
    ev.inc(ve.tensor_copy(dlo[:, :].bitcast(mybir.dt.uint16), u[:, :]), 1)
    ev.inc(ve.tensor_copy(dhi[:, :].bitcast(mybir.dt.uint16), u2[:, :]), 1)


def build_decode(fz=None):
    pfx = fz["pfx"] if fz else ""
    _NPFX[0] = pfx
    nc = fz["nc"] if fz else bass.Bass("TRN2", target_bir_lowering=False, num_devices=W)
    e = SimpleNamespace()
    e.nc = nc
    ins, outs, scs = {}, {}, {}
    for nm in ("g", "u", "d"):
        ins[nm] = nc.dram_tensor(pfx + nm + "pk", [1024, 2048], I32, kind="ExternalInput").ap()
        scs[nm] = nc.dram_tensor(pfx + nm + "scT", [64, 2048], F32, kind="ExternalInput").ap()
        if fz:
            outs[nm] = fz[nm + "dec"]
        else:
            outs[nm] = nc.dram_tensor(nm + "dec", [2048, 2048], BF16, kind="ExternalOutput").ap()
    G_np = np.zeros((8, 128), dtype=np.float32)
    for a in range(8):
        G_np[a, a * 16:(a + 1) * 16] = 1
    G_i = nc.inline_tensor(G_np, name=pfx + "G_c").ap()

    es = contextlib.ExitStack()
    with es:
        pk_sb = sbufT(es, nc, "pk_sb", [128, 2 * 2048], I32)
        m_sb = sbufT(es, nc, "m_sb", [128, 2048], I32)
        t_sb = sbufT(es, nc, "t_sb", [128, 2048], I32)
        u_sb = sbufT(es, nc, "u_sb", [128, 2048], I32)
        m2_sb = sbufT(es, nc, "m2_sb", [128, 2048], I32)
        t2_sb = sbufT(es, nc, "t2_sb", [128, 2048], I32)
        u2_sb = sbufT(es, nc, "u2_sb", [128, 2048], I32)
        dlo_sb = sbufT(es, nc, "dlo_sb", [128, 2048], BF16)
        dhi_sb = sbufT(es, nc, "dhi_sb", [128, 2048], BF16)
        olo_sb = sbufT(es, nc, "olo_sb", [128, 2 * 2048], BF16)
        ohi_sb = sbufT(es, nc, "ohi_sb", [128, 2 * 2048], BF16)
        sc_sb = sbufT(es, nc, "sc_sb", [8, 2 * 2048], F32)
        G_sb = sbufT(es, nc, "G_sb", [8, 128], F32)
        ps_sc = psumT(es, nc, "ps_sc", [128, 512], F32)
        ps_s2 = psumT(es, nc, "ps_s2", [128, 512], F32)

        if fz:
            sm = fz["sems"]
            ld = sm["ld"]
            e.mm = mm = sm["mm"]; e.ev = ev = sm["ev"]; e.so = so = sm["so"]
        else:
            ld = Sem(es, nc, "ld")
            e.mm = mm = Sem(es, nc, "mm")
            e.ev = ev = Sem(es, nc, "ev")
            e.so = so = Sem(es, nc, "so")
        sy, gg, ve, te = nc.sync, nc.gpsimd, nc.vector, nc.tensor

        ld.inc(sy.dma_start(out=G_sb[:], in_=G_i[:, :]), 16)
        its = [(nm, rt) for nm in ("g", "u", "d") for rt in range(8)]
        pk_ev = [0, 0]
        sc_mm = [0, 0]
        ld_mark = [0, 0]
        out_so = [0, 0]
        psc_ev = [0, 0]

        def emit_loads(it):
            nm, rt = its[it]
            h = it % 2
            sy.wait_ge(ev.h, pk_ev[h])
            sy.wait_ge(mm.h, sc_mm[h])
            ld.inc(sy.dma_start(out=pk_sb[:, h * 2048:(h + 1) * 2048],
                                in_=ins[nm][rt * 128:(rt + 1) * 128, :]), 16)
            ld.inc(sy.dma_start(out=sc_sb[0:8, h * 2048:(h + 1) * 2048],
                                in_=scs[nm][rt * 8:(rt + 1) * 8, :]), 16)
            ld_mark[h] = ld.n

        emit_loads(0)
        for it in range(24):
            nm, rt = its[it]
            h = it % 2
            if it + 1 < 24:
                emit_loads(it + 1)
            pk = pk_sb[:, h * 2048:(h + 1) * 2048]
            ve.wait_ge(ld.h, ld_mark[h])
            ve.wait_ge(so.h, out_so[h])
            emit_decode(e, pk, dlo_sb, dhi_sb,
                        (m_sb, t_sb, u_sb, m2_sb, t2_sb, u2_sb))
            pk_ev[h] = ev.n
            te.wait_ge(ld.h, ld_mark[h])
            for cb in range(4):
                psc = (ps_sc, ps_s2)[cb % 2]
                te.wait_ge(ev.h, psc_ev[cb % 2])
                mm.inc(te.matmul(psc[:, :], G_sb[0:8, :],
                                 sc_sb[0:8, h * 2048 + cb * 512:
                                       h * 2048 + cb * 512 + 512],
                                 start=True, stop=True), 1)
                ve.wait_ge(mm.h, mm.n)
                cbs = slice(h * 2048 + cb * 512, h * 2048 + (cb + 1) * 512)
                dbs = slice(cb * 512, (cb + 1) * 512)
                ev.inc(ve.tensor_tensor(olo_sb[:, cbs], dlo_sb[:, dbs],
                                        psc[:, :], ALU.mult), 1)
                ev.inc(ve.tensor_tensor(ohi_sb[:, cbs], dhi_sb[:, dbs],
                                        psc[:, :], ALU.mult), 1)
                psc_ev[cb % 2] = ev.n
            sc_mm[h] = mm.n
            gg.wait_ge(ev.h, ev.n)
            ool = olo_sb[:, h * 2048:(h + 1) * 2048]
            ooh = ohi_sb[:, h * 2048:(h + 1) * 2048]
            if nm == "d":
                so.inc(gg.dma_start(out=outs[nm][(2 * rt) * 128:(2 * rt + 1) * 128, :],
                                    in_=ool), 16)
                so.inc(gg.dma_start(out=outs[nm][(2 * rt + 1) * 128:(2 * rt + 2) * 128, :],
                                    in_=ooh), 16)
            else:
                so.inc(gg.dma_start(out=outs[nm][256 * rt:256 * rt + 256:2, :],
                                    in_=ool), 16)
                so.inc(gg.dma_start(out=outs[nm][256 * rt + 1:256 * rt + 256:2, :],
                                    in_=ooh), 16)
            out_so[h] = so.n
        gg.wait_ge(so.h, so.n)
    return nc


def l1_perm(c):
    perm = np.zeros(2048, dtype=np.int64)
    for ci in range(16):
        for p in range(128):
            perm[ci * 128 + p] = 2048 * c + 256 * (ci // 2) + 2 * p + (ci % 2)
    return perm


def host_decode_inputs(inp):
    gp = np.asarray(inp["L1_ffn_gate_W_packed"])
    up = np.asarray(inp["L1_ffn_up_W_packed"])
    dp = np.asarray(inp["L1_ffn_down_W_packed"])
    gs = np.asarray(inp["L1_ffn_gate_scales"])
    us = np.asarray(inp["L1_ffn_up_scales"])
    ds = np.asarray(inp["L1_ffn_down_scales"])
    gpT = gp.T
    upT = up.T
    dpT = dp.T
    maps = []
    for c in range(W):
        perm = l1_perm(c)
        m = {
            "gpk": np.ascontiguousarray(gpT[:, perm]),
            "upk": np.ascontiguousarray(upT[:, perm]),
            "dpk": np.ascontiguousarray(dpT[1024 * c:1024 * (c + 1), :]),
            "gscT": np.ascontiguousarray(gs[perm, :].T),
            "uscT": np.ascontiguousarray(us[perm, :].T),
            "dscT": np.ascontiguousarray(ds[:, 64 * c:64 * (c + 1)].T),
        }
        maps.append(m)
    return maps


def host_oln_inputs(layer, inp, xcur, ys):
    x = np.asarray(xcur, dtype=np.float32).reshape(T, D)
    wo = np.asarray(inp["L0_attn_Wo"]).T if layer == 0 else np.asarray(inp["L1_attn_Wo"])
    wo = np.ascontiguousarray(wo.astype(np.float32))
    maps = []
    for c in range(W):
        yfm = np.ascontiguousarray(
            np.concatenate([np.asarray(ys[r])[:, c * TOK:(c + 1) * TOK]
                            for r in range(W)], axis=0))
        maps.append({"y": yfm, "wo": wo,
                     "x_own": np.ascontiguousarray(x[c * TOK:(c + 1) * TOK].T)})
    return maps


def host_ffn_inputs(layer, inp, xcur, decoded=None):
    x = np.asarray(xcur, dtype=np.float32).reshape(T, D)
    maps = []
    for c in range(W):
        m = {"x_own": np.ascontiguousarray(x[c * TOK:(c + 1) * TOK].T)}
        if layer == 0:
            s = slice(1024 * c, 1024 * (c + 1))
            m["wg"] = np.ascontiguousarray(np.asarray(inp["L0_ffn_gate_w"])[s].T)
            m["wu"] = np.ascontiguousarray(np.asarray(inp["L0_ffn_up_w"])[s].T)
            m["wd"] = np.ascontiguousarray(np.asarray(inp["L0_ffn_down_w"])[:, s].T)
            m["gsc"] = np.ascontiguousarray(
                np.asarray(inp["L0_ffn_gate_scale"])[s].reshape(8, 128).T)
            m["usc"] = np.ascontiguousarray(
                np.asarray(inp["L0_ffn_up_scale"])[s].reshape(8, 128).T)
            m["dsc"] = np.ascontiguousarray(
                np.asarray(inp["L0_ffn_down_scale"]).reshape(16, 128).T)
        else:
            m["wg"] = decoded[c]["gdec"]
            m["wu"] = decoded[c]["udec"]
            m["wd"] = decoded[c]["ddec"]
        maps.append(m)
    return maps




def build_fused():
    nc = bass.Bass("TRN2", target_bir_lowering=False, num_devices=W)
    es_top = contextlib.ExitStack()
    nc._fused_es = es_top
    sems = {nm: Sem(es_top, nc, nm) for nm in
            ("ld", "cv", "ag", "xt", "wl", "mm", "ev", "ex", "so")}

    x0 = nc.dram_tensor("x_own", [D, TOK], F32, kind="ExternalInput").ap()
    out_h = nc.dram_tensor("xn", [D, TOK], F32, kind="ExternalOutput").ap()
    agi = nc.dram_tensor("agi", [D, TOK], BF16, kind="Internal").ap()
    xg = nc.dram_tensor("xg", [W * D, TOK], BF16, kind="Internal").ap()
    part = nc.dram_tensor("part", [W * D, TOK], F32, kind="Internal").ap()
    rso = nc.dram_tensor("rso", [D, TOK], F32, kind="Internal").ap()
    ytm = nc.dram_tensor("ytm", [T, 256], BF16, kind="Internal").ap()
    ygath = nc.dram_tensor("ygath", [W * T, 256], BF16, kind="Internal").ap()
    x1 = nc.dram_tensor("x1", [D, TOK], F32, kind="Internal").ap()
    x2 = nc.dram_tensor("x2", [D, TOK], F32, kind="Internal").ap()
    x3 = nc.dram_tensor("x3", [D, TOK], F32, kind="Internal").ap()
    gdec = nc.dram_tensor("gdec", [2048, 2048], BF16, kind="Internal").ap()
    udec = nc.dram_tensor("udec", [2048, 2048], BF16, kind="Internal").ap()
    ddec = nc.dram_tensor("ddec", [2048, 2048], BF16, kind="Internal").ap()

    base = dict(nc=nc, sems=sems, agi=agi, xg=xg, part=part, rso=rso,
                ytm=ytm, ygath=ygath)

    with nc.named_scope("dec"):
        build_decode(fz={**base, "pfx": "dc_", "gdec": gdec, "udec": udec, "ddec": ddec})
        nc.all_engine_barrier()
    with nc.named_scope("a0"):
        build_attn(0, fz={**base, "pfx": "a0_", "x_ap": x0})
        nc.all_engine_barrier()
    with nc.named_scope("o0"):
        build_o_ln(fz={**base, "pfx": "o0_", "x_ap": x0, "out_ap": x1})
        nc.all_engine_barrier()
    with nc.named_scope("f0"):
        build_ffn(0, fz={**base, "pfx": "f0_", "x_ap": x1, "out_ap": x2})
        nc.all_engine_barrier()
    with nc.named_scope("a1"):
        build_attn(1, fz={**base, "pfx": "a1_", "x_ap": x2})
        nc.all_engine_barrier()
    with nc.named_scope("o1"):
        build_o_ln(fz={**base, "pfx": "o1_", "x_ap": x2, "out_ap": x3})
        nc.all_engine_barrier()
    with nc.named_scope("f1"):
        build_ffn(1, fz={**base, "pfx": "f1_", "x_ap": x3,
                         "wg_ap": gdec, "wu_ap": udec, "wd_ap": ddec,
                         "out_ap": out_h})
    es_top.close()
    return nc


def host_fused_inputs(inp):
    x = np.asarray(inp["L0_attn_x"], dtype=np.float32).reshape(T, D)
    a0 = host_attn_inputs(0, inp)
    a1 = host_attn_inputs(1, inp)
    dc = host_decode_inputs(inp)
    f0 = host_ffn_inputs(0, inp, x)
    wo0 = np.ascontiguousarray(np.asarray(inp["L0_attn_Wo"]).T.astype(np.float32))
    wo1 = np.ascontiguousarray(np.asarray(inp["L1_attn_Wo"]).astype(np.float32))
    maps = []
    for c in range(W):
        yidx = np.zeros((128, 64), dtype=np.int32)
        for r in range(W):
            for tt in range(8):
                yidx[:, r * 8 + tt] = r * T + c * TOK + tt * 128 + np.arange(128)
        m = {"x_own": np.ascontiguousarray(x[c * TOK:(c + 1) * TOK].T)}
        for k, v in a0[c].items():
            if k != "x_own":
                m["a0_" + k] = v
        for k, v in a1[c].items():
            if k != "x_own":
                m["a1_" + k] = v
        for k, v in dc[c].items():
            m["dc_" + k] = v
        for k, v in f0[c].items():
            if k != "x_own":
                m["f0_" + k] = v
        m["o0_wo"] = wo0
        m["o1_wo"] = wo1
        m["o0_yidx"] = yidx
        m["o1_yidx"] = yidx
        maps.append(m)
    return maps


def kernel_fused(**inputs):
    nc = build_fused()
    maps = host_fused_inputs(inputs)
    res = run_bass_kernel_spmd(nc, maps, core_ids=list(range(W)))
    x4 = _gather_xn(res.results)
    return np.ascontiguousarray(x4.reshape(B, TB, D).astype(np.float32))


def _run(nc, maps):
    res = run_bass_kernel_spmd(nc, maps, core_ids=list(range(W)))
    return res.results


def _gather_xn(res):
    return np.concatenate([np.asarray(res[c]["xn"]).T for c in range(W)], axis=0)


def kernel_multi(**inputs):
    inp = inputs
    x0 = np.asarray(inp["L0_attn_x"], dtype=np.float32)

    res = _run(build_attn(0), host_attn_inputs(0, {**inp, "L0_attn_x": x0}))
    ys = [np.asarray(res[c]["y"]) for c in range(W)]
    res = _run(build_o_ln(), host_oln_inputs(0, inp, x0, ys))
    x1 = _gather_xn(res)

    res = _run(build_ffn(0), host_ffn_inputs(0, inp, x1))
    x2 = _gather_xn(res)

    res = _run(build_attn(1), host_attn_inputs(1, {**inp, "L0_attn_x": x2}))
    ys = [np.asarray(res[c]["y"]) for c in range(W)]
    res = _run(build_o_ln(), host_oln_inputs(1, inp, x2, ys))
    x3 = _gather_xn(res)

    res = _run(build_decode(), host_decode_inputs(inp))
    decoded = [{k: np.asarray(res[c][k]) for k in ("gdec", "udec", "ddec")}
               for c in range(W)]
    res = _run(build_ffn(1), host_ffn_inputs(1, inp, x3, decoded))
    x4 = _gather_xn(res)

    return np.ascontiguousarray(x4.reshape(B, TB, D).astype(np.float32))


def kernel(**inputs):
    return kernel_fused(**inputs)


# version 11
# speedup vs baseline: 1.6141x; 1.0390x over previous
import math
import contextlib
from types import SimpleNamespace
import numpy as np
import ml_dtypes
import concourse.bass as bass
import concourse.mybir as mybir
from concourse.bass_core_rust import IndirectOffsetOnAxis
from concourse.bass_utils import run_bass_kernel_spmd

F32 = mybir.dt.float32
BF16 = mybir.dt.bfloat16
I32 = mybir.dt.int32
AF = mybir.ActivationFunctionType
ALU = mybir.AluOpType
AX = mybir.AxisListType

W = 8
B = 2
TB = 4096
T = B * TB
D = 2048
TOK = T // W
NEG = -1.0e30
bfdt = ml_dtypes.bfloat16


class Sem:
    def __init__(self, es, nc, name):
        self.h = es.enter_context(nc.semaphore(name))
        self.n = 0

    def inc(self, inst, k):
        inst.then_inc(self.h, k)
        self.n += k
        return self.n


def vw(e, inst):
    e.ev.inc(inst, 1)
    e.nc.vector.wait_ge(e.ev.h, e.ev.n)


_NPFX = [""]


def sbufT(es, nc, name, shape, dtype):
    return es.enter_context(nc.sbuf_tensor(_NPFX[0] + name, shape, dtype))


def psumT(es, nc, name, shape, dtype=F32):
    return es.enter_context(nc.psum_tensor(_NPFX[0] + name, shape, dtype))


def build_masks(layer):
    p = np.arange(128)[:, None]
    f = np.arange(512)[None, :]
    tiles, idx = [], {}

    def add(key, allow):
        idx[key] = len(tiles)
        allow = np.broadcast_to(allow, (128, 512))
        tiles.append(np.where(allow, np.float32(0.0), np.float32(NEG)))

    for o in (0, 128, 256, 384):
        add(("c", o), f <= p + o)
    if layer == 0:
        for o in (1, 129, 257, 385):
            add(("w", o), f >= p + o)
            add(("ws", o), (f >= p + o) | (f < 4))
            add(("wp", o - 512), f >= p + o - 512)
            add(("wsp", o - 512), (f >= p + o - 512) | (f < 4))
        add(("s",), f < 4)
    return np.stack(tiles).astype(np.float32), idx


def chunk_plan(layer, i0b):
    qc = i0b // 512
    if layer == 1:
        return [(jc, None) for jc in range(qc)] + [(qc, ("c", i0b % 512))]
    wlo = i0b - 1023
    wbc = max(0, wlo // 512)
    plan = []
    if wbc > 0:
        plan.append((0, ("s",)))
    for jc in range(wbc, qc + 1):
        if jc == qc:
            plan.append((jc, ("c", i0b % 512)))
            continue
        o = wlo - 512 * jc
        if o >= 1:
            plan.append((jc, ("ws" if jc == 0 else "w", o)))
        elif o > -511:
            plan.append((jc, ("wsp" if jc == 0 else "wp", o)))
        else:
            plan.append((jc, None))
    return plan


def build_attn(layer, debug_qkv=False, debug_attn=False, fz=None):
    NH = 2
    KVD = 128 if layer == 0 else 256
    KT = KVD // 128
    scale = 1.0 / math.sqrt(128 if layer == 0 else 64)
    masks_np, midx = build_masks(layer)
    NM = masks_np.shape[0]
    MAXC = 4 if layer == 0 else 8
    SLAB = max(MAXC, 8) * 512

    pfx = fz["pfx"] if fz else ""
    _NPFX[0] = pfx
    nc = fz["nc"] if fz else bass.Bass("TRN2", target_bir_lowering=False, num_devices=W)
    e = SimpleNamespace()
    e.nc, e.layer, e.scale, e.midx, e.MAXC, e.KT, e.NH = nc, layer, scale, midx, MAXC, KT, NH

    x_own = fz["x_ap"] if fz else nc.dram_tensor("x_own", [D, TOK], F32, kind="ExternalInput").ap()
    wq_h = nc.dram_tensor(pfx + "wq", [D, 256], F32, kind="ExternalInput").ap()
    wk_h = nc.dram_tensor(pfx + "wk", [D, KVD], F32, kind="ExternalInput").ap()
    wv_h = nc.dram_tensor(pfx + "wv", [D, KVD], F32, kind="ExternalInput").ap()
    masks_h = nc.dram_tensor(pfx + "masks", [NM * 128, 512], F32, kind="ExternalInput").ap()
    if layer == 1:
        lam_h = nc.dram_tensor(pfx + "lam", [1, 2], F32, kind="ExternalInput").ap()
    if fz:
        e.y_h = None
        e.ytm_ap = fz["ytm"]
    else:
        e.ytm_ap = None
        e.y_h = nc.dram_tensor("y", [NH * 128, T], BF16, kind="ExternalOutput").ap()
    if debug_qkv:
        xgdb = nc.dram_tensor("xgdb", [W * D, TOK], BF16, kind="ExternalOutput").ap()
        agdb = nc.dram_tensor("agdb", [D, TOK], BF16, kind="ExternalOutput").ap()
        agdb2 = nc.dram_tensor("agdb2", [D, TOK], BF16, kind="ExternalOutput").ap()
        qdb = nc.dram_tensor("qdb", [256, T], BF16, kind="ExternalOutput").ap()
        kdb = nc.dram_tensor("kdb", [KVD, T], BF16, kind="ExternalOutput").ap()
        vdb = nc.dram_tensor("vdb", [T, KVD], BF16, kind="ExternalOutput").ap()

    if debug_attn:
        e.dbg_bd = nc.dram_tensor("dbg_bd", [128, 64], F32, kind="ExternalOutput").ap()
        e.dbg_nrm = nc.dram_tensor("dbg_nrm", [128, 64], F32, kind="ExternalOutput").ap()
        e.dbg_kmb = nc.dram_tensor("dbg_kmb", [128, 4], F32, kind="ExternalOutput").ap()
        e.dbg_p1 = nc.dram_tensor("dbg_p1", [128, 4096], BF16, kind="ExternalOutput").ap()
        e.dbg_acc = nc.dram_tensor("dbg_acc", [128, 16], F32, kind="ExternalOutput").ap()
        e.dbg_sml = nc.dram_tensor("dbg_sml", [128, 8], F32, kind="ExternalOutput").ap()
    e.debug_attn = debug_attn
    if fz:
        agi, xg = fz["agi"], fz["xg"]
    else:
        agi = nc.dram_tensor("agi", [D, TOK], BF16, kind="Internal").ap()
        xg = nc.dram_tensor("xg", [W * D, TOK], BF16, kind="Internal").ap()

    ones_i = nc.inline_tensor(np.ones((128, 128), dtype=bfdt), name=pfx + "ones_c").ap()
    ident_i = nc.inline_tensor(np.eye(128, dtype=bfdt), name=pfx + "ident_c").ap()

    es = contextlib.ExitStack()
    with es:
        wq_sb = sbufT(es, nc, "wq_sb", [128, 16 * 256], BF16)
        wk_sb = sbufT(es, nc, "wk_sb", [128, 16 * KVD], BF16)
        wv_sb = sbufT(es, nc, "wv_sb", [128, 16 * KVD], BF16)
        xk_sb = sbufT(es, nc, "xk_sb", [128, 16 * 512], BF16)
        e.q_sb = sbufT(es, nc, "q_sb", [128, NH * T], BF16)
        e.k_sb = sbufT(es, nc, "k_sb", [128, KT * T], BF16)
        e.v_sb = sbufT(es, nc, "v_sb", [128, (T // 128) * KVD], BF16)
        e.y_sb = sbufT(es, nc, "y_sb", [128, TB], BF16)
        e.p1f = sbufT(es, nc, "p1f", [128, MAXC * 512], F32)
        if layer == 1:
            e.p2f = sbufT(es, nc, "p2f", [128, MAXC * 512], F32)
        e.pn_sb = sbufT(es, nc, "pn_sb", [128, 4096], BF16)
        e.p1_sb = e.pn_sb
        e.pt_sb = sbufT(es, nc, "pt_sb", [128, 2 * 512], BF16)
        e.mask_sb = sbufT(es, nc, "mask_sb", [128, NM * 512], F32)
        e.ones_sb = sbufT(es, nc, "ones_sb", [128, 128], BF16)
        e.id_sb = sbufT(es, nc, "id_sb", [128, 128], BF16)
        e.bd_sb = sbufT(es, nc, "bd_sb", [128, 64], F32)
        e.nrm_sb = sbufT(es, nc, "nrm_sb", [128, 256], F32)
        e.acc_sb = sbufT(es, nc, "acc_sb", [128, 2 * MAXC], F32)
        e.sml_sb = sbufT(es, nc, "sml_sb", [128, 8], F32)
        e.km_sb = sbufT(es, nc, "km_sb", [1, 8], F32)
        e.kmb16 = sbufT(es, nc, "kmb16", [1, 8], BF16)
        e.kmb_sb = sbufT(es, nc, "kmb_sb", [128, 4], F32)
        e.o32_sb = sbufT(es, nc, "o32_sb", [128, 128], F32)
        e.lamb_sb = sbufT(es, nc, "lamb_sb", [128, 2], F32)
        e.fl_sb = sbufT(es, nc, "fl_sb", [128, 2], F32)
        e.ytp_sb = sbufT(es, nc, "ytp_sb", [128, 256], BF16)

        e.ps_a = psumT(es, nc, "ps_a", [128, 512], F32)
        e.ps_b = psumT(es, nc, "ps_b", [128, 512], F32)
        e.ps_c = psumT(es, nc, "ps_c", [128, 512], F32)
        e.ps_d = psumT(es, nc, "ps_d", [128, 512], F32)
        e.ps_p0 = psumT(es, nc, "ps_p0", [128, 512], BF16)
        e.ps_p1 = psumT(es, nc, "ps_p1", [128, 512], BF16)
        e.ps_y = psumT(es, nc, "ps_y", [128, 128], F32)
        e.ps_n = psumT(es, nc, "ps_n", [128, 64], F32)

        if fz:
            sm = fz["sems"]
            ld, cv, ag, xt = sm["ld"], sm["cv"], sm["ag"], sm["xt"]
            e.mm = mm = sm["mm"]; e.ev = ev = sm["ev"]
            e.ex = ex = sm["ex"]; e.so = so = sm["so"]
        else:
            ld = Sem(es, nc, "ld")
            cv = Sem(es, nc, "cv")
            ag = Sem(es, nc, "ag")
            xt = Sem(es, nc, "xt")
            e.mm = mm = Sem(es, nc, "mm")
            e.ev = ev = Sem(es, nc, "ev")
            e.ex = ex = Sem(es, nc, "ex")
            e.so = so = Sem(es, nc, "so")

        sy, gg, ve, sc, te = nc.sync, nc.gpsimd, nc.vector, nc.scalar, nc.tensor

        ld.inc(gg.dma_start(out=wq_sb[:].rearrange("p (c o) -> p c o", c=16), in_=wq_h.rearrange("(c p) o -> p c o", p=128)), 16)
        ld.inc(gg.dma_start(out=wk_sb[:].rearrange("p (c o) -> p c o", c=16), in_=wk_h.rearrange("(c p) o -> p c o", p=128)), 16)
        ld.inc(gg.dma_start(out=wv_sb[:].rearrange("p (c o) -> p c o", c=16), in_=wv_h.rearrange("(c p) o -> p c o", p=128)), 16)
        ld.inc(sy.dma_start(out=e.mask_sb[:].rearrange("p (m f) -> p m f", f=512),
                            in_=masks_h.rearrange("(m p) f -> p m f", p=128)), 16)
        ld.inc(sy.dma_start(out=e.ones_sb[:], in_=ones_i[:, :]), 16)
        ld.inc(sy.dma_start(out=e.id_sb[:], in_=ident_i[:, :]), 16)
        if layer == 1:
            ld.inc(sy.dma_start(out=e.km_sb[0:1, 0:2], in_=lam_h[:, :]), 16)
        cv.inc(gg.dma_start(out=agi[:, :], in_=x_own[:, :]), 16)
        if debug_qkv:
            sy.wait_ge(cv.h, cv.n)
            ag.inc(sy.dma_start(out=agdb[:, :], in_=agi[:, :]), 16)
            gg.wait_ge(ag.h, ag.n)
        gg.wait_ge(cv.h, cv.n)
        ag.inc(gg.collective_compute("AllGather", ALU.bypass,
                                     replica_groups=[list(range(W))],
                                     ins=[agi[:, :]], outs=[xg[:, :]]), 1)
        sy.wait_ge(ag.h, ag.n)

        def xg_win(tt):
            r, cs = tt // 2, 512 * (tt % 2)
            return (xg[r * D:(r + 1) * D, cs:cs + 512]
                    .rearrange("(c p) t -> p c t", p=128))

        pj_ev = [0, 0, 0, 0]
        for tt in range(16):
            if tt > 0:
                sy.wait_ge(ev.h, ev.n)
            xt.inc(sy.dma_start(out=xk_sb[:].rearrange("p (c t) -> p c t", c=16), in_=xg_win(tt)), 16)
            te.wait_ge(xt.h, xt.n)
            rot = 0
            for dst_sb, wsb, no in ((e.q_sb, wq_sb, 2), (e.k_sb, wk_sb, KT)):
                for oh in range(no):
                    ps = (e.ps_a, e.ps_b)[rot % 2]
                    te.wait_ge(ev.h, pj_ev[rot % 2])
                    rot += 1
                    for kc in range(16):
                        inst = te.matmul(ps[:, :],
                                         wsb[:, kc * (no * 128) + oh * 128:
                                             kc * (no * 128) + oh * 128 + 128],
                                         xk_sb[:, kc * 512:(kc + 1) * 512],
                                         start=(kc == 0), stop=(kc == 15))
                    mm.inc(inst, 1)
                    ve.wait_ge(mm.h, mm.n)
                    ev.inc(ve.tensor_copy(dst_sb[:, oh * T + tt * 512:
                                                 oh * T + tt * 512 + 512], ps[:, :]), 1)
                    pj_ev[(rot - 1) % 2] = ev.n
            for ts in range(4):
                ps = (e.ps_c, e.ps_d)[ts % 2]
                te.wait_ge(ev.h, pj_ev[2 + ts % 2])
                for kc in range(16):
                    inst = te.matmul(ps[:, 0:KVD],
                                     xk_sb[:, kc * 512 + ts * 128:
                                           kc * 512 + ts * 128 + 128],
                                     wv_sb[:, kc * KVD:(kc + 1) * KVD],
                                     start=(kc == 0), stop=(kc == 15))
                mm.inc(inst, 1)
                ve.wait_ge(mm.h, mm.n)
                tg = tt * 4 + ts
                ev.inc(ve.tensor_copy(e.v_sb[:, tg * KVD:(tg + 1) * KVD],
                                      ps[:, 0:KVD]), 1)
                pj_ev[2 + ts % 2] = ev.n

        if debug_qkv:
            ev.inc(ve.tensor_copy(e.fl_sb[:, 0:1], e.fl_sb[:, 1:2]), 1)
            gg.wait_ge(ev.h, ev.n)
            so.inc(gg.dma_start(out=xgdb[:, :], in_=xg[:, :]), 16)
            so.inc(gg.dma_start(out=agdb2[:, :], in_=agi[:, :]), 16)
            so.inc(gg.dma_start(out=qdb.rearrange("(h p) t -> p h t", p=128),
                                in_=e.q_sb[:, :].rearrange("p (h t) -> p h t", t=T)), 16)
            so.inc(gg.dma_start(out=kdb.rearrange("(h p) t -> p h t", p=128),
                                in_=e.k_sb[:, :].rearrange("p (h t) -> p h t", t=T)), 16)
            so.inc(gg.dma_start(out=vdb.rearrange("(g p) o -> p g o", p=128),
                                in_=e.v_sb[:, :].rearrange("p (g o) -> p g o", g=T // 128)), 16)
            gg.wait_ge(so.h, so.n)
            return nc

        if layer == 1:
            sc.wait_ge(ld.h, ld.n)
            ex.inc(sc.activation(e.km_sb[0:1, 2:4], e.km_sb[0:1, 0:2], AF.Sigmoid), 1)
            ve.wait_ge(ex.h, ex.n)
            vw(e, ve.memset(e.o32_sb[0:1, :], 1.0))
            te.wait_ge(ev.h, ev.n)
            mm.inc(te.matmul(e.ps_n[:, 0:2], e.o32_sb[0:1, :], e.km_sb[0:1, 2:4],
                             start=True, stop=True), 1)
            ve.wait_ge(mm.h, mm.n)
            vw(e, ve.tensor_copy(e.lamb_sb[:, 0:2], e.ps_n[:, 0:2]))

        for b in range(B):
            for hl in range(NH):
                emit_attn_bh(e, b, hl)

        gg.wait_ge(so.h, so.n)
    return nc


def emit_attn_bh(e, b, hl):
    nc, layer = e.nc, e.layer
    sy, gg, ve, sc, te = nc.sync, nc.gpsimd, nc.vector, nc.scalar, nc.tensor
    mm, ev, ex, so = e.mm, e.ev, e.ex, e.so
    qcol = hl * T + b * TB
    kcol = (hl * T + b * TB) if layer == 1 else b * TB
    halves = ((0,),) if layer == 0 else ((0,), (64,))
    nsets = len(halves)
    MAXC, scale = e.MAXC, e.scale

    sc.wait_ge(mm.h, mm.n)
    ve.wait_ge(so.h, so.n)
    sc.wait_ge(ev.h, ev.n)
    ex.inc(sc.activation(e.p1_sb[:, 0:TB], e.q_sb[:, qcol:qcol + TB], AF.Square), 1)
    te.wait_ge(ex.h, ex.n)
    for si, (base,) in enumerate(halves):
        kw = {} if base == 0 else {"tile_position": (64, 0)}
        npart = 128 // nsets
        for qt in range(32):
            inst = te.matmul(e.ps_n[:, si * 32 + qt:si * 32 + qt + 1],
                             e.p1_sb[base:base + npart, qt * 128:(qt + 1) * 128],
                             e.ones_sb[base:base + npart, 0:1],
                             start=True, stop=True, **kw)
        mm.inc(inst, 1)
    ve.wait_ge(mm.h, mm.n)
    vw(e, ve.tensor_copy(e.nrm_sb[:, 0:64], e.ps_n[:, 0:64]))
    sc.wait_ge(ev.h, ev.n)
    ex.inc(sc.activation(e.p1_sb[:, 0:TB], e.k_sb[:, kcol:kcol + TB], AF.Square), 1)
    te.wait_ge(ex.h, ex.n)
    te.wait_ge(ev.h, ev.n)
    for si, (base,) in enumerate(halves):
        kw = {} if base == 0 else {"tile_position": (64, 0)}
        npart = 128 // nsets
        for jt in range(32):
            inst = te.matmul(e.ps_n[:, si * 32 + jt:si * 32 + jt + 1],
                             e.p1_sb[base:base + npart, jt * 128:(jt + 1) * 128],
                             e.ones_sb[base:base + npart, 0:1],
                             start=True, stop=True, **kw)
        mm.inc(inst, 1)
    ve.wait_ge(mm.h, mm.n)
    vw(e, ve.tensor_copy(e.nrm_sb[:, 64:128], e.ps_n[:, 0:64]))
    for si in range(nsets):
        vw(e, ve.reduce_max(e.sml_sb[:, si:si + 1],
                            e.nrm_sb[:, 64 + si * 32:64 + si * 32 + 32], axis=AX.X))
    vw(e, ve.memset(e.p1_sb[:, 0:128], NEG))
    for si in range(nsets):
        vw(e, ve.tensor_copy(e.p1_sb[:, si * 64:si * 64 + 1],
                             e.sml_sb[:, si:si + 1]))
    te.wait_ge(ev.h, ev.n)
    mm.inc(te.transpose(e.ps_p0[:, 0:128], e.p1_sb[:, 0:128], e.id_sb[:, :]), 1)
    ve.wait_ge(mm.h, mm.n)
    for si in range(nsets):
        vw(e, ve.tensor_copy(e.nrm_sb[si * 64:si * 64 + 1, 128:256],
                             e.ps_p0[si * 64:si * 64 + 1, 0:128]))
        vw(e, ve.reduce_max(e.km_sb[0:1, 4 + si:5 + si],
                            e.nrm_sb[si * 64:si * 64 + 1, 128:256], axis=AX.X))
    sc.wait_ge(ev.h, ev.n)
    ex.inc(sc.activation(e.km_sb[0:1, 6:6 + nsets], e.km_sb[0:1, 4:4 + nsets], AF.Sqrt), 1)
    ve.wait_ge(ex.h, ex.n)
    vw(e, ve.tensor_copy(e.kmb16[0:1, 6:6 + nsets], e.km_sb[0:1, 6:6 + nsets]))
    te.wait_ge(ev.h, ev.n)
    mm.inc(te.matmul(e.ps_n[:, 0:nsets], e.ones_sb[0:1, :], e.kmb16[0:1, 6:6 + nsets],
                     start=True, stop=True), 1)
    ve.wait_ge(mm.h, mm.n)
    vw(e, ve.tensor_copy(e.kmb_sb[:, 0:nsets], e.ps_n[:, 0:nsets]))
    sc.wait_ge(ev.h, ev.n)
    ex.inc(sc.activation(e.nrm_sb[:, 0:64], e.nrm_sb[:, 0:64], AF.Sqrt), 1)
    ve.wait_ge(ex.h, ex.n)
    for si in range(nsets):
        vw(e, ve.tensor_scalar(e.bd_sb[:, si * 32:si * 32 + 32],
                               e.nrm_sb[:, si * 32:si * 32 + 32],
                               e.kmb_sb[:, si:si + 1], -scale * 1.01,
                               ALU.mult, ALU.mult))

    if e.debug_attn and b == 0 and hl == 0:
        ev.inc(ve.tensor_copy(e.fl_sb[:, 0:1], e.fl_sb[:, 1:2]), 1)
        gg.wait_ge(ev.h, ev.n)
        so.inc(gg.dma_start(out=e.dbg_bd[:, :], in_=e.bd_sb[:, :]), 16)
        so.inc(gg.dma_start(out=e.dbg_nrm[:, :], in_=e.nrm_sb[:, 0:64]), 16)
        so.inc(gg.dma_start(out=e.dbg_kmb[:, :], in_=e.kmb_sb[:, :]), 16)
    rot_ex = [0, 0]
    ptc_ev = [0, 0]
    pts_mm = [0, 0]
    yso = [0, 0]
    pn_mm = [0]
    for qt in range(32):
        i0b = qt * 128
        plan = chunk_plan(layer, i0b)
        nj = len(plan)
        for ci, (jc, mk) in enumerate(plan):
            s1p = (e.ps_a, e.ps_b)[ci % 2]
            s2p = (e.ps_c, e.ps_d)[ci % 2]
            pss = (s1p,) if layer == 0 else (s1p, s2p)
            te.wait_ge(ex.h, rot_ex[ci % 2])
            for si, (base,) in enumerate(halves):
                kw = {} if base == 0 else {"tile_position": (64, 0)}
                npart = 128 // nsets
                inst = te.matmul(pss[si][:, :],
                                 e.q_sb[base:base + npart, qcol + i0b:qcol + i0b + 128],
                                 e.k_sb[base:base + npart,
                                        kcol + jc * 512:kcol + jc * 512 + 512],
                                 start=True, stop=True, **kw)
                mm.inc(inst, 1)
            if mk is not None:
                mi = e.midx[mk]
                ve.wait_ge(mm.h, mm.n)
                for ps in pss:
                    ev.inc(ve.tensor_tensor(ps[:, :], ps[:, :],
                                            e.mask_sb[:, mi * 512:(mi + 1) * 512],
                                            ALU.add), 1)
            sc.wait_ge(mm.h, mm.n)
            sc.wait_ge(ev.h, ev.n)
            for si, ps in enumerate(pss):
                slab = (e.p1f, e.p2f if layer == 1 else None)[si]
                ex.inc(sc.activation(slab[:, ci * 512:(ci + 1) * 512], ps[:, :],
                                     AF.Exp,
                                     bias=e.bd_sb[:, si * 32 + qt:si * 32 + qt + 1],
                                     scale=scale,
                                     accum_out=e.acc_sb[:, si * MAXC + ci:
                                                        si * MAXC + ci + 1]), 1)
            rot_ex[ci % 2] = ex.n
        ve.wait_ge(ex.h, ex.n)
        ve.wait_ge(mm.h, pn_mm[0])
        vw(e, ve.reduce_sum(e.sml_sb[:, 0:1], e.acc_sb[:, 0:nj], axis=AX.X))
        vw(e, ve.reciprocal(e.sml_sb[:, 1:2], e.sml_sb[:, 0:1]))
        if layer == 0:
            ev.inc(ve.tensor_scalar(e.pn_sb[:, 0:nj * 512], e.p1f[:, 0:nj * 512],
                                    e.sml_sb[:, 1:2], None, ALU.mult), 1)
        else:
            vw(e, ve.reduce_sum(e.sml_sb[:, 2:3], e.acc_sb[:, MAXC:MAXC + nj],
                                axis=AX.X))
            vw(e, ve.reciprocal(e.sml_sb[:, 3:4], e.sml_sb[:, 2:3]))
            vw(e, ve.tensor_tensor(e.sml_sb[:, 4:5], e.sml_sb[:, 0:1],
                                   e.sml_sb[:, 3:4], ALU.mult))
            vw(e, ve.tensor_scalar(e.sml_sb[:, 4:5], e.sml_sb[:, 4:5],
                                   e.lamb_sb[:, hl:hl + 1], -1.0,
                                   ALU.mult, ALU.mult))
            ev.inc(ve.scalar_tensor_tensor(e.p2f[:, 0:nj * 512],
                                           e.p2f[:, 0:nj * 512],
                                           e.sml_sb[:, 4:5],
                                           e.p1f[:, 0:nj * 512],
                                           ALU.mult, ALU.add), 1)
            ve.wait_ge(ev.h, ev.n)
            ev.inc(ve.tensor_scalar(e.pn_sb[:, 0:nj * 512], e.p2f[:, 0:nj * 512],
                                    e.sml_sb[:, 1:2], None, ALU.mult), 1)
        if e.debug_attn and b == 0 and hl == 0 and qt == 2:
            ev.inc(ve.tensor_copy(e.fl_sb[:, 0:1], e.fl_sb[:, 1:2]), 1)
            gg.wait_ge(ev.h, ev.n)
            so.inc(gg.dma_start(out=e.dbg_p1[:, 0:nj * 512], in_=e.p1_sb[:, 0:nj * 512]), 16)
            so.inc(gg.dma_start(out=e.dbg_acc[:, 0:2 * e.MAXC], in_=e.acc_sb[:, :]), 16)
            so.inc(gg.dma_start(out=e.dbg_sml[:, :], in_=e.sml_sb[:, :]), 16)
            gg.wait_ge(so.h, so.n)
        first = True
        tr_mm = [0] * nj
        norm_ev = ev.n

        def emit_tr(cj):
            r1 = cj % 2
            pt_ps1 = (e.ps_p0, e.ps_p1)[r1]
            te.wait_ge(ev.h, max(ptc_ev[r1], norm_ev))
            for jt in range(4):
                inst = te.transpose(pt_ps1[:, jt * 128:(jt + 1) * 128],
                                    e.pn_sb[:, cj * 512 + jt * 128:
                                            cj * 512 + jt * 128 + 128],
                                    e.id_sb[:, :])
            e.mm.inc(inst, 1)
            tr_mm[cj] = e.mm.n

        emit_tr(0)
        for ci, (jc, mk) in enumerate(plan):
            r = ci % 2
            pt_ps = (e.ps_p0, e.ps_p1)[r]
            pt_slot = e.pt_sb[:, r * 512:r * 512 + 512]
            ve.wait_ge(mm.h, max(tr_mm[ci], pts_mm[r]))
            ev.inc(ve.tensor_copy(pt_slot, pt_ps[:, :]), 1)
            ptc_ev[r] = ev.n
            cpc = ev.n
            if ci + 1 < nj:
                emit_tr(ci + 1)
            te.wait_ge(ev.h, cpc)
            hofs = hl * 128 if layer == 1 else 0
            for jt in range(4):
                tg = b * 32 + jc * 4 + jt
                inst = te.matmul(e.ps_y[:, :],
                                 e.v_sb[:, tg * (e.KT * 128) + hofs:
                                        tg * (e.KT * 128) + hofs + 128],
                                 pt_slot[:, jt * 128:(jt + 1) * 128],
                                 start=first, stop=(ci == nj - 1 and jt == 3))
                first = False
            mm.inc(inst, 1)
            pts_mm[r] = mm.n
            pn_mm[0] = max(pn_mm[0], tr_mm[ci])
        ve.wait_ge(mm.h, mm.n)
        ev.inc(ve.tensor_copy(e.y_sb[:, i0b:i0b + 128], e.ps_y[:, :]), 1)
    if e.ytm_ap is None:
        ev.inc(ve.tensor_copy(e.fl_sb[:, 0:1], e.fl_sb[:, 1:2]), 1)
        gg.wait_ge(ev.h, ev.n)
        so.inc(gg.dma_start(out=e.y_h[hl * 128:(hl + 1) * 128, b * TB:(b + 1) * TB],
                            in_=e.y_sb[:, :]), 16)
    else:
        te.wait_ge(ev.h, ev.n)
        for tt in range(32):
            r = tt % 2
            pt_ps = (e.ps_p0, e.ps_p1)[r]
            slot = r * 128
            te.wait_ge(ev.h, ptc_ev[r])
            mm.inc(te.transpose(pt_ps[:, 0:128], e.y_sb[:, tt * 128:(tt + 1) * 128],
                                e.id_sb[:, :]), 1)
            ve.wait_ge(mm.h, mm.n)
            ve.wait_ge(so.h, yso[r])
            vw(e, ve.tensor_copy(e.ytp_sb[:, slot:slot + 128], pt_ps[:, 0:128]))
            ptc_ev[r] = ev.n
            gg.wait_ge(ev.h, ev.n)
            so.inc(gg.dma_start(
                out=e.ytm_ap[b * TB + tt * 128:b * TB + (tt + 1) * 128,
                             hl * 128:(hl + 1) * 128],
                in_=e.ytp_sb[:, slot:slot + 128]), 16)
            yso[r] = so.n


def host_attn_inputs(layer, inp):
    x = np.asarray(inp["L0_attn_x"], dtype=np.float32).reshape(T, D)
    masks_np, _ = build_masks(layer)
    masks2 = np.ascontiguousarray(masks_np.reshape(-1, 512))
    maps = []
    for c in range(W):
        m = {"masks": masks2,
             "x_own": np.ascontiguousarray(x[c * TOK:(c + 1) * TOK].T)}
        if layer == 0:
            kv = c // 2
            m["wq"] = np.ascontiguousarray(np.asarray(inp["L0_attn_Wq"])[256 * c:256 * (c + 1)].T)
            m["wk"] = np.ascontiguousarray(np.asarray(inp["L0_attn_Wk"])[128 * kv:128 * (kv + 1)].T)
            m["wv"] = np.ascontiguousarray(np.asarray(inp["L0_attn_Wv"])[128 * kv:128 * (kv + 1)].T)
        else:
            m["wq"] = np.ascontiguousarray(np.asarray(inp["L1_attn_Wq"])[:, 256 * c:256 * (c + 1)])
            m["wk"] = np.ascontiguousarray(np.asarray(inp["L1_attn_Wk"])[:, 256 * c:256 * (c + 1)])
            m["wv"] = np.ascontiguousarray(np.asarray(inp["L1_attn_Wv"])[:, 256 * c:256 * (c + 1)])
            m["lam"] = np.ascontiguousarray(
                np.asarray(inp["L1_attn_lambda_init"])[2 * c:2 * c + 2].reshape(1, 2))
        maps.append(m)
    return maps




def emit_ln(e, out_h):
    nc = e.nc
    sy, gg, ve, sc, te = nc.sync, nc.gpsimd, nc.vector, nc.scalar, nc.tensor
    mm, ev, ex, so = e.mm, e.ev, e.ex, e.so
    vw(e, ve.memset(e.o32_sb[:, :], 1.0))
    vw(e, ve.memset(e.st_sb[0:1, 0:2048], 0.0))
    te.wait_ge(ev.h, ev.n)
    for dt in range(16):
        sc.wait_ge(ev.h, ev.n)
        ex.inc(sc.activation(e.sq1_sb[:, 0:1024], e.xr_sb[:, dt * 1024:(dt + 1) * 1024],
                             AF.Square), 1)
        te.wait_ge(ex.h, ex.n)
        te.wait_ge(ev.h, ev.n)
        for th in range(2):
            inst = te.matmul((e.ps_c, e.ps_d)[th][0:1, 0:512], e.o32_sb[:, 0:1],
                             e.xr_sb[:, dt * 1024 + th * 512:dt * 1024 + th * 512 + 512],
                             start=True, stop=True)
            inst = te.matmul((e.ps_a, e.ps_b)[th][0:1, 0:512], e.o32_sb[:, 0:1],
                             e.sq1_sb[:, th * 512:th * 512 + 512],
                             start=True, stop=True)
        mm.inc(inst, 1)
        ve.wait_ge(mm.h, mm.n)
        vw(e, ve.tensor_tensor(e.st_sb[0:1, 0:512], e.st_sb[0:1, 0:512],
                               e.ps_c[0:1, 0:512], ALU.add))
        vw(e, ve.tensor_tensor(e.st_sb[0:1, 512:1024], e.st_sb[0:1, 512:1024],
                               e.ps_d[0:1, 0:512], ALU.add))
        vw(e, ve.tensor_tensor(e.st_sb[0:1, 1024:1536], e.st_sb[0:1, 1024:1536],
                               e.ps_a[0:1, 0:512], ALU.add))
        vw(e, ve.tensor_tensor(e.st_sb[0:1, 1536:2048], e.st_sb[0:1, 1536:2048],
                               e.ps_b[0:1, 0:512], ALU.add))
    vw(e, ve.tensor_scalar(e.st_sb[0:1, 0:1024], e.st_sb[0:1, 0:1024],
                           1.0 / D, None, ALU.mult))
    vw(e, ve.tensor_scalar(e.st_sb[0:1, 1024:2048], e.st_sb[0:1, 1024:2048],
                           1.0 / D, None, ALU.mult))
    vw(e, ve.tensor_tensor(e.st_sb[0:1, 2048:3072], e.st_sb[0:1, 0:1024],
                           e.st_sb[0:1, 0:1024], ALU.mult))
    vw(e, ve.tensor_tensor(e.st_sb[0:1, 1024:2048], e.st_sb[0:1, 1024:2048],
                           e.st_sb[0:1, 2048:3072], ALU.subtract))
    vw(e, ve.tensor_scalar(e.st_sb[0:1, 1024:2048], e.st_sb[0:1, 1024:2048],
                           1e-5, None, ALU.add))
    sc.wait_ge(ev.h, ev.n)
    ex.inc(sc.activation(e.st_sb[0:1, 2048:3072], e.st_sb[0:1, 1024:2048],
                         AF.Sqrt), 1)
    ve.wait_ge(ex.h, ex.n)
    vw(e, ve.reciprocal(e.st_sb[0:1, 2048:3072], e.st_sb[0:1, 2048:3072]))
    vw(e, ve.tensor_tensor(e.st_sb[0:1, 3072:4096], e.st_sb[0:1, 0:1024],
                           e.st_sb[0:1, 2048:3072], ALU.mult))
    vw(e, ve.tensor_scalar(e.st_sb[0:1, 3072:4096], e.st_sb[0:1, 3072:4096],
                           -1.0, None, ALU.mult))
    vw(e, ve.memset(e.o32_sb[0:1, :], 1.0))
    te.wait_ge(ev.h, ev.n)
    mm.inc(te.matmul(e.ps_ba[:, 0:512], e.o32_sb[0:1, :], e.st_sb[0:1, 2048:2560],
                     start=True, stop=True), 1)
    mm.inc(te.matmul(e.ps_bb[:, 0:512], e.o32_sb[0:1, :], e.st_sb[0:1, 2560:3072],
                     start=True, stop=True), 1)
    mm.inc(te.matmul(e.ps_bc[:, 0:512], e.o32_sb[0:1, :], e.st_sb[0:1, 3072:3584],
                     start=True, stop=True), 1)
    mm.inc(te.matmul(e.ps_bd[:, 0:512], e.o32_sb[0:1, :], e.st_sb[0:1, 3584:4096],
                     start=True, stop=True), 1)
    ve.wait_ge(mm.h, mm.n)
    for dt in range(16):
        ve.wait_ge(so.h, so.n)
        for th in range(2):
            Ab = (e.ps_ba, e.ps_bb)[th]
            Bb = (e.ps_bc, e.ps_bd)[th]
            cols = slice(dt * 1024 + th * 512, dt * 1024 + th * 512 + 512)
            vw(e, ve.tensor_tensor(e.xo_sb[:, th * 512:th * 512 + 512],
                                   e.xr_sb[:, cols], Ab[:, 0:512], ALU.mult))
            vw(e, ve.tensor_tensor(e.xo_sb[:, th * 512:th * 512 + 512],
                                   e.xo_sb[:, th * 512:th * 512 + 512],
                                   Bb[:, 0:512], ALU.add))
        gg.wait_ge(ev.h, ev.n)
        so.inc(gg.dma_start(out=out_h[dt * 128:(dt + 1) * 128, :],
                            in_=e.xo_sb[:, 0:1024]), 16)


def build_o_ln(fz=None):
    pfx = fz["pfx"] if fz else ""
    _NPFX[0] = pfx
    nc = fz["nc"] if fz else bass.Bass("TRN2", target_bir_lowering=False, num_devices=W)
    e = SimpleNamespace()
    e.nc = nc
    if fz:
        y_h = None
        yidx_h = nc.dram_tensor(pfx + "yidx", [128, 64], I32, kind="ExternalInput").ap()
        x_h = fz["x_ap"]
        xn_h = fz["out_ap"]
    else:
        y_h = nc.dram_tensor("y", [D, TOK], BF16, kind="ExternalInput").ap()
        x_h = nc.dram_tensor("x_own", [D, TOK], F32, kind="ExternalInput").ap()
        xn_h = nc.dram_tensor("xn", [D, TOK], F32, kind="ExternalOutput").ap()
    wo_h = nc.dram_tensor(pfx + "wo", [D, D], F32, kind="ExternalInput").ap()
    ones_i = nc.inline_tensor(np.ones((128, 128), dtype=bfdt), name=pfx + "ones_c").ap()
    ident_i = nc.inline_tensor(np.eye(128, dtype=bfdt), name=pfx + "ident_c").ap()

    es = contextlib.ExitStack()
    with es:
        wo_sb = sbufT(es, nc, "wo_sb", [128, 16 * D], BF16)
        yk_sb = sbufT(es, nc, "yk_sb", [128, 16 * TOK], BF16)
        e.xr_sb = sbufT(es, nc, "xr_sb", [128, 16 * TOK], F32)
        e.xo_sb = sbufT(es, nc, "xo_sb", [128, 2 * TOK], F32)
        e.xb1_sb = sbufT(es, nc, "xb1_sb", [128, TOK], F32)
        e.sq1_sb = sbufT(es, nc, "sq1_sb", [128, 2 * TOK], F32)
        e.st_sb = sbufT(es, nc, "st_sb", [1, 4096], F32)
        e.o32_sb = sbufT(es, nc, "o32_sb", [128, 128], F32)
        e.ones_sb = sbufT(es, nc, "ones_sb", [128, 128], BF16)
        e.fl_sb = sbufT(es, nc, "fl_sb", [128, 2], F32)
        e.ps_a = psumT(es, nc, "ps_a", [128, 512], F32)
        e.ps_b = psumT(es, nc, "ps_b", [128, 512], F32)
        e.ps_c = psumT(es, nc, "ps_c", [128, 512], F32)
        e.ps_d = psumT(es, nc, "ps_d", [128, 512], F32)
        e.ps_ba = psumT(es, nc, "ps_ba", [128, 512], F32)
        e.ps_bb = psumT(es, nc, "ps_bb", [128, 512], F32)
        e.ps_bc = psumT(es, nc, "ps_bc", [128, 512], F32)
        e.ps_bd = psumT(es, nc, "ps_bd", [128, 512], F32)

        if fz:
            sm = fz["sems"]
            ld, ag, xt = sm["ld"], sm["ag"], sm["xt"]
            e.mm = mm = sm["mm"]; e.ev = ev = sm["ev"]
            e.ex = ex = sm["ex"]; e.so = so = sm["so"]
        else:
            ld = Sem(es, nc, "ld")
            e.mm = mm = Sem(es, nc, "mm")
            e.ev = ev = Sem(es, nc, "ev")
            e.ex = ex = Sem(es, nc, "ex")
            e.so = so = Sem(es, nc, "so")
        sy, gg, ve, sc, te = nc.sync, nc.gpsimd, nc.vector, nc.scalar, nc.tensor

        ld.inc(gg.dma_start(out=wo_sb[:].rearrange("p (c o) -> p c o", c=16),
                            in_=wo_h.rearrange("(c p) o -> p c o", p=128)), 16)
        if not fz:
            ld.inc(gg.dma_start(out=yk_sb[:].rearrange("p (c t) -> p c t", c=16),
                                in_=y_h.rearrange("(c p) t -> p c t", p=128)), 16)
        ld.inc(sy.dma_start(out=e.xr_sb[:].rearrange("p (c t) -> p c t", c=16),
                            in_=x_h.rearrange("(c p) t -> p c t", p=128)), 16)
        ld.inc(sy.dma_start(out=e.ones_sb[:], in_=ones_i[:, :]), 16)
        if fz:
            id_sb = sbufT(es, nc, "id_sb", [128, 128], BF16)
            yidx_sb = sbufT(es, nc, "yidx_sb", [128, 64], I32)
            ygb_sb = sbufT(es, nc, "ygb_sb", [128, 256], BF16)
            ps_p0 = e.ps_a[:, 0:64].bitcast(BF16)
            ps_p1 = e.ps_b[:, 0:64].bitcast(BF16)
            ld.inc(sy.dma_start(out=id_sb[:], in_=ident_i[:, :]), 16)
            ld.inc(sy.dma_start(out=yidx_sb[:], in_=yidx_h[:, :]), 16)
            gg.wait_ge(e.so.h, e.so.n)
            ag.inc(gg.collective_compute("AllGather", ALU.bypass,
                                         replica_groups=[list(range(W))],
                                         ins=[fz["ytm"][:, :]],
                                         outs=[fz["ygath"][:, :]]), 1)
            gg.wait_ge(ag.h, ag.n)
            gg.wait_ge(ld.h, ld.n)
            for r in range(W):
                for tt in range(8):
                    col = r * 8 + tt
                    gg.wait_ge(ev.h, ev.n)
                    xt.inc(gg.indirect_dma_start(
                        ygb_sb[:, :], None, fz["ygath"][:, :],
                        IndirectOffsetOnAxis(yidx_sb[:, col:col + 1], 0)), 16)
                    te.wait_ge(xt.h, xt.n)
                    te.wait_ge(ev.h, ev.n)
                    for half in range(2):
                        pt = (ps_p0, ps_p1)[half]
                        mm.inc(te.transpose(pt, ygb_sb[:, half * 128:(half + 1) * 128],
                                            id_sb[:, :]), 1)
                    ve.wait_ge(mm.h, mm.n)
                    for half in range(2):
                        pt = (ps_p0, ps_p1)[half]
                        kc = 2 * r + half
                        ev.inc(ve.tensor_copy(
                            yk_sb[:, kc * TOK + tt * 128:kc * TOK + tt * 128 + 128],
                            pt), 1)

        te.wait_ge(ld.h, ld.n)
        te.wait_ge(ev.h, ev.n)
        for dt in range(16):
            for th in range(2):
                ps = (e.ps_a, e.ps_b)[th]
                te.wait_ge(ev.h, ev.n)
                for kc in range(16):
                    inst = te.matmul(ps[:, :],
                                     wo_sb[:, kc * D + dt * 128:kc * D + dt * 128 + 128],
                                     yk_sb[:, kc * TOK + th * 512:kc * TOK + th * 512 + 512],
                                     start=(kc == 0), stop=(kc == 15))
                mm.inc(inst, 1)
                ve.wait_ge(mm.h, mm.n)
                cols = slice(dt * 1024 + th * 512, dt * 1024 + th * 512 + 512)
                ev.inc(ve.tensor_tensor(e.xr_sb[:, cols], e.xr_sb[:, cols],
                                        ps[:, :], ALU.add), 1)
        emit_ln(e, xn_h)
        gg.wait_ge(so.h, so.n)
    return nc


def build_ffn(layer, debug_ffn=False, fz=None):
    NS = 1024 if layer == 0 else 2048
    NSUB = NS // 1024
    pfx = fz["pfx"] if fz else ""
    _NPFX[0] = pfx
    nc = fz["nc"] if fz else bass.Bass("TRN2", target_bir_lowering=False, num_devices=W)
    e = SimpleNamespace()
    e.nc = nc
    x_h = fz["x_ap"] if fz else nc.dram_tensor("x_own", [D, TOK], F32, kind="ExternalInput").ap()
    if layer == 0:
        wg_h = nc.dram_tensor(pfx + "wg", [D, NS], I32, kind="ExternalInput").ap()
        wu_h = nc.dram_tensor(pfx + "wu", [D, NS], I32, kind="ExternalInput").ap()
        wd_h = nc.dram_tensor(pfx + "wd", [NS, D], I32, kind="ExternalInput").ap()
        gsc_h = nc.dram_tensor(pfx + "gsc", [128, 8], F32, kind="ExternalInput").ap()
        usc_h = nc.dram_tensor(pfx + "usc", [128, 8], F32, kind="ExternalInput").ap()
        dsc_h = nc.dram_tensor(pfx + "dsc", [128, 16], F32, kind="ExternalInput").ap()
    elif fz:
        wg_h, wu_h, wd_h = fz["wg_ap"], fz["wu_ap"], fz["wd_ap"]
    else:
        wg_h = nc.dram_tensor("wg", [D, NS], BF16, kind="ExternalInput").ap()
        wu_h = nc.dram_tensor("wu", [D, NS], BF16, kind="ExternalInput").ap()
        wd_h = nc.dram_tensor("wd", [NS, D], BF16, kind="ExternalInput").ap()
    xn_h = fz["out_ap"] if fz else nc.dram_tensor("xn", [D, TOK], F32, kind="ExternalOutput").ap()
    if debug_ffn:
        drs_h = nc.dram_tensor("drs", [D, TOK], F32, kind="ExternalOutput").ap()
        dit_h = nc.dram_tensor("dit", [128, 8 * 512], BF16, kind="ExternalOutput").ap()
        dpt_h = nc.dram_tensor("dpt", [D, TOK], F32, kind="ExternalOutput").ap()
        dxg_h = nc.dram_tensor("dxg", [W * D, TOK], BF16, kind="ExternalOutput").ap()
    ones_i = nc.inline_tensor(np.ones((128, 128), dtype=bfdt), name=pfx + "ones_c").ap()

    if fz:
        agi, xg, part, rso = fz["agi"], fz["xg"], fz["part"], fz["rso"]
    else:
        agi = nc.dram_tensor("agi", [D, TOK], BF16, kind="Internal").ap()
        xg = nc.dram_tensor("xg", [W * D, TOK], BF16, kind="Internal").ap()
        part = nc.dram_tensor("part", [W * D, TOK], F32, kind="Internal").ap()
        rso = nc.dram_tensor("rso", [D, TOK], F32, kind="Internal").ap()

    es = contextlib.ExitStack()
    with es:
        e.xr_sb = sbufT(es, nc, "xr_sb", [128, 16 * TOK], F32)
        _wgu = e.xr_sb[:, :].bitcast(BF16)
        wg_sb = _wgu[:, 0:16 * 1024]
        wu_sb = _wgu[:, 16 * 1024:32 * 1024]
        wd_sb = sbufT(es, nc, "wd_sb", [128, 8 * D], BF16)
        wi_sb = sbufT(es, nc, "wi_sb", [128, 2 * 2048], I32)
        xk_sb = sbufT(es, nc, "xk_sb", [128, 2 * 16 * 512], BF16)
        it_sb = sbufT(es, nc, "it_sb", [128, 8 * 512], BF16)
        t1_sb = sbufT(es, nc, "t1_sb", [128, 2 * 512], BF16)
        t2_sb = sbufT(es, nc, "t2_sb", [128, 2 * 512], F32)
        po_sb = sbufT(es, nc, "po_sb", [128, 4 * 512], F32)
        e.xo_sb = sbufT(es, nc, "xo_sb", [128, 2 * TOK], F32)
        e.xb1_sb = sbufT(es, nc, "xb1_sb", [128, TOK], F32)
        e.sq1_sb = sbufT(es, nc, "sq1_sb", [128, 2 * TOK], F32)
        e.st_sb = sbufT(es, nc, "st_sb", [1, 4096], F32)
        e.o32_sb = sbufT(es, nc, "o32_sb", [128, 128], F32)
        e.ones_sb = sbufT(es, nc, "ones_sb", [128, 128], BF16)
        e.fl_sb = sbufT(es, nc, "fl_sb", [128, 2], F32)
        if layer == 0:
            gsc_sb = sbufT(es, nc, "gsc_sb", [128, 8], F32)
            usc_sb = sbufT(es, nc, "usc_sb", [128, 8], F32)
            dsc_sb = sbufT(es, nc, "dsc_sb", [128, 16], F32)

        e.ps_a = psumT(es, nc, "ps_a", [128, 512], F32)
        e.ps_b = psumT(es, nc, "ps_b", [128, 512], F32)
        e.ps_c = psumT(es, nc, "ps_c", [128, 512], F32)
        e.ps_d = psumT(es, nc, "ps_d", [128, 512], F32)
        e.ps_ba = psumT(es, nc, "ps_ba", [128, 512], F32)
        e.ps_bb = psumT(es, nc, "ps_bb", [128, 512], F32)
        e.ps_bc = psumT(es, nc, "ps_bc", [128, 512], F32)
        e.ps_bd = psumT(es, nc, "ps_bd", [128, 512], F32)

        if fz:
            sm = fz["sems"]
            ld, ag, wl, xt = sm["ld"], sm["ag"], sm["wl"], sm["xt"]
            e.mm = mm = sm["mm"]; e.ev = ev = sm["ev"]
            e.ex = ex = sm["ex"]; e.so = so = sm["so"]
        else:
            ld = Sem(es, nc, "ld")
            ag = Sem(es, nc, "ag")
            wl = Sem(es, nc, "wl")
            xt = Sem(es, nc, "xt")
            e.mm = mm = Sem(es, nc, "mm")
            e.ev = ev = Sem(es, nc, "ev")
            e.ex = ex = Sem(es, nc, "ex")
            e.so = so = Sem(es, nc, "so")
        sy, gg, ve, sc, te = nc.sync, nc.gpsimd, nc.vector, nc.scalar, nc.tensor

        ld.inc(gg.dma_start(out=agi[:, :], in_=x_h[:, :]), 16)
        ld.inc(sy.dma_start(out=e.ones_sb[:], in_=ones_i[:, :]), 16)
        if layer == 0:
            ld.inc(sy.dma_start(out=gsc_sb[:], in_=gsc_h[:, :]), 16)
            ld.inc(sy.dma_start(out=usc_sb[:], in_=usc_h[:, :]), 16)
            ld.inc(sy.dma_start(out=dsc_sb[:], in_=dsc_h[:, :]), 16)
        gg.wait_ge(ld.h, ld.n)
        ag.inc(gg.collective_compute("AllGather", ALU.bypass,
                                     replica_groups=[list(range(W))],
                                     ins=[agi[:, :]], outs=[xg[:, :]]), 1)
        sy.wait_ge(ag.h, ag.n)
        gg.wait_ge(ag.h, ag.n)

        def xg_win(tt):
            r, cs = tt // 2, 512 * (tt % 2)
            return (xg[r * D:(r + 1) * D, cs:cs + 512]
                    .rearrange("(c p) t -> p c t", p=128))

        xk_mm = [0, 0]
        xt_at = [0, 0]
        g_ex = [0, 0]
        g_ev = [0, 0]
        t_ev = [0, 0]
        o_ev = [0, 0]
        po_so = [0, 0, 0, 0]
        it_mm = [0]
        for sub in range(NSUB):
            if sub > 0:
                gg.wait_ge(mm.h, mm.n)
                ve.wait_ge(mm.h, mm.n)
            if layer == 0:
                def deq_load(w_h, dst_sb, nrows, width, iblk):
                    for ib in range(nrows):
                        h = ib % 2
                        if ib >= 2 or iblk > 0:
                            gg.wait_ge(ev.h, ev.n)
                        wl.inc(gg.dma_start(out=wi_sb[:, h * 2048:h * 2048 + width],
                                            in_=w_h[ib * 128:(ib + 1) * 128, :]), 16)
                        ve.wait_ge(wl.h, wl.n)
                        ev.inc(ve.tensor_scalar(dst_sb[:, ib * width:(ib + 1) * width],
                                                wi_sb[:, h * 2048:h * 2048 + width],
                                                -128.0, None, ALU.add), 1)
                deq_load(wg_h, wg_sb, 16, 1024, 0)
                deq_load(wu_h, wu_sb, 16, 1024, 1)
                deq_load(wd_h, wd_sb, 8, 2048, 2)
            else:
                wl.inc(gg.dma_start(out=wg_sb[:].rearrange("p (c n) -> p c n", c=16),
                                    in_=wg_h[:, sub * 1024:(sub + 1) * 1024]
                                    .rearrange("(c p) n -> p c n", p=128)), 16)
                wl.inc(gg.dma_start(out=wu_sb[:].rearrange("p (c n) -> p c n", c=16),
                                    in_=wu_h[:, sub * 1024:(sub + 1) * 1024]
                                    .rearrange("(c p) n -> p c n", p=128)), 16)
                wl.inc(gg.dma_start(out=wd_sb[:].rearrange("p (c n) -> p c n", c=8),
                                    in_=wd_h[sub * 1024:(sub + 1) * 1024, :]
                                    .rearrange("(c p) n -> p c n", p=128)), 16)
            te.wait_ge(wl.h, wl.n)
            te.wait_ge(ev.h, ev.n)

            for tt in range(16):
                xslot = tt % 2
                sy.wait_ge(mm.h, xk_mm[xslot])
                xt.inc(sy.dma_start(out=xk_sb[:, xslot * 8192:(xslot + 1) * 8192]
                                    .rearrange("p (c t) -> p c t", c=16),
                                    in_=xg_win(tt)), 16)
                xt_at[xslot] = xt.n
                xkv = xk_sb[:, xslot * 8192:(xslot + 1) * 8192]
                te.wait_ge(xt.h, xt_at[xslot])
                for nch in range(8):
                    r = nch % 2
                    psg = (e.ps_a, e.ps_b)[r]
                    psu = (e.ps_c, e.ps_d)[r]
                    te.wait_ge(ex.h, g_ex[r])
                    te.wait_ge(ev.h, g_ev[r])
                    for kc in range(16):
                        inst = te.matmul(psg[:, :],
                                         wg_sb[:, kc * 1024 + nch * 128:
                                               kc * 1024 + nch * 128 + 128],
                                         xkv[:, kc * 512:(kc + 1) * 512],
                                         start=(kc == 0), stop=(kc == 15))
                    for kc in range(16):
                        inst = te.matmul(psu[:, :],
                                         wu_sb[:, kc * 1024 + nch * 128:
                                               kc * 1024 + nch * 128 + 128],
                                         xkv[:, kc * 512:(kc + 1) * 512],
                                         start=(kc == 0), stop=(kc == 15))
                    mm.inc(inst, 1)
                    sc.wait_ge(mm.h, mm.n)
                    sc.wait_ge(ev.h, t_ev[r])
                    t1v = t1_sb[:, r * 512:(r + 1) * 512]
                    t2v = t2_sb[:, r * 512:(r + 1) * 512]
                    ve.wait_ge(mm.h, it_mm[0])
                    if layer == 0:
                        ex.inc(sc.activation(t1v, psg[:, :], AF.Silu,
                                             scale=gsc_sb[:, sub * 8 + nch:
                                                          sub * 8 + nch + 1]), 1)
                        ve.wait_ge(ex.h, ex.n)
                        ev.inc(ve.scalar_tensor_tensor(
                            it_sb[:, nch * 512:(nch + 1) * 512], psu[:, :],
                            usc_sb[:, sub * 8 + nch:sub * 8 + nch + 1],
                            t1v, ALU.mult, ALU.mult), 1)
                    else:
                        ex.inc(sc.activation(t1v, psg[:, :], AF.Square), 1)
                        ve.wait_ge(ex.h, ex.n)
                        vw(e, ve.tensor_scalar(t2v, t1v,
                                               0.7978845608028654 * 0.044715,
                                               0.7978845608028654, ALU.mult, ALU.add))
                        vw(e, ve.tensor_tensor(t2v, t2v, psg[:, :], ALU.mult))
                        sc.wait_ge(ev.h, ev.n)
                        ex.inc(sc.activation(t2v, t2v, AF.Tanh), 1)
                        ve.wait_ge(ex.h, ex.n)
                        vw(e, ve.scalar_tensor_tensor(t2v, t2v, 1.0,
                                                      psg[:, :], ALU.add, ALU.mult))
                        ev.inc(ve.scalar_tensor_tensor(
                            it_sb[:, nch * 512:(nch + 1) * 512], psu[:, :], 0.5,
                            t2v, ALU.mult, ALU.mult), 1)
                    g_ex[r] = ex.n
                    g_ev[r] = ev.n
                    t_ev[r] = ev.n
                for ht in range(16):
                    r = ht % 2
                    pso = (e.ps_ba, e.ps_bb)[r]
                    te.wait_ge(ev.h, o_ev[r])
                    if ht == 0:
                        te.wait_ge(ev.h, ev.n)
                    for nch in range(8):
                        inst = te.matmul(pso[:, :],
                                         wd_sb[:, nch * D + ht * 128:
                                               nch * D + ht * 128 + 128],
                                         it_sb[:, nch * 512:(nch + 1) * 512],
                                         start=(nch == 0), stop=(nch == 7))
                    mm.inc(inst, 1)
                    ve.wait_ge(mm.h, mm.n)
                    ps4 = ht % 4
                    ve.wait_ge(so.h, po_so[ps4])
                    pc = po_sb[:, ps4 * 512:ps4 * 512 + 512]
                    if layer == 0:
                        ev.inc(ve.tensor_scalar(pc, pso[:, :],
                                                dsc_sb[:, ht:ht + 1], None,
                                                ALU.mult), 1)
                    else:
                        ev.inc(ve.tensor_copy(pc, pso[:, :]), 1)
                    o_ev[r] = ev.n
                    gg.wait_ge(ev.h, ev.n)
                    rr, cs = tt // 2, 512 * (tt % 2)
                    dst = part[rr * D + ht * 128:rr * D + ht * 128 + 128, cs:cs + 512]
                    if sub == 0:
                        so.inc(gg.dma_start(out=dst, in_=pc), 16)
                    else:
                        so.inc(gg.dma_start(out=dst, in_=pc,
                                            accum_op=ALU.add), 16)
                    po_so[ps4] = so.n
                xk_mm[xslot] = mm.n
                it_mm[0] = mm.n
        gg.wait_ge(so.h, so.n)
        ag.inc(gg.collective_compute("ReduceScatter", ALU.add,
                                     replica_groups=[list(range(W))],
                                     ins=[part[:, :]], outs=[rso[:, :]]), 1)
        sy.wait_ge(ag.h, ag.n)
        if debug_ffn:
            gg.wait_ge(ag.h, ag.n)
            so.inc(gg.dma_start(out=drs_h[:, :], in_=rso[:, :]), 16)
        ld.inc(sy.dma_start(out=e.xr_sb[:].rearrange("p (c t) -> p c t", c=16),
                            in_=x_h.rearrange("(c p) t -> p c t", p=128)), 16)
        for dt in range(16):
            if dt > 0:
                sy.wait_ge(ev.h, ev.n)
            ld.inc(sy.dma_start(out=e.xo_sb[:, 0:1024],
                                in_=rso[dt * 128:(dt + 1) * 128, :]), 16)
            ve.wait_ge(ld.h, ld.n)
            ev.inc(ve.tensor_tensor(e.xr_sb[:, dt * 1024:(dt + 1) * 1024],
                                    e.xr_sb[:, dt * 1024:(dt + 1) * 1024],
                                    e.xo_sb[:, 0:1024], ALU.add), 1)
        emit_ln(e, xn_h)
        gg.wait_ge(so.h, so.n)
    return nc


def emit_decode(e, pk, dlo, dhi, tmp6):
    ve = e.nc.vector
    a, b, u, a2, b2, u2 = tmp6
    ev = e.ev
    ve.tensor_single_scalar(a[:, :], pk[:, :], 7, ALU.bitwise_and)
    ve.tensor_scalar(a2[:, :], pk[:, :], 4, 7, ALU.logical_shift_right,
                     ALU.bitwise_and)
    ve.tensor_scalar(b[:, :], pk[:, :], 8, 12, ALU.bitwise_and,
                     ALU.logical_shift_left)
    vw(e, ve.tensor_scalar(b2[:, :], pk[:, :], 128, 8, ALU.bitwise_and,
                           ALU.logical_shift_left))
    e.pk_done = ev.n
    ve.tensor_single_scalar(a[:, :], a[:, :], 252, ALU.add)
    ve.tensor_single_scalar(a2[:, :], a2[:, :], 252, ALU.add)
    ve.tensor_single_scalar(u[:, :], a[:, :], 253, ALU.is_equal)
    ve.tensor_single_scalar(u2[:, :], a2[:, :], 253, ALU.is_equal)
    ve.tensor_tensor(u[:, :], a[:, :], u[:, :], ALU.subtract)
    ve.tensor_tensor(u2[:, :], a2[:, :], u2[:, :], ALU.subtract)
    ve.tensor_scalar(a[:, :], a[:, :], 253, 64, ALU.is_ge, ALU.mult)
    ve.tensor_scalar(a2[:, :], a2[:, :], 253, 64, ALU.is_ge, ALU.mult)
    ve.tensor_tensor(u[:, :], u[:, :], a[:, :], ALU.mult)
    ve.tensor_tensor(u2[:, :], u2[:, :], a2[:, :], ALU.mult)
    ve.tensor_tensor(u[:, :], u[:, :], b[:, :], ALU.bitwise_or)
    ve.tensor_tensor(u2[:, :], u2[:, :], b2[:, :], ALU.bitwise_or)
    ev.inc(ve.tensor_copy(dlo[:, :].bitcast(mybir.dt.uint16), u[:, :]), 1)
    ev.inc(ve.tensor_copy(dhi[:, :].bitcast(mybir.dt.uint16), u2[:, :]), 1)


def build_decode(fz=None):
    pfx = fz["pfx"] if fz else ""
    _NPFX[0] = pfx
    nc = fz["nc"] if fz else bass.Bass("TRN2", target_bir_lowering=False, num_devices=W)
    e = SimpleNamespace()
    e.nc = nc
    ins, outs, scs = {}, {}, {}
    for nm in ("g", "u", "d"):
        ins[nm] = nc.dram_tensor(pfx + nm + "pk", [1024, 2048], I32, kind="ExternalInput").ap()
        scs[nm] = nc.dram_tensor(pfx + nm + "scT", [64, 2048], F32, kind="ExternalInput").ap()
        if fz:
            outs[nm] = fz[nm + "dec"]
        else:
            outs[nm] = nc.dram_tensor(nm + "dec", [2048, 2048], BF16, kind="ExternalOutput").ap()
    G_np = np.zeros((8, 128), dtype=np.float32)
    for a in range(8):
        G_np[a, a * 16:(a + 1) * 16] = 1
    G_i = nc.inline_tensor(G_np, name=pfx + "G_c").ap()

    es = contextlib.ExitStack()
    with es:
        pk_sb = sbufT(es, nc, "pk_sb", [128, 2 * 2048], I32)
        m_sb = sbufT(es, nc, "m_sb", [128, 2048], I32)
        t_sb = sbufT(es, nc, "t_sb", [128, 2048], I32)
        u_sb = sbufT(es, nc, "u_sb", [128, 2048], I32)
        m2_sb = sbufT(es, nc, "m2_sb", [128, 2048], I32)
        t2_sb = sbufT(es, nc, "t2_sb", [128, 2048], I32)
        u2_sb = sbufT(es, nc, "u2_sb", [128, 2048], I32)
        dlo_sb = sbufT(es, nc, "dlo_sb", [128, 2048], BF16)
        dhi_sb = sbufT(es, nc, "dhi_sb", [128, 2048], BF16)
        olo_sb = sbufT(es, nc, "olo_sb", [128, 2 * 2048], BF16)
        ohi_sb = sbufT(es, nc, "ohi_sb", [128, 2 * 2048], BF16)
        sc_sb = sbufT(es, nc, "sc_sb", [8, 2 * 2048], F32)
        G_sb = sbufT(es, nc, "G_sb", [8, 128], F32)
        ps_sc = psumT(es, nc, "ps_sc", [128, 512], F32)
        ps_s2 = psumT(es, nc, "ps_s2", [128, 512], F32)

        if fz:
            sm = fz["sems"]
            ld = sm["ld"]
            e.mm = mm = sm["mm"]; e.ev = ev = sm["ev"]; e.so = so = sm["so"]
        else:
            ld = Sem(es, nc, "ld")
            e.mm = mm = Sem(es, nc, "mm")
            e.ev = ev = Sem(es, nc, "ev")
            e.so = so = Sem(es, nc, "so")
        sy, gg, ve, te = nc.sync, nc.gpsimd, nc.vector, nc.tensor

        ld.inc(sy.dma_start(out=G_sb[:], in_=G_i[:, :]), 16)
        its = [(nm, rt) for nm in ("g", "u", "d") for rt in range(8)]
        pk_ev = [0, 0]
        sc_mm = [0, 0]
        ld_mark = [0, 0]
        out_so = [0, 0]
        psc_ev = [0, 0]

        def emit_loads(it):
            nm, rt = its[it]
            h = it % 2
            sy.wait_ge(ev.h, pk_ev[h])
            sy.wait_ge(mm.h, sc_mm[h])
            ld.inc(sy.dma_start(out=pk_sb[:, h * 2048:(h + 1) * 2048],
                                in_=ins[nm][rt * 128:(rt + 1) * 128, :]), 16)
            ld.inc(sy.dma_start(out=sc_sb[0:8, h * 2048:(h + 1) * 2048],
                                in_=scs[nm][rt * 8:(rt + 1) * 8, :]), 16)
            ld_mark[h] = ld.n

        emit_loads(0)
        for it in range(24):
            nm, rt = its[it]
            h = it % 2
            if it + 1 < 24:
                emit_loads(it + 1)
            pk = pk_sb[:, h * 2048:(h + 1) * 2048]
            ve.wait_ge(ld.h, ld_mark[h])
            ve.wait_ge(so.h, out_so[h])
            emit_decode(e, pk, dlo_sb, dhi_sb,
                        (m_sb, t_sb, u_sb, m2_sb, t2_sb, u2_sb))
            pk_ev[h] = e.pk_done
            te.wait_ge(ld.h, ld_mark[h])
            for cb in range(4):
                psc = (ps_sc, ps_s2)[cb % 2]
                te.wait_ge(ev.h, psc_ev[cb % 2])
                mm.inc(te.matmul(psc[:, :], G_sb[0:8, :],
                                 sc_sb[0:8, h * 2048 + cb * 512:
                                       h * 2048 + cb * 512 + 512],
                                 start=True, stop=True), 1)
                ve.wait_ge(mm.h, mm.n)
                cbs = slice(h * 2048 + cb * 512, h * 2048 + (cb + 1) * 512)
                dbs = slice(cb * 512, (cb + 1) * 512)
                ev.inc(ve.tensor_tensor(olo_sb[:, cbs], dlo_sb[:, dbs],
                                        psc[:, :], ALU.mult), 1)
                ev.inc(ve.tensor_tensor(ohi_sb[:, cbs], dhi_sb[:, dbs],
                                        psc[:, :], ALU.mult), 1)
                psc_ev[cb % 2] = ev.n
            sc_mm[h] = mm.n
            gg.wait_ge(ev.h, ev.n)
            ool = olo_sb[:, h * 2048:(h + 1) * 2048]
            ooh = ohi_sb[:, h * 2048:(h + 1) * 2048]
            if nm == "d":
                so.inc(gg.dma_start(out=outs[nm][(2 * rt) * 128:(2 * rt + 1) * 128, :],
                                    in_=ool), 16)
                so.inc(gg.dma_start(out=outs[nm][(2 * rt + 1) * 128:(2 * rt + 2) * 128, :],
                                    in_=ooh), 16)
            else:
                so.inc(gg.dma_start(out=outs[nm][256 * rt:256 * rt + 256:2, :],
                                    in_=ool), 16)
                so.inc(gg.dma_start(out=outs[nm][256 * rt + 1:256 * rt + 256:2, :],
                                    in_=ooh), 16)
            out_so[h] = so.n
        gg.wait_ge(so.h, so.n)
    return nc


def l1_perm(c):
    perm = np.zeros(2048, dtype=np.int64)
    for ci in range(16):
        for p in range(128):
            perm[ci * 128 + p] = 2048 * c + 256 * (ci // 2) + 2 * p + (ci % 2)
    return perm


def host_decode_inputs(inp):
    gp = np.asarray(inp["L1_ffn_gate_W_packed"])
    up = np.asarray(inp["L1_ffn_up_W_packed"])
    dp = np.asarray(inp["L1_ffn_down_W_packed"])
    gs = np.asarray(inp["L1_ffn_gate_scales"])
    us = np.asarray(inp["L1_ffn_up_scales"])
    ds = np.asarray(inp["L1_ffn_down_scales"])
    gpT = gp.T
    upT = up.T
    dpT = dp.T
    maps = []
    for c in range(W):
        perm = l1_perm(c)
        m = {
            "gpk": np.ascontiguousarray(gpT[:, perm]),
            "upk": np.ascontiguousarray(upT[:, perm]),
            "dpk": np.ascontiguousarray(dpT[1024 * c:1024 * (c + 1), :]),
            "gscT": np.ascontiguousarray(gs[perm, :].T),
            "uscT": np.ascontiguousarray(us[perm, :].T),
            "dscT": np.ascontiguousarray(ds[:, 64 * c:64 * (c + 1)].T),
        }
        maps.append(m)
    return maps


def host_oln_inputs(layer, inp, xcur, ys):
    x = np.asarray(xcur, dtype=np.float32).reshape(T, D)
    wo = np.asarray(inp["L0_attn_Wo"]).T if layer == 0 else np.asarray(inp["L1_attn_Wo"])
    wo = np.ascontiguousarray(wo.astype(np.float32))
    maps = []
    for c in range(W):
        yfm = np.ascontiguousarray(
            np.concatenate([np.asarray(ys[r])[:, c * TOK:(c + 1) * TOK]
                            for r in range(W)], axis=0))
        maps.append({"y": yfm, "wo": wo,
                     "x_own": np.ascontiguousarray(x[c * TOK:(c + 1) * TOK].T)})
    return maps


def host_ffn_inputs(layer, inp, xcur, decoded=None):
    x = np.asarray(xcur, dtype=np.float32).reshape(T, D)
    maps = []
    for c in range(W):
        m = {"x_own": np.ascontiguousarray(x[c * TOK:(c + 1) * TOK].T)}
        if layer == 0:
            s = slice(1024 * c, 1024 * (c + 1))
            m["wg"] = np.ascontiguousarray(np.asarray(inp["L0_ffn_gate_w"])[s].T)
            m["wu"] = np.ascontiguousarray(np.asarray(inp["L0_ffn_up_w"])[s].T)
            m["wd"] = np.ascontiguousarray(np.asarray(inp["L0_ffn_down_w"])[:, s].T)
            m["gsc"] = np.ascontiguousarray(
                np.asarray(inp["L0_ffn_gate_scale"])[s].reshape(8, 128).T)
            m["usc"] = np.ascontiguousarray(
                np.asarray(inp["L0_ffn_up_scale"])[s].reshape(8, 128).T)
            m["dsc"] = np.ascontiguousarray(
                np.asarray(inp["L0_ffn_down_scale"]).reshape(16, 128).T)
        else:
            m["wg"] = decoded[c]["gdec"]
            m["wu"] = decoded[c]["udec"]
            m["wd"] = decoded[c]["ddec"]
        maps.append(m)
    return maps




def build_fused():
    nc = bass.Bass("TRN2", target_bir_lowering=False, num_devices=W)
    es_top = contextlib.ExitStack()
    nc._fused_es = es_top
    sems = {nm: Sem(es_top, nc, nm) for nm in
            ("ld", "cv", "ag", "xt", "wl", "mm", "ev", "ex", "so")}

    x0 = nc.dram_tensor("x_own", [D, TOK], F32, kind="ExternalInput").ap()
    out_h = nc.dram_tensor("xn", [D, TOK], F32, kind="ExternalOutput").ap()
    agi = nc.dram_tensor("agi", [D, TOK], BF16, kind="Internal").ap()
    xg = nc.dram_tensor("xg", [W * D, TOK], BF16, kind="Internal").ap()
    part = nc.dram_tensor("part", [W * D, TOK], F32, kind="Internal").ap()
    rso = nc.dram_tensor("rso", [D, TOK], F32, kind="Internal").ap()
    ytm = nc.dram_tensor("ytm", [T, 256], BF16, kind="Internal").ap()
    ygath = nc.dram_tensor("ygath", [W * T, 256], BF16, kind="Internal").ap()
    x1 = nc.dram_tensor("x1", [D, TOK], F32, kind="Internal").ap()
    x2 = nc.dram_tensor("x2", [D, TOK], F32, kind="Internal").ap()
    x3 = nc.dram_tensor("x3", [D, TOK], F32, kind="Internal").ap()
    gdec = nc.dram_tensor("gdec", [2048, 2048], BF16, kind="Internal").ap()
    udec = nc.dram_tensor("udec", [2048, 2048], BF16, kind="Internal").ap()
    ddec = nc.dram_tensor("ddec", [2048, 2048], BF16, kind="Internal").ap()

    base = dict(nc=nc, sems=sems, agi=agi, xg=xg, part=part, rso=rso,
                ytm=ytm, ygath=ygath)

    with nc.named_scope("dec"):
        build_decode(fz={**base, "pfx": "dc_", "gdec": gdec, "udec": udec, "ddec": ddec})
        nc.all_engine_barrier()
    with nc.named_scope("a0"):
        build_attn(0, fz={**base, "pfx": "a0_", "x_ap": x0})
        nc.all_engine_barrier()
    with nc.named_scope("o0"):
        build_o_ln(fz={**base, "pfx": "o0_", "x_ap": x0, "out_ap": x1})
        nc.all_engine_barrier()
    with nc.named_scope("f0"):
        build_ffn(0, fz={**base, "pfx": "f0_", "x_ap": x1, "out_ap": x2})
        nc.all_engine_barrier()
    with nc.named_scope("a1"):
        build_attn(1, fz={**base, "pfx": "a1_", "x_ap": x2})
        nc.all_engine_barrier()
    with nc.named_scope("o1"):
        build_o_ln(fz={**base, "pfx": "o1_", "x_ap": x2, "out_ap": x3})
        nc.all_engine_barrier()
    with nc.named_scope("f1"):
        build_ffn(1, fz={**base, "pfx": "f1_", "x_ap": x3,
                         "wg_ap": gdec, "wu_ap": udec, "wd_ap": ddec,
                         "out_ap": out_h})
    es_top.close()
    return nc


def host_fused_inputs(inp):
    x = np.asarray(inp["L0_attn_x"], dtype=np.float32).reshape(T, D)
    a0 = host_attn_inputs(0, inp)
    a1 = host_attn_inputs(1, inp)
    dc = host_decode_inputs(inp)
    f0 = host_ffn_inputs(0, inp, x)
    wo0 = np.ascontiguousarray(np.asarray(inp["L0_attn_Wo"]).T.astype(np.float32))
    wo1 = np.ascontiguousarray(np.asarray(inp["L1_attn_Wo"]).astype(np.float32))
    maps = []
    for c in range(W):
        yidx = np.zeros((128, 64), dtype=np.int32)
        for r in range(W):
            for tt in range(8):
                yidx[:, r * 8 + tt] = r * T + c * TOK + tt * 128 + np.arange(128)
        m = {"x_own": np.ascontiguousarray(x[c * TOK:(c + 1) * TOK].T)}
        for k, v in a0[c].items():
            if k != "x_own":
                m["a0_" + k] = v
        for k, v in a1[c].items():
            if k != "x_own":
                m["a1_" + k] = v
        for k, v in dc[c].items():
            m["dc_" + k] = v
        for k, v in f0[c].items():
            if k != "x_own":
                m["f0_" + k] = v
        m["o0_wo"] = wo0
        m["o1_wo"] = wo1
        m["o0_yidx"] = yidx
        m["o1_yidx"] = yidx
        maps.append(m)
    return maps


def kernel_fused(**inputs):
    nc = build_fused()
    maps = host_fused_inputs(inputs)
    res = run_bass_kernel_spmd(nc, maps, core_ids=list(range(W)))
    x4 = _gather_xn(res.results)
    return np.ascontiguousarray(x4.reshape(B, TB, D).astype(np.float32))


def _run(nc, maps):
    res = run_bass_kernel_spmd(nc, maps, core_ids=list(range(W)))
    return res.results


def _gather_xn(res):
    return np.concatenate([np.asarray(res[c]["xn"]).T for c in range(W)], axis=0)


def kernel_multi(**inputs):
    inp = inputs
    x0 = np.asarray(inp["L0_attn_x"], dtype=np.float32)

    res = _run(build_attn(0), host_attn_inputs(0, {**inp, "L0_attn_x": x0}))
    ys = [np.asarray(res[c]["y"]) for c in range(W)]
    res = _run(build_o_ln(), host_oln_inputs(0, inp, x0, ys))
    x1 = _gather_xn(res)

    res = _run(build_ffn(0), host_ffn_inputs(0, inp, x1))
    x2 = _gather_xn(res)

    res = _run(build_attn(1), host_attn_inputs(1, {**inp, "L0_attn_x": x2}))
    ys = [np.asarray(res[c]["y"]) for c in range(W)]
    res = _run(build_o_ln(), host_oln_inputs(1, inp, x2, ys))
    x3 = _gather_xn(res)

    res = _run(build_decode(), host_decode_inputs(inp))
    decoded = [{k: np.asarray(res[c][k]) for k in ("gdec", "udec", "ddec")}
               for c in range(W)]
    res = _run(build_ffn(1), host_ffn_inputs(1, inp, x3, decoded))
    x4 = _gather_xn(res)

    return np.ascontiguousarray(x4.reshape(B, TB, D).astype(np.float32))


def kernel(**inputs):
    return kernel_fused(**inputs)


# version 12
# speedup vs baseline: 1.6142x; 1.0000x over previous
import math
import contextlib
from types import SimpleNamespace
import numpy as np
import ml_dtypes
import concourse.bass as bass
import concourse.mybir as mybir
from concourse.bass_core_rust import IndirectOffsetOnAxis
from concourse.bass_utils import run_bass_kernel_spmd

F32 = mybir.dt.float32
BF16 = mybir.dt.bfloat16
I32 = mybir.dt.int32
AF = mybir.ActivationFunctionType
ALU = mybir.AluOpType
AX = mybir.AxisListType

W = 8
B = 2
TB = 4096
T = B * TB
D = 2048
TOK = T // W
NEG = -1.0e30
bfdt = ml_dtypes.bfloat16


class Sem:
    def __init__(self, es, nc, name):
        self.h = es.enter_context(nc.semaphore(name))
        self.n = 0

    def inc(self, inst, k):
        inst.then_inc(self.h, k)
        self.n += k
        return self.n


def vw(e, inst):
    e.ev.inc(inst, 1)
    e.nc.vector.wait_ge(e.ev.h, e.ev.n)


_NPFX = [""]


def sbufT(es, nc, name, shape, dtype):
    return es.enter_context(nc.sbuf_tensor(_NPFX[0] + name, shape, dtype))


def psumT(es, nc, name, shape, dtype=F32):
    return es.enter_context(nc.psum_tensor(_NPFX[0] + name, shape, dtype))


def build_masks(layer):
    p = np.arange(128)[:, None]
    f = np.arange(512)[None, :]
    tiles, idx = [], {}

    def add(key, allow):
        idx[key] = len(tiles)
        allow = np.broadcast_to(allow, (128, 512))
        tiles.append(np.where(allow, np.float32(0.0), np.float32(NEG)))

    for o in (0, 128, 256, 384):
        add(("c", o), f <= p + o)
    if layer == 0:
        for o in (1, 129, 257, 385):
            add(("w", o), f >= p + o)
            add(("ws", o), (f >= p + o) | (f < 4))
            add(("wp", o - 512), f >= p + o - 512)
            add(("wsp", o - 512), (f >= p + o - 512) | (f < 4))
        add(("s",), f < 4)
    return np.stack(tiles).astype(np.float32), idx


def chunk_plan(layer, i0b):
    qc = i0b // 512
    if layer == 1:
        return [(jc, None) for jc in range(qc)] + [(qc, ("c", i0b % 512))]
    wlo = i0b - 1023
    wbc = max(0, wlo // 512)
    plan = []
    if wbc > 0:
        plan.append((0, ("s",)))
    for jc in range(wbc, qc + 1):
        if jc == qc:
            plan.append((jc, ("c", i0b % 512)))
            continue
        o = wlo - 512 * jc
        if o >= 1:
            plan.append((jc, ("ws" if jc == 0 else "w", o)))
        elif o > -511:
            plan.append((jc, ("wsp" if jc == 0 else "wp", o)))
        else:
            plan.append((jc, None))
    return plan


def build_attn(layer, debug_qkv=False, debug_attn=False, fz=None):
    NH = 2
    KVD = 128 if layer == 0 else 256
    KT = KVD // 128
    scale = 1.0 / math.sqrt(128 if layer == 0 else 64)
    masks_np, midx = build_masks(layer)
    NM = masks_np.shape[0]
    MAXC = 4 if layer == 0 else 8
    SLAB = max(MAXC, 8) * 512

    pfx = fz["pfx"] if fz else ""
    _NPFX[0] = pfx
    nc = fz["nc"] if fz else bass.Bass("TRN2", target_bir_lowering=False, num_devices=W)
    e = SimpleNamespace()
    e.nc, e.layer, e.scale, e.midx, e.MAXC, e.KT, e.NH = nc, layer, scale, midx, MAXC, KT, NH

    x_own = fz["x_ap"] if fz else nc.dram_tensor("x_own", [D, TOK], F32, kind="ExternalInput").ap()
    wq_h = nc.dram_tensor(pfx + "wq", [D, 256], F32, kind="ExternalInput").ap()
    wk_h = nc.dram_tensor(pfx + "wk", [D, KVD], F32, kind="ExternalInput").ap()
    wv_h = nc.dram_tensor(pfx + "wv", [D, KVD], F32, kind="ExternalInput").ap()
    masks_h = nc.dram_tensor(pfx + "masks", [NM * 128, 512], F32, kind="ExternalInput").ap()
    if layer == 1:
        lam_h = nc.dram_tensor(pfx + "lam", [1, 2], F32, kind="ExternalInput").ap()
    if fz:
        e.y_h = None
        e.ytm_ap = fz["ytm"]
    else:
        e.ytm_ap = None
        e.y_h = nc.dram_tensor("y", [NH * 128, T], BF16, kind="ExternalOutput").ap()
    if debug_qkv:
        xgdb = nc.dram_tensor("xgdb", [W * D, TOK], BF16, kind="ExternalOutput").ap()
        agdb = nc.dram_tensor("agdb", [D, TOK], BF16, kind="ExternalOutput").ap()
        agdb2 = nc.dram_tensor("agdb2", [D, TOK], BF16, kind="ExternalOutput").ap()
        qdb = nc.dram_tensor("qdb", [256, T], BF16, kind="ExternalOutput").ap()
        kdb = nc.dram_tensor("kdb", [KVD, T], BF16, kind="ExternalOutput").ap()
        vdb = nc.dram_tensor("vdb", [T, KVD], BF16, kind="ExternalOutput").ap()

    if debug_attn:
        e.dbg_bd = nc.dram_tensor("dbg_bd", [128, 64], F32, kind="ExternalOutput").ap()
        e.dbg_nrm = nc.dram_tensor("dbg_nrm", [128, 64], F32, kind="ExternalOutput").ap()
        e.dbg_kmb = nc.dram_tensor("dbg_kmb", [128, 4], F32, kind="ExternalOutput").ap()
        e.dbg_p1 = nc.dram_tensor("dbg_p1", [128, 4096], BF16, kind="ExternalOutput").ap()
        e.dbg_acc = nc.dram_tensor("dbg_acc", [128, 16], F32, kind="ExternalOutput").ap()
        e.dbg_sml = nc.dram_tensor("dbg_sml", [128, 8], F32, kind="ExternalOutput").ap()
    e.debug_attn = debug_attn
    if fz:
        agi, xg = fz["agi"], fz["xg"]
    else:
        agi = nc.dram_tensor("agi", [D, TOK], BF16, kind="Internal").ap()
        xg = nc.dram_tensor("xg", [W * D, TOK], BF16, kind="Internal").ap()

    ones_i = nc.inline_tensor(np.ones((128, 128), dtype=bfdt), name=pfx + "ones_c").ap()
    ident_i = nc.inline_tensor(np.eye(128, dtype=bfdt), name=pfx + "ident_c").ap()

    es = contextlib.ExitStack()
    with es:
        wq_sb = sbufT(es, nc, "wq_sb", [128, 16 * 256], BF16)
        wk_sb = sbufT(es, nc, "wk_sb", [128, 16 * KVD], BF16)
        wv_sb = sbufT(es, nc, "wv_sb", [128, 16 * KVD], BF16)
        xk_sb = sbufT(es, nc, "xk_sb", [128, 16 * 512], BF16)
        e.q_sb = sbufT(es, nc, "q_sb", [128, NH * T], BF16)
        e.k_sb = sbufT(es, nc, "k_sb", [128, KT * T], BF16)
        e.v_sb = sbufT(es, nc, "v_sb", [128, (T // 128) * KVD], BF16)
        e.y_sb = sbufT(es, nc, "y_sb", [128, TB], BF16)
        e.p1f = sbufT(es, nc, "p1f", [128, MAXC * 512], F32)
        if layer == 1:
            e.p2f = sbufT(es, nc, "p2f", [128, MAXC * 512], F32)
        e.pn_sb = sbufT(es, nc, "pn_sb", [128, 4096], BF16)
        e.p1_sb = e.pn_sb
        e.pt_sb = sbufT(es, nc, "pt_sb", [128, 2 * 512], BF16)
        e.mask_sb = sbufT(es, nc, "mask_sb", [128, NM * 512], F32)
        e.ones_sb = sbufT(es, nc, "ones_sb", [128, 128], BF16)
        e.id_sb = sbufT(es, nc, "id_sb", [128, 128], BF16)
        e.bd_sb = sbufT(es, nc, "bd_sb", [128, 64], F32)
        e.nrm_sb = sbufT(es, nc, "nrm_sb", [128, 256], F32)
        e.acc_sb = sbufT(es, nc, "acc_sb", [128, 2 * MAXC], F32)
        e.sml_sb = sbufT(es, nc, "sml_sb", [128, 8], F32)
        e.km_sb = sbufT(es, nc, "km_sb", [1, 8], F32)
        e.kmb16 = sbufT(es, nc, "kmb16", [1, 8], BF16)
        e.kmb_sb = sbufT(es, nc, "kmb_sb", [128, 4], F32)
        e.o32_sb = sbufT(es, nc, "o32_sb", [128, 128], F32)
        e.lamb_sb = sbufT(es, nc, "lamb_sb", [128, 2], F32)
        e.fl_sb = sbufT(es, nc, "fl_sb", [128, 2], F32)
        e.ytp_sb = sbufT(es, nc, "ytp_sb", [128, 256], BF16)

        e.ps_a = psumT(es, nc, "ps_a", [128, 512], F32)
        e.ps_b = psumT(es, nc, "ps_b", [128, 512], F32)
        e.ps_c = psumT(es, nc, "ps_c", [128, 512], F32)
        e.ps_d = psumT(es, nc, "ps_d", [128, 512], F32)
        e.ps_p0 = psumT(es, nc, "ps_p0", [128, 512], BF16)
        e.ps_p1 = psumT(es, nc, "ps_p1", [128, 512], BF16)
        e.ps_y = psumT(es, nc, "ps_y", [128, 128], F32)
        e.ps_n = psumT(es, nc, "ps_n", [128, 64], F32)

        if fz:
            sm = fz["sems"]
            ld, cv, ag, xt = sm["ld"], sm["cv"], sm["ag"], sm["xt"]
            e.mm = mm = sm["mm"]; e.ev = ev = sm["ev"]
            e.ex = ex = sm["ex"]; e.so = so = sm["so"]
        else:
            ld = Sem(es, nc, "ld")
            cv = Sem(es, nc, "cv")
            ag = Sem(es, nc, "ag")
            xt = Sem(es, nc, "xt")
            e.mm = mm = Sem(es, nc, "mm")
            e.ev = ev = Sem(es, nc, "ev")
            e.ex = ex = Sem(es, nc, "ex")
            e.so = so = Sem(es, nc, "so")

        sy, gg, ve, sc, te = nc.sync, nc.gpsimd, nc.vector, nc.scalar, nc.tensor

        ld.inc(gg.dma_start(out=wq_sb[:].rearrange("p (c o) -> p c o", c=16), in_=wq_h.rearrange("(c p) o -> p c o", p=128)), 16)
        ld.inc(gg.dma_start(out=wk_sb[:].rearrange("p (c o) -> p c o", c=16), in_=wk_h.rearrange("(c p) o -> p c o", p=128)), 16)
        ld.inc(gg.dma_start(out=wv_sb[:].rearrange("p (c o) -> p c o", c=16), in_=wv_h.rearrange("(c p) o -> p c o", p=128)), 16)
        ld.inc(sy.dma_start(out=e.mask_sb[:].rearrange("p (m f) -> p m f", f=512),
                            in_=masks_h.rearrange("(m p) f -> p m f", p=128)), 16)
        ld.inc(sy.dma_start(out=e.ones_sb[:], in_=ones_i[:, :]), 16)
        ld.inc(sy.dma_start(out=e.id_sb[:], in_=ident_i[:, :]), 16)
        if layer == 1:
            ld.inc(sy.dma_start(out=e.km_sb[0:1, 0:2], in_=lam_h[:, :]), 16)
        cv.inc(gg.dma_start(out=agi[:, :], in_=x_own[:, :]), 16)
        if debug_qkv:
            sy.wait_ge(cv.h, cv.n)
            ag.inc(sy.dma_start(out=agdb[:, :], in_=agi[:, :]), 16)
            gg.wait_ge(ag.h, ag.n)
        gg.wait_ge(cv.h, cv.n)
        ag.inc(gg.collective_compute("AllGather", ALU.bypass,
                                     replica_groups=[list(range(W))],
                                     ins=[agi[:, :]], outs=[xg[:, :]]), 1)
        sy.wait_ge(ag.h, ag.n)

        def xg_win(tt):
            r, cs = tt // 2, 512 * (tt % 2)
            return (xg[r * D:(r + 1) * D, cs:cs + 512]
                    .rearrange("(c p) t -> p c t", p=128))

        pj_ev = [0, 0, 0, 0]
        for tt in range(16):
            if tt > 0:
                sy.wait_ge(ev.h, ev.n)
            xt.inc(sy.dma_start(out=xk_sb[:].rearrange("p (c t) -> p c t", c=16), in_=xg_win(tt)), 16)
            te.wait_ge(xt.h, xt.n)
            rot = 0
            for dst_sb, wsb, no in ((e.q_sb, wq_sb, 2), (e.k_sb, wk_sb, KT)):
                for oh in range(no):
                    ps = (e.ps_a, e.ps_b)[rot % 2]
                    te.wait_ge(ev.h, pj_ev[rot % 2])
                    rot += 1
                    for kc in range(16):
                        inst = te.matmul(ps[:, :],
                                         wsb[:, kc * (no * 128) + oh * 128:
                                             kc * (no * 128) + oh * 128 + 128],
                                         xk_sb[:, kc * 512:(kc + 1) * 512],
                                         start=(kc == 0), stop=(kc == 15))
                    mm.inc(inst, 1)
                    ve.wait_ge(mm.h, mm.n)
                    ev.inc(ve.tensor_copy(dst_sb[:, oh * T + tt * 512:
                                                 oh * T + tt * 512 + 512], ps[:, :]), 1)
                    pj_ev[(rot - 1) % 2] = ev.n
            for ts in range(4):
                ps = (e.ps_c, e.ps_d)[ts % 2]
                te.wait_ge(ev.h, pj_ev[2 + ts % 2])
                for kc in range(16):
                    inst = te.matmul(ps[:, 0:KVD],
                                     xk_sb[:, kc * 512 + ts * 128:
                                           kc * 512 + ts * 128 + 128],
                                     wv_sb[:, kc * KVD:(kc + 1) * KVD],
                                     start=(kc == 0), stop=(kc == 15))
                mm.inc(inst, 1)
                ve.wait_ge(mm.h, mm.n)
                tg = tt * 4 + ts
                ev.inc(ve.tensor_copy(e.v_sb[:, tg * KVD:(tg + 1) * KVD],
                                      ps[:, 0:KVD]), 1)
                pj_ev[2 + ts % 2] = ev.n

        if debug_qkv:
            ev.inc(ve.tensor_copy(e.fl_sb[:, 0:1], e.fl_sb[:, 1:2]), 1)
            gg.wait_ge(ev.h, ev.n)
            so.inc(gg.dma_start(out=xgdb[:, :], in_=xg[:, :]), 16)
            so.inc(gg.dma_start(out=agdb2[:, :], in_=agi[:, :]), 16)
            so.inc(gg.dma_start(out=qdb.rearrange("(h p) t -> p h t", p=128),
                                in_=e.q_sb[:, :].rearrange("p (h t) -> p h t", t=T)), 16)
            so.inc(gg.dma_start(out=kdb.rearrange("(h p) t -> p h t", p=128),
                                in_=e.k_sb[:, :].rearrange("p (h t) -> p h t", t=T)), 16)
            so.inc(gg.dma_start(out=vdb.rearrange("(g p) o -> p g o", p=128),
                                in_=e.v_sb[:, :].rearrange("p (g o) -> p g o", g=T // 128)), 16)
            gg.wait_ge(so.h, so.n)
            return nc

        if layer == 1:
            sc.wait_ge(ld.h, ld.n)
            ex.inc(sc.activation(e.km_sb[0:1, 2:4], e.km_sb[0:1, 0:2], AF.Sigmoid), 1)
            ve.wait_ge(ex.h, ex.n)
            vw(e, ve.memset(e.o32_sb[0:1, :], 1.0))
            te.wait_ge(ev.h, ev.n)
            mm.inc(te.matmul(e.ps_n[:, 0:2], e.o32_sb[0:1, :], e.km_sb[0:1, 2:4],
                             start=True, stop=True), 1)
            ve.wait_ge(mm.h, mm.n)
            vw(e, ve.tensor_copy(e.lamb_sb[:, 0:2], e.ps_n[:, 0:2]))

        for b in range(B):
            for hl in range(NH):
                emit_attn_bh(e, b, hl)

        gg.wait_ge(so.h, so.n)
    return nc


def emit_attn_bh(e, b, hl):
    nc, layer = e.nc, e.layer
    sy, gg, ve, sc, te = nc.sync, nc.gpsimd, nc.vector, nc.scalar, nc.tensor
    mm, ev, ex, so = e.mm, e.ev, e.ex, e.so
    qcol = hl * T + b * TB
    kcol = (hl * T + b * TB) if layer == 1 else b * TB
    halves = ((0,),) if layer == 0 else ((0,), (64,))
    nsets = len(halves)
    MAXC, scale = e.MAXC, e.scale

    sc.wait_ge(mm.h, mm.n)
    ve.wait_ge(so.h, so.n)
    sc.wait_ge(ev.h, ev.n)
    ex.inc(sc.activation(e.p1_sb[:, 0:TB], e.q_sb[:, qcol:qcol + TB], AF.Square), 1)
    te.wait_ge(ex.h, ex.n)
    for si, (base,) in enumerate(halves):
        kw = {} if base == 0 else {"tile_position": (64, 0)}
        npart = 128 // nsets
        for qt in range(32):
            inst = te.matmul(e.ps_n[:, si * 32 + qt:si * 32 + qt + 1],
                             e.p1_sb[base:base + npart, qt * 128:(qt + 1) * 128],
                             e.ones_sb[base:base + npart, 0:1],
                             start=True, stop=True, **kw)
        mm.inc(inst, 1)
    ve.wait_ge(mm.h, mm.n)
    vw(e, ve.tensor_copy(e.nrm_sb[:, 0:64], e.ps_n[:, 0:64]))
    sc.wait_ge(ev.h, ev.n)
    ex.inc(sc.activation(e.p1_sb[:, 0:TB], e.k_sb[:, kcol:kcol + TB], AF.Square), 1)
    te.wait_ge(ex.h, ex.n)
    te.wait_ge(ev.h, ev.n)
    for si, (base,) in enumerate(halves):
        kw = {} if base == 0 else {"tile_position": (64, 0)}
        npart = 128 // nsets
        for jt in range(32):
            inst = te.matmul(e.ps_n[:, si * 32 + jt:si * 32 + jt + 1],
                             e.p1_sb[base:base + npart, jt * 128:(jt + 1) * 128],
                             e.ones_sb[base:base + npart, 0:1],
                             start=True, stop=True, **kw)
        mm.inc(inst, 1)
    ve.wait_ge(mm.h, mm.n)
    vw(e, ve.tensor_copy(e.nrm_sb[:, 64:128], e.ps_n[:, 0:64]))
    for si in range(nsets):
        vw(e, ve.reduce_max(e.sml_sb[:, si:si + 1],
                            e.nrm_sb[:, 64 + si * 32:64 + si * 32 + 32], axis=AX.X))
    vw(e, ve.memset(e.p1_sb[:, 0:128], NEG))
    for si in range(nsets):
        vw(e, ve.tensor_copy(e.p1_sb[:, si * 64:si * 64 + 1],
                             e.sml_sb[:, si:si + 1]))
    te.wait_ge(ev.h, ev.n)
    mm.inc(te.transpose(e.ps_p0[:, 0:128], e.p1_sb[:, 0:128], e.id_sb[:, :]), 1)
    ve.wait_ge(mm.h, mm.n)
    for si in range(nsets):
        vw(e, ve.tensor_copy(e.nrm_sb[si * 64:si * 64 + 1, 128:256],
                             e.ps_p0[si * 64:si * 64 + 1, 0:128]))
        vw(e, ve.reduce_max(e.km_sb[0:1, 4 + si:5 + si],
                            e.nrm_sb[si * 64:si * 64 + 1, 128:256], axis=AX.X))
    sc.wait_ge(ev.h, ev.n)
    ex.inc(sc.activation(e.km_sb[0:1, 6:6 + nsets], e.km_sb[0:1, 4:4 + nsets], AF.Sqrt), 1)
    ve.wait_ge(ex.h, ex.n)
    vw(e, ve.tensor_copy(e.kmb16[0:1, 6:6 + nsets], e.km_sb[0:1, 6:6 + nsets]))
    te.wait_ge(ev.h, ev.n)
    mm.inc(te.matmul(e.ps_n[:, 0:nsets], e.ones_sb[0:1, :], e.kmb16[0:1, 6:6 + nsets],
                     start=True, stop=True), 1)
    ve.wait_ge(mm.h, mm.n)
    vw(e, ve.tensor_copy(e.kmb_sb[:, 0:nsets], e.ps_n[:, 0:nsets]))
    sc.wait_ge(ev.h, ev.n)
    ex.inc(sc.activation(e.nrm_sb[:, 0:64], e.nrm_sb[:, 0:64], AF.Sqrt), 1)
    ve.wait_ge(ex.h, ex.n)
    for si in range(nsets):
        vw(e, ve.tensor_scalar(e.bd_sb[:, si * 32:si * 32 + 32],
                               e.nrm_sb[:, si * 32:si * 32 + 32],
                               e.kmb_sb[:, si:si + 1], -scale * 1.01,
                               ALU.mult, ALU.mult))

    if e.debug_attn and b == 0 and hl == 0:
        ev.inc(ve.tensor_copy(e.fl_sb[:, 0:1], e.fl_sb[:, 1:2]), 1)
        gg.wait_ge(ev.h, ev.n)
        so.inc(gg.dma_start(out=e.dbg_bd[:, :], in_=e.bd_sb[:, :]), 16)
        so.inc(gg.dma_start(out=e.dbg_nrm[:, :], in_=e.nrm_sb[:, 0:64]), 16)
        so.inc(gg.dma_start(out=e.dbg_kmb[:, :], in_=e.kmb_sb[:, :]), 16)
    rot_ex = [0, 0]
    ptc_ev = [0, 0]
    pts_mm = [0, 0]
    yso = [0, 0]
    pn_mm = [0]
    for qt in range(32):
        i0b = qt * 128
        plan = chunk_plan(layer, i0b)
        nj = len(plan)
        for ci, (jc, mk) in enumerate(plan):
            s1p = (e.ps_a, e.ps_b)[ci % 2]
            s2p = (e.ps_c, e.ps_d)[ci % 2]
            pss = (s1p,) if layer == 0 else (s1p, s2p)
            te.wait_ge(ex.h, rot_ex[ci % 2])
            for si, (base,) in enumerate(halves):
                kw = {} if base == 0 else {"tile_position": (64, 0)}
                npart = 128 // nsets
                inst = te.matmul(pss[si][:, :],
                                 e.q_sb[base:base + npart, qcol + i0b:qcol + i0b + 128],
                                 e.k_sb[base:base + npart,
                                        kcol + jc * 512:kcol + jc * 512 + 512],
                                 start=True, stop=True, **kw)
                mm.inc(inst, 1)
            if mk is not None:
                mi = e.midx[mk]
                ve.wait_ge(mm.h, mm.n)
                for ps in pss:
                    ev.inc(ve.tensor_tensor(ps[:, :], ps[:, :],
                                            e.mask_sb[:, mi * 512:(mi + 1) * 512],
                                            ALU.add), 1)
            sc.wait_ge(mm.h, mm.n)
            sc.wait_ge(ev.h, ev.n)
            for si, ps in enumerate(pss):
                slab = (e.p1f, e.p2f if layer == 1 else None)[si]
                ex.inc(sc.activation(slab[:, ci * 512:(ci + 1) * 512], ps[:, :],
                                     AF.Exp,
                                     bias=e.bd_sb[:, si * 32 + qt:si * 32 + qt + 1],
                                     scale=scale,
                                     accum_out=e.acc_sb[:, si * MAXC + ci:
                                                        si * MAXC + ci + 1]), 1)
            rot_ex[ci % 2] = ex.n
        ve.wait_ge(ex.h, ex.n)
        ve.wait_ge(mm.h, pn_mm[0])
        vw(e, ve.reduce_sum(e.sml_sb[:, 0:1], e.acc_sb[:, 0:nj], axis=AX.X))
        vw(e, ve.reciprocal(e.sml_sb[:, 1:2], e.sml_sb[:, 0:1]))
        if layer == 0:
            ev.inc(ve.tensor_scalar(e.pn_sb[:, 0:nj * 512], e.p1f[:, 0:nj * 512],
                                    e.sml_sb[:, 1:2], None, ALU.mult), 1)
        else:
            vw(e, ve.reduce_sum(e.sml_sb[:, 2:3], e.acc_sb[:, MAXC:MAXC + nj],
                                axis=AX.X))
            vw(e, ve.reciprocal(e.sml_sb[:, 3:4], e.sml_sb[:, 2:3]))
            vw(e, ve.tensor_tensor(e.sml_sb[:, 4:5], e.sml_sb[:, 0:1],
                                   e.sml_sb[:, 3:4], ALU.mult))
            vw(e, ve.tensor_scalar(e.sml_sb[:, 4:5], e.sml_sb[:, 4:5],
                                   e.lamb_sb[:, hl:hl + 1], -1.0,
                                   ALU.mult, ALU.mult))
            ev.inc(ve.scalar_tensor_tensor(e.p2f[:, 0:nj * 512],
                                           e.p2f[:, 0:nj * 512],
                                           e.sml_sb[:, 4:5],
                                           e.p1f[:, 0:nj * 512],
                                           ALU.mult, ALU.add), 1)
            ve.wait_ge(ev.h, ev.n)
            ev.inc(ve.tensor_scalar(e.pn_sb[:, 0:nj * 512], e.p2f[:, 0:nj * 512],
                                    e.sml_sb[:, 1:2], None, ALU.mult), 1)
        if e.debug_attn and b == 0 and hl == 0 and qt == 2:
            ev.inc(ve.tensor_copy(e.fl_sb[:, 0:1], e.fl_sb[:, 1:2]), 1)
            gg.wait_ge(ev.h, ev.n)
            so.inc(gg.dma_start(out=e.dbg_p1[:, 0:nj * 512], in_=e.p1_sb[:, 0:nj * 512]), 16)
            so.inc(gg.dma_start(out=e.dbg_acc[:, 0:2 * e.MAXC], in_=e.acc_sb[:, :]), 16)
            so.inc(gg.dma_start(out=e.dbg_sml[:, :], in_=e.sml_sb[:, :]), 16)
            gg.wait_ge(so.h, so.n)
        first = True
        tr_mm = [0] * nj
        norm_ev = ev.n

        def emit_tr(cj):
            r1 = cj % 2
            pt_ps1 = (e.ps_p0, e.ps_p1)[r1]
            te.wait_ge(ev.h, max(ptc_ev[r1], norm_ev))
            for jt in range(4):
                inst = te.transpose(pt_ps1[:, jt * 128:(jt + 1) * 128],
                                    e.pn_sb[:, cj * 512 + jt * 128:
                                            cj * 512 + jt * 128 + 128],
                                    e.id_sb[:, :])
            e.mm.inc(inst, 1)
            tr_mm[cj] = e.mm.n

        emit_tr(0)
        for ci, (jc, mk) in enumerate(plan):
            r = ci % 2
            pt_ps = (e.ps_p0, e.ps_p1)[r]
            pt_slot = e.pt_sb[:, r * 512:r * 512 + 512]
            ve.wait_ge(mm.h, max(tr_mm[ci], pts_mm[r]))
            ev.inc(ve.tensor_copy(pt_slot, pt_ps[:, :]), 1)
            ptc_ev[r] = ev.n
            cpc = ev.n
            if ci + 1 < nj:
                emit_tr(ci + 1)
            te.wait_ge(ev.h, cpc)
            hofs = hl * 128 if layer == 1 else 0
            for jt in range(4):
                tg = b * 32 + jc * 4 + jt
                inst = te.matmul(e.ps_y[:, :],
                                 e.v_sb[:, tg * (e.KT * 128) + hofs:
                                        tg * (e.KT * 128) + hofs + 128],
                                 pt_slot[:, jt * 128:(jt + 1) * 128],
                                 start=first, stop=(ci == nj - 1 and jt == 3))
                first = False
            mm.inc(inst, 1)
            pts_mm[r] = mm.n
            pn_mm[0] = max(pn_mm[0], tr_mm[ci])
        ve.wait_ge(mm.h, mm.n)
        ev.inc(ve.tensor_copy(e.y_sb[:, i0b:i0b + 128], e.ps_y[:, :]), 1)
    if e.ytm_ap is None:
        ev.inc(ve.tensor_copy(e.fl_sb[:, 0:1], e.fl_sb[:, 1:2]), 1)
        gg.wait_ge(ev.h, ev.n)
        so.inc(gg.dma_start(out=e.y_h[hl * 128:(hl + 1) * 128, b * TB:(b + 1) * TB],
                            in_=e.y_sb[:, :]), 16)
    else:
        te.wait_ge(ev.h, ev.n)
        for tt in range(32):
            r = tt % 2
            pt_ps = (e.ps_p0, e.ps_p1)[r]
            slot = r * 128
            te.wait_ge(ev.h, ptc_ev[r])
            mm.inc(te.transpose(pt_ps[:, 0:128], e.y_sb[:, tt * 128:(tt + 1) * 128],
                                e.id_sb[:, :]), 1)
            ve.wait_ge(mm.h, mm.n)
            ve.wait_ge(so.h, yso[r])
            vw(e, ve.tensor_copy(e.ytp_sb[:, slot:slot + 128], pt_ps[:, 0:128]))
            ptc_ev[r] = ev.n
            gg.wait_ge(ev.h, ev.n)
            so.inc(gg.dma_start(
                out=e.ytm_ap[b * TB + tt * 128:b * TB + (tt + 1) * 128,
                             hl * 128:(hl + 1) * 128],
                in_=e.ytp_sb[:, slot:slot + 128]), 16)
            yso[r] = so.n


def host_attn_inputs(layer, inp):
    x = np.asarray(inp["L0_attn_x"], dtype=np.float32).reshape(T, D)
    masks_np, _ = build_masks(layer)
    masks2 = np.ascontiguousarray(masks_np.reshape(-1, 512))
    maps = []
    for c in range(W):
        m = {"masks": masks2,
             "x_own": np.ascontiguousarray(x[c * TOK:(c + 1) * TOK].T)}
        if layer == 0:
            kv = c // 2
            m["wq"] = np.ascontiguousarray(np.asarray(inp["L0_attn_Wq"])[256 * c:256 * (c + 1)].T)
            m["wk"] = np.ascontiguousarray(np.asarray(inp["L0_attn_Wk"])[128 * kv:128 * (kv + 1)].T)
            m["wv"] = np.ascontiguousarray(np.asarray(inp["L0_attn_Wv"])[128 * kv:128 * (kv + 1)].T)
        else:
            m["wq"] = np.ascontiguousarray(np.asarray(inp["L1_attn_Wq"])[:, 256 * c:256 * (c + 1)])
            m["wk"] = np.ascontiguousarray(np.asarray(inp["L1_attn_Wk"])[:, 256 * c:256 * (c + 1)])
            m["wv"] = np.ascontiguousarray(np.asarray(inp["L1_attn_Wv"])[:, 256 * c:256 * (c + 1)])
            m["lam"] = np.ascontiguousarray(
                np.asarray(inp["L1_attn_lambda_init"])[2 * c:2 * c + 2].reshape(1, 2))
        maps.append(m)
    return maps




def emit_ln(e, out_h):
    nc = e.nc
    sy, gg, ve, sc, te = nc.sync, nc.gpsimd, nc.vector, nc.scalar, nc.tensor
    mm, ev, ex, so = e.mm, e.ev, e.ex, e.so
    vw(e, ve.memset(e.o32_sb[:, :], 1.0))
    vw(e, ve.memset(e.st_sb[0:1, 0:2048], 0.0))
    te.wait_ge(ev.h, ev.n)
    for dt in range(16):
        sc.wait_ge(ev.h, ev.n)
        ex.inc(sc.activation(e.sq1_sb[:, 0:1024], e.xr_sb[:, dt * 1024:(dt + 1) * 1024],
                             AF.Square), 1)
        te.wait_ge(ex.h, ex.n)
        te.wait_ge(ev.h, ev.n)
        for th in range(2):
            inst = te.matmul((e.ps_c, e.ps_d)[th][0:1, 0:512], e.o32_sb[:, 0:1],
                             e.xr_sb[:, dt * 1024 + th * 512:dt * 1024 + th * 512 + 512],
                             start=True, stop=True)
            inst = te.matmul((e.ps_a, e.ps_b)[th][0:1, 0:512], e.o32_sb[:, 0:1],
                             e.sq1_sb[:, th * 512:th * 512 + 512],
                             start=True, stop=True)
        mm.inc(inst, 1)
        ve.wait_ge(mm.h, mm.n)
        vw(e, ve.tensor_tensor(e.st_sb[0:1, 0:512], e.st_sb[0:1, 0:512],
                               e.ps_c[0:1, 0:512], ALU.add))
        vw(e, ve.tensor_tensor(e.st_sb[0:1, 512:1024], e.st_sb[0:1, 512:1024],
                               e.ps_d[0:1, 0:512], ALU.add))
        vw(e, ve.tensor_tensor(e.st_sb[0:1, 1024:1536], e.st_sb[0:1, 1024:1536],
                               e.ps_a[0:1, 0:512], ALU.add))
        vw(e, ve.tensor_tensor(e.st_sb[0:1, 1536:2048], e.st_sb[0:1, 1536:2048],
                               e.ps_b[0:1, 0:512], ALU.add))
    vw(e, ve.tensor_scalar(e.st_sb[0:1, 0:1024], e.st_sb[0:1, 0:1024],
                           1.0 / D, None, ALU.mult))
    vw(e, ve.tensor_scalar(e.st_sb[0:1, 1024:2048], e.st_sb[0:1, 1024:2048],
                           1.0 / D, None, ALU.mult))
    vw(e, ve.tensor_tensor(e.st_sb[0:1, 2048:3072], e.st_sb[0:1, 0:1024],
                           e.st_sb[0:1, 0:1024], ALU.mult))
    vw(e, ve.tensor_tensor(e.st_sb[0:1, 1024:2048], e.st_sb[0:1, 1024:2048],
                           e.st_sb[0:1, 2048:3072], ALU.subtract))
    vw(e, ve.tensor_scalar(e.st_sb[0:1, 1024:2048], e.st_sb[0:1, 1024:2048],
                           1e-5, None, ALU.add))
    sc.wait_ge(ev.h, ev.n)
    ex.inc(sc.activation(e.st_sb[0:1, 2048:3072], e.st_sb[0:1, 1024:2048],
                         AF.Sqrt), 1)
    ve.wait_ge(ex.h, ex.n)
    vw(e, ve.reciprocal(e.st_sb[0:1, 2048:3072], e.st_sb[0:1, 2048:3072]))
    vw(e, ve.tensor_tensor(e.st_sb[0:1, 3072:4096], e.st_sb[0:1, 0:1024],
                           e.st_sb[0:1, 2048:3072], ALU.mult))
    vw(e, ve.tensor_scalar(e.st_sb[0:1, 3072:4096], e.st_sb[0:1, 3072:4096],
                           -1.0, None, ALU.mult))
    vw(e, ve.memset(e.o32_sb[0:1, :], 1.0))
    te.wait_ge(ev.h, ev.n)
    mm.inc(te.matmul(e.ps_ba[:, 0:512], e.o32_sb[0:1, :], e.st_sb[0:1, 2048:2560],
                     start=True, stop=True), 1)
    mm.inc(te.matmul(e.ps_bb[:, 0:512], e.o32_sb[0:1, :], e.st_sb[0:1, 2560:3072],
                     start=True, stop=True), 1)
    mm.inc(te.matmul(e.ps_bc[:, 0:512], e.o32_sb[0:1, :], e.st_sb[0:1, 3072:3584],
                     start=True, stop=True), 1)
    mm.inc(te.matmul(e.ps_bd[:, 0:512], e.o32_sb[0:1, :], e.st_sb[0:1, 3584:4096],
                     start=True, stop=True), 1)
    ve.wait_ge(mm.h, mm.n)
    for dt in range(16):
        ve.wait_ge(so.h, so.n)
        for th in range(2):
            Ab = (e.ps_ba, e.ps_bb)[th]
            Bb = (e.ps_bc, e.ps_bd)[th]
            cols = slice(dt * 1024 + th * 512, dt * 1024 + th * 512 + 512)
            vw(e, ve.tensor_tensor(e.xo_sb[:, th * 512:th * 512 + 512],
                                   e.xr_sb[:, cols], Ab[:, 0:512], ALU.mult))
            vw(e, ve.tensor_tensor(e.xo_sb[:, th * 512:th * 512 + 512],
                                   e.xo_sb[:, th * 512:th * 512 + 512],
                                   Bb[:, 0:512], ALU.add))
        gg.wait_ge(ev.h, ev.n)
        so.inc(gg.dma_start(out=out_h[dt * 128:(dt + 1) * 128, :],
                            in_=e.xo_sb[:, 0:1024]), 16)


def build_o_ln(fz=None):
    pfx = fz["pfx"] if fz else ""
    _NPFX[0] = pfx
    nc = fz["nc"] if fz else bass.Bass("TRN2", target_bir_lowering=False, num_devices=W)
    e = SimpleNamespace()
    e.nc = nc
    if fz:
        y_h = None
        yidx_h = nc.dram_tensor(pfx + "yidx", [128, 64], I32, kind="ExternalInput").ap()
        x_h = fz["x_ap"]
        xn_h = fz["out_ap"]
    else:
        y_h = nc.dram_tensor("y", [D, TOK], BF16, kind="ExternalInput").ap()
        x_h = nc.dram_tensor("x_own", [D, TOK], F32, kind="ExternalInput").ap()
        xn_h = nc.dram_tensor("xn", [D, TOK], F32, kind="ExternalOutput").ap()
    wo_h = nc.dram_tensor(pfx + "wo", [D, D], F32, kind="ExternalInput").ap()
    ones_i = nc.inline_tensor(np.ones((128, 128), dtype=bfdt), name=pfx + "ones_c").ap()
    ident_i = nc.inline_tensor(np.eye(128, dtype=bfdt), name=pfx + "ident_c").ap()

    es = contextlib.ExitStack()
    with es:
        wo_sb = sbufT(es, nc, "wo_sb", [128, 16 * D], BF16)
        yk_sb = sbufT(es, nc, "yk_sb", [128, 16 * TOK], BF16)
        e.xr_sb = sbufT(es, nc, "xr_sb", [128, 16 * TOK], F32)
        e.xo_sb = sbufT(es, nc, "xo_sb", [128, 2 * TOK], F32)
        e.xb1_sb = sbufT(es, nc, "xb1_sb", [128, TOK], F32)
        e.sq1_sb = sbufT(es, nc, "sq1_sb", [128, 2 * TOK], F32)
        e.st_sb = sbufT(es, nc, "st_sb", [1, 4096], F32)
        e.o32_sb = sbufT(es, nc, "o32_sb", [128, 128], F32)
        e.ones_sb = sbufT(es, nc, "ones_sb", [128, 128], BF16)
        e.fl_sb = sbufT(es, nc, "fl_sb", [128, 2], F32)
        e.ps_a = psumT(es, nc, "ps_a", [128, 512], F32)
        e.ps_b = psumT(es, nc, "ps_b", [128, 512], F32)
        e.ps_c = psumT(es, nc, "ps_c", [128, 512], F32)
        e.ps_d = psumT(es, nc, "ps_d", [128, 512], F32)
        e.ps_ba = psumT(es, nc, "ps_ba", [128, 512], F32)
        e.ps_bb = psumT(es, nc, "ps_bb", [128, 512], F32)
        e.ps_bc = psumT(es, nc, "ps_bc", [128, 512], F32)
        e.ps_bd = psumT(es, nc, "ps_bd", [128, 512], F32)

        if fz:
            sm = fz["sems"]
            ld, ag, xt = sm["ld"], sm["ag"], sm["xt"]
            e.mm = mm = sm["mm"]; e.ev = ev = sm["ev"]
            e.ex = ex = sm["ex"]; e.so = so = sm["so"]
        else:
            ld = Sem(es, nc, "ld")
            e.mm = mm = Sem(es, nc, "mm")
            e.ev = ev = Sem(es, nc, "ev")
            e.ex = ex = Sem(es, nc, "ex")
            e.so = so = Sem(es, nc, "so")
        sy, gg, ve, sc, te = nc.sync, nc.gpsimd, nc.vector, nc.scalar, nc.tensor

        ld.inc(gg.dma_start(out=wo_sb[:].rearrange("p (c o) -> p c o", c=16),
                            in_=wo_h.rearrange("(c p) o -> p c o", p=128)), 16)
        if not fz:
            ld.inc(gg.dma_start(out=yk_sb[:].rearrange("p (c t) -> p c t", c=16),
                                in_=y_h.rearrange("(c p) t -> p c t", p=128)), 16)
        ld.inc(sy.dma_start(out=e.xr_sb[:].rearrange("p (c t) -> p c t", c=16),
                            in_=x_h.rearrange("(c p) t -> p c t", p=128)), 16)
        ld.inc(sy.dma_start(out=e.ones_sb[:], in_=ones_i[:, :]), 16)
        if fz:
            id_sb = sbufT(es, nc, "id_sb", [128, 128], BF16)
            yidx_sb = sbufT(es, nc, "yidx_sb", [128, 64], I32)
            ygb_sb = sbufT(es, nc, "ygb_sb", [128, 256], BF16)
            ps_p0 = e.ps_a[:, 0:64].bitcast(BF16)
            ps_p1 = e.ps_b[:, 0:64].bitcast(BF16)
            ld.inc(sy.dma_start(out=id_sb[:], in_=ident_i[:, :]), 16)
            ld.inc(sy.dma_start(out=yidx_sb[:], in_=yidx_h[:, :]), 16)
            gg.wait_ge(e.so.h, e.so.n)
            ag.inc(gg.collective_compute("AllGather", ALU.bypass,
                                         replica_groups=[list(range(W))],
                                         ins=[fz["ytm"][:, :]],
                                         outs=[fz["ygath"][:, :]]), 1)
            gg.wait_ge(ag.h, ag.n)
            gg.wait_ge(ld.h, ld.n)
            for r in range(W):
                for tt in range(8):
                    col = r * 8 + tt
                    gg.wait_ge(ev.h, ev.n)
                    xt.inc(gg.indirect_dma_start(
                        ygb_sb[:, :], None, fz["ygath"][:, :],
                        IndirectOffsetOnAxis(yidx_sb[:, col:col + 1], 0)), 16)
                    te.wait_ge(xt.h, xt.n)
                    te.wait_ge(ev.h, ev.n)
                    for half in range(2):
                        pt = (ps_p0, ps_p1)[half]
                        mm.inc(te.transpose(pt, ygb_sb[:, half * 128:(half + 1) * 128],
                                            id_sb[:, :]), 1)
                    ve.wait_ge(mm.h, mm.n)
                    for half in range(2):
                        pt = (ps_p0, ps_p1)[half]
                        kc = 2 * r + half
                        ev.inc(ve.tensor_copy(
                            yk_sb[:, kc * TOK + tt * 128:kc * TOK + tt * 128 + 128],
                            pt), 1)

        te.wait_ge(ld.h, ld.n)
        te.wait_ge(ev.h, ev.n)
        for dt in range(16):
            for th in range(2):
                ps = (e.ps_a, e.ps_b)[th]
                te.wait_ge(ev.h, ev.n)
                for kc in range(16):
                    inst = te.matmul(ps[:, :],
                                     wo_sb[:, kc * D + dt * 128:kc * D + dt * 128 + 128],
                                     yk_sb[:, kc * TOK + th * 512:kc * TOK + th * 512 + 512],
                                     start=(kc == 0), stop=(kc == 15))
                mm.inc(inst, 1)
                ve.wait_ge(mm.h, mm.n)
                cols = slice(dt * 1024 + th * 512, dt * 1024 + th * 512 + 512)
                ev.inc(ve.tensor_tensor(e.xr_sb[:, cols], e.xr_sb[:, cols],
                                        ps[:, :], ALU.add), 1)
        emit_ln(e, xn_h)
        gg.wait_ge(so.h, so.n)
    return nc


def build_ffn(layer, debug_ffn=False, fz=None):
    NS = 1024 if layer == 0 else 2048
    NSUB = NS // 1024
    pfx = fz["pfx"] if fz else ""
    _NPFX[0] = pfx
    nc = fz["nc"] if fz else bass.Bass("TRN2", target_bir_lowering=False, num_devices=W)
    e = SimpleNamespace()
    e.nc = nc
    x_h = fz["x_ap"] if fz else nc.dram_tensor("x_own", [D, TOK], F32, kind="ExternalInput").ap()
    if layer == 0:
        wg_h = nc.dram_tensor(pfx + "wg", [D, NS], I32, kind="ExternalInput").ap()
        wu_h = nc.dram_tensor(pfx + "wu", [D, NS], I32, kind="ExternalInput").ap()
        wd_h = nc.dram_tensor(pfx + "wd", [NS, D], I32, kind="ExternalInput").ap()
        gsc_h = nc.dram_tensor(pfx + "gsc", [128, 8], F32, kind="ExternalInput").ap()
        usc_h = nc.dram_tensor(pfx + "usc", [128, 8], F32, kind="ExternalInput").ap()
        dsc_h = nc.dram_tensor(pfx + "dsc", [128, 16], F32, kind="ExternalInput").ap()
    elif fz:
        wg_h, wu_h, wd_h = fz["wg_ap"], fz["wu_ap"], fz["wd_ap"]
    else:
        wg_h = nc.dram_tensor("wg", [D, NS], BF16, kind="ExternalInput").ap()
        wu_h = nc.dram_tensor("wu", [D, NS], BF16, kind="ExternalInput").ap()
        wd_h = nc.dram_tensor("wd", [NS, D], BF16, kind="ExternalInput").ap()
    xn_h = fz["out_ap"] if fz else nc.dram_tensor("xn", [D, TOK], F32, kind="ExternalOutput").ap()
    if debug_ffn:
        drs_h = nc.dram_tensor("drs", [D, TOK], F32, kind="ExternalOutput").ap()
        dit_h = nc.dram_tensor("dit", [128, 8 * 512], BF16, kind="ExternalOutput").ap()
        dpt_h = nc.dram_tensor("dpt", [D, TOK], F32, kind="ExternalOutput").ap()
        dxg_h = nc.dram_tensor("dxg", [W * D, TOK], BF16, kind="ExternalOutput").ap()
    ones_i = nc.inline_tensor(np.ones((128, 128), dtype=bfdt), name=pfx + "ones_c").ap()

    if fz:
        agi, xg, part, rso = fz["agi"], fz["xg"], fz["part"], fz["rso"]
    else:
        agi = nc.dram_tensor("agi", [D, TOK], BF16, kind="Internal").ap()
        xg = nc.dram_tensor("xg", [W * D, TOK], BF16, kind="Internal").ap()
        part = nc.dram_tensor("part", [W * D, TOK], F32, kind="Internal").ap()
        rso = nc.dram_tensor("rso", [D, TOK], F32, kind="Internal").ap()

    es = contextlib.ExitStack()
    with es:
        e.xr_sb = sbufT(es, nc, "xr_sb", [128, 16 * TOK], F32)
        _wgu = e.xr_sb[:, :].bitcast(BF16)
        wg_sb = _wgu[:, 0:16 * 1024]
        wu_sb = _wgu[:, 16 * 1024:32 * 1024]
        wd_sb = sbufT(es, nc, "wd_sb", [128, 8 * D], BF16)
        wi_sb = sbufT(es, nc, "wi_sb", [128, 2 * 2048], I32)
        xk_sb = sbufT(es, nc, "xk_sb", [128, 2 * 16 * 512], BF16)
        it_sb = sbufT(es, nc, "it_sb", [128, 8 * 512], BF16)
        t1_sb = sbufT(es, nc, "t1_sb", [128, 2 * 512], BF16)
        t2_sb = sbufT(es, nc, "t2_sb", [128, 2 * 512], F32)
        po_sb = sbufT(es, nc, "po_sb", [128, 4 * 512], F32)
        e.xo_sb = sbufT(es, nc, "xo_sb", [128, 2 * TOK], F32)
        e.xb1_sb = sbufT(es, nc, "xb1_sb", [128, TOK], F32)
        e.sq1_sb = sbufT(es, nc, "sq1_sb", [128, 2 * TOK], F32)
        e.st_sb = sbufT(es, nc, "st_sb", [1, 4096], F32)
        e.o32_sb = sbufT(es, nc, "o32_sb", [128, 128], F32)
        e.ones_sb = sbufT(es, nc, "ones_sb", [128, 128], BF16)
        e.fl_sb = sbufT(es, nc, "fl_sb", [128, 2], F32)
        if layer == 0:
            gsc_sb = sbufT(es, nc, "gsc_sb", [128, 8], F32)
            usc_sb = sbufT(es, nc, "usc_sb", [128, 8], F32)
            dsc_sb = sbufT(es, nc, "dsc_sb", [128, 16], F32)

        e.ps_a = psumT(es, nc, "ps_a", [128, 512], F32)
        e.ps_b = psumT(es, nc, "ps_b", [128, 512], F32)
        e.ps_c = psumT(es, nc, "ps_c", [128, 512], F32)
        e.ps_d = psumT(es, nc, "ps_d", [128, 512], F32)
        e.ps_ba = psumT(es, nc, "ps_ba", [128, 512], F32)
        e.ps_bb = psumT(es, nc, "ps_bb", [128, 512], F32)
        e.ps_bc = psumT(es, nc, "ps_bc", [128, 512], F32)
        e.ps_bd = psumT(es, nc, "ps_bd", [128, 512], F32)

        if fz:
            sm = fz["sems"]
            ld, ag, wl, xt = sm["ld"], sm["ag"], sm["wl"], sm["xt"]
            e.mm = mm = sm["mm"]; e.ev = ev = sm["ev"]
            e.ex = ex = sm["ex"]; e.so = so = sm["so"]
        else:
            ld = Sem(es, nc, "ld")
            ag = Sem(es, nc, "ag")
            wl = Sem(es, nc, "wl")
            xt = Sem(es, nc, "xt")
            e.mm = mm = Sem(es, nc, "mm")
            e.ev = ev = Sem(es, nc, "ev")
            e.ex = ex = Sem(es, nc, "ex")
            e.so = so = Sem(es, nc, "so")
        sy, gg, ve, sc, te = nc.sync, nc.gpsimd, nc.vector, nc.scalar, nc.tensor

        ld.inc(gg.dma_start(out=agi[:, :], in_=x_h[:, :]), 16)
        ld.inc(sy.dma_start(out=e.ones_sb[:], in_=ones_i[:, :]), 16)
        if layer == 0:
            ld.inc(sy.dma_start(out=gsc_sb[:], in_=gsc_h[:, :]), 16)
            ld.inc(sy.dma_start(out=usc_sb[:], in_=usc_h[:, :]), 16)
            ld.inc(sy.dma_start(out=dsc_sb[:], in_=dsc_h[:, :]), 16)
        gg.wait_ge(ld.h, ld.n)
        ag.inc(gg.collective_compute("AllGather", ALU.bypass,
                                     replica_groups=[list(range(W))],
                                     ins=[agi[:, :]], outs=[xg[:, :]]), 1)
        sy.wait_ge(ag.h, ag.n)
        gg.wait_ge(ag.h, ag.n)

        def xg_win(tt):
            r, cs = tt // 2, 512 * (tt % 2)
            return (xg[r * D:(r + 1) * D, cs:cs + 512]
                    .rearrange("(c p) t -> p c t", p=128))

        xk_mm = [0, 0]
        xt_at = [0, 0]
        g_ex = [0, 0]
        g_ev = [0, 0]
        t_ev = [0, 0]
        o_ev = [0, 0]
        po_so = [0, 0, 0, 0]
        it_mm = [0]
        for sub in range(NSUB):
            if sub > 0:
                gg.wait_ge(mm.h, mm.n)
                ve.wait_ge(mm.h, mm.n)
            if layer == 0:
                def deq_load(w_h, dst_sb, nrows, width, iblk):
                    for ib in range(nrows):
                        h = ib % 2
                        if ib >= 2 or iblk > 0:
                            gg.wait_ge(ev.h, ev.n)
                        wl.inc(gg.dma_start(out=wi_sb[:, h * 2048:h * 2048 + width],
                                            in_=w_h[ib * 128:(ib + 1) * 128, :]), 16)
                        ve.wait_ge(wl.h, wl.n)
                        ev.inc(ve.tensor_scalar(dst_sb[:, ib * width:(ib + 1) * width],
                                                wi_sb[:, h * 2048:h * 2048 + width],
                                                -128.0, None, ALU.add), 1)
                deq_load(wg_h, wg_sb, 16, 1024, 0)
                deq_load(wu_h, wu_sb, 16, 1024, 1)
                deq_load(wd_h, wd_sb, 8, 2048, 2)
            else:
                wl.inc(gg.dma_start(out=wg_sb[:].rearrange("p (c n) -> p c n", c=16),
                                    in_=wg_h[:, sub * 1024:(sub + 1) * 1024]
                                    .rearrange("(c p) n -> p c n", p=128)), 16)
                wl.inc(gg.dma_start(out=wu_sb[:].rearrange("p (c n) -> p c n", c=16),
                                    in_=wu_h[:, sub * 1024:(sub + 1) * 1024]
                                    .rearrange("(c p) n -> p c n", p=128)), 16)
                wl.inc(gg.dma_start(out=wd_sb[:].rearrange("p (c n) -> p c n", c=8),
                                    in_=wd_h[sub * 1024:(sub + 1) * 1024, :]
                                    .rearrange("(c p) n -> p c n", p=128)), 16)
            te.wait_ge(wl.h, wl.n)
            te.wait_ge(ev.h, ev.n)

            for tt in range(16):
                xslot = tt % 2
                sy.wait_ge(mm.h, xk_mm[xslot])
                xt.inc(sy.dma_start(out=xk_sb[:, xslot * 8192:(xslot + 1) * 8192]
                                    .rearrange("p (c t) -> p c t", c=16),
                                    in_=xg_win(tt)), 16)
                xt_at[xslot] = xt.n
                xkv = xk_sb[:, xslot * 8192:(xslot + 1) * 8192]
                te.wait_ge(xt.h, xt_at[xslot])
                for nch in range(8):
                    r = nch % 2
                    psg = (e.ps_a, e.ps_b)[r]
                    psu = (e.ps_c, e.ps_d)[r]
                    te.wait_ge(ex.h, g_ex[r])
                    te.wait_ge(ev.h, g_ev[r])
                    for kc in range(16):
                        inst = te.matmul(psg[:, :],
                                         wg_sb[:, kc * 1024 + nch * 128:
                                               kc * 1024 + nch * 128 + 128],
                                         xkv[:, kc * 512:(kc + 1) * 512],
                                         start=(kc == 0), stop=(kc == 15))
                    for kc in range(16):
                        inst = te.matmul(psu[:, :],
                                         wu_sb[:, kc * 1024 + nch * 128:
                                               kc * 1024 + nch * 128 + 128],
                                         xkv[:, kc * 512:(kc + 1) * 512],
                                         start=(kc == 0), stop=(kc == 15))
                    mm.inc(inst, 1)
                    sc.wait_ge(mm.h, mm.n)
                    sc.wait_ge(ev.h, t_ev[r])
                    t1v = t1_sb[:, r * 512:(r + 1) * 512]
                    t2v = t2_sb[:, r * 512:(r + 1) * 512]
                    ve.wait_ge(mm.h, it_mm[0])
                    if layer == 0:
                        ex.inc(sc.activation(t1v, psg[:, :], AF.Silu,
                                             scale=gsc_sb[:, sub * 8 + nch:
                                                          sub * 8 + nch + 1]), 1)
                        ve.wait_ge(ex.h, ex.n)
                        ev.inc(ve.scalar_tensor_tensor(
                            it_sb[:, nch * 512:(nch + 1) * 512], psu[:, :],
                            usc_sb[:, sub * 8 + nch:sub * 8 + nch + 1],
                            t1v, ALU.mult, ALU.mult), 1)
                    else:
                        ex.inc(sc.activation(t1v, psg[:, :], AF.Square), 1)
                        ve.wait_ge(ex.h, ex.n)
                        vw(e, ve.tensor_scalar(t2v, t1v,
                                               0.7978845608028654 * 0.044715,
                                               0.7978845608028654, ALU.mult, ALU.add))
                        vw(e, ve.tensor_tensor(t2v, t2v, psg[:, :], ALU.mult))
                        sc.wait_ge(ev.h, ev.n)
                        ex.inc(sc.activation(t2v, t2v, AF.Tanh), 1)
                        ve.wait_ge(ex.h, ex.n)
                        vw(e, ve.scalar_tensor_tensor(t2v, t2v, 1.0,
                                                      psg[:, :], ALU.add, ALU.mult))
                        ev.inc(ve.scalar_tensor_tensor(
                            it_sb[:, nch * 512:(nch + 1) * 512], psu[:, :], 0.5,
                            t2v, ALU.mult, ALU.mult), 1)
                    g_ex[r] = ex.n
                    g_ev[r] = ev.n
                    t_ev[r] = ev.n
                for ht in range(16):
                    r = ht % 2
                    pso = (e.ps_ba, e.ps_bb)[r]
                    te.wait_ge(ev.h, o_ev[r])
                    if ht == 0:
                        te.wait_ge(ev.h, ev.n)
                    for nch in range(8):
                        inst = te.matmul(pso[:, :],
                                         wd_sb[:, nch * D + ht * 128:
                                               nch * D + ht * 128 + 128],
                                         it_sb[:, nch * 512:(nch + 1) * 512],
                                         start=(nch == 0), stop=(nch == 7))
                    mm.inc(inst, 1)
                    ve.wait_ge(mm.h, mm.n)
                    ps4 = ht % 4
                    ve.wait_ge(so.h, po_so[ps4])
                    pc = po_sb[:, ps4 * 512:ps4 * 512 + 512]
                    if layer == 0:
                        ev.inc(ve.tensor_scalar(pc, pso[:, :],
                                                dsc_sb[:, ht:ht + 1], None,
                                                ALU.mult), 1)
                    else:
                        ev.inc(ve.tensor_copy(pc, pso[:, :]), 1)
                    o_ev[r] = ev.n
                    gg.wait_ge(ev.h, ev.n)
                    rr, cs = tt // 2, 512 * (tt % 2)
                    dst = part[rr * D + ht * 128:rr * D + ht * 128 + 128, cs:cs + 512]
                    if sub == 0:
                        so.inc(gg.dma_start(out=dst, in_=pc), 16)
                    else:
                        so.inc(gg.dma_start(out=dst, in_=pc,
                                            accum_op=ALU.add), 16)
                    po_so[ps4] = so.n
                xk_mm[xslot] = mm.n
                it_mm[0] = mm.n
        gg.wait_ge(so.h, so.n)
        ag.inc(gg.collective_compute("ReduceScatter", ALU.add,
                                     replica_groups=[list(range(W))],
                                     ins=[part[:, :]], outs=[rso[:, :]]), 1)
        sy.wait_ge(ag.h, ag.n)
        if debug_ffn:
            gg.wait_ge(ag.h, ag.n)
            so.inc(gg.dma_start(out=drs_h[:, :], in_=rso[:, :]), 16)
        ld.inc(sy.dma_start(out=e.xr_sb[:].rearrange("p (c t) -> p c t", c=16),
                            in_=x_h.rearrange("(c p) t -> p c t", p=128)), 16)
        for dt in range(16):
            if dt > 0:
                sy.wait_ge(ev.h, ev.n)
            ld.inc(sy.dma_start(out=e.xo_sb[:, 0:1024],
                                in_=rso[dt * 128:(dt + 1) * 128, :]), 16)
            ve.wait_ge(ld.h, ld.n)
            ev.inc(ve.tensor_tensor(e.xr_sb[:, dt * 1024:(dt + 1) * 1024],
                                    e.xr_sb[:, dt * 1024:(dt + 1) * 1024],
                                    e.xo_sb[:, 0:1024], ALU.add), 1)
        emit_ln(e, xn_h)
        gg.wait_ge(so.h, so.n)
    return nc


def emit_decode(e, pk, dlo, dhi, tmp6):
    ve = e.nc.vector
    a, b, u, a2, b2, u2 = tmp6
    ev = e.ev
    ve.tensor_single_scalar(a[:, :], pk[:, :], 7, ALU.bitwise_and)
    ve.tensor_scalar(a2[:, :], pk[:, :], 4, 7, ALU.logical_shift_right,
                     ALU.bitwise_and)
    ve.tensor_scalar(b[:, :], pk[:, :], 8, 12, ALU.bitwise_and,
                     ALU.logical_shift_left)
    vw(e, ve.tensor_scalar(b2[:, :], pk[:, :], 128, 8, ALU.bitwise_and,
                           ALU.logical_shift_left))
    e.pk_done = ev.n
    ve.tensor_single_scalar(a[:, :], a[:, :], 252, ALU.add)
    vw(e, ve.tensor_single_scalar(a2[:, :], a2[:, :], 252, ALU.add))
    ve.tensor_single_scalar(u[:, :], a[:, :], 253, ALU.is_equal)
    vw(e, ve.tensor_single_scalar(u2[:, :], a2[:, :], 253, ALU.is_equal))
    ve.tensor_tensor(u[:, :], a[:, :], u[:, :], ALU.subtract)
    vw(e, ve.tensor_tensor(u2[:, :], a2[:, :], u2[:, :], ALU.subtract))
    ve.tensor_scalar(a[:, :], a[:, :], 253, 64, ALU.is_ge, ALU.mult)
    vw(e, ve.tensor_scalar(a2[:, :], a2[:, :], 253, 64, ALU.is_ge, ALU.mult))
    ve.tensor_tensor(u[:, :], u[:, :], a[:, :], ALU.mult)
    vw(e, ve.tensor_tensor(u2[:, :], u2[:, :], a2[:, :], ALU.mult))
    ve.tensor_tensor(u[:, :], u[:, :], b[:, :], ALU.bitwise_or)
    vw(e, ve.tensor_tensor(u2[:, :], u2[:, :], b2[:, :], ALU.bitwise_or))
    ev.inc(ve.tensor_copy(dlo[:, :].bitcast(mybir.dt.uint16), u[:, :]), 1)
    ev.inc(ve.tensor_copy(dhi[:, :].bitcast(mybir.dt.uint16), u2[:, :]), 1)


def build_decode(fz=None):
    pfx = fz["pfx"] if fz else ""
    _NPFX[0] = pfx
    nc = fz["nc"] if fz else bass.Bass("TRN2", target_bir_lowering=False, num_devices=W)
    e = SimpleNamespace()
    e.nc = nc
    ins, outs, scs = {}, {}, {}
    for nm in ("g", "u", "d"):
        ins[nm] = nc.dram_tensor(pfx + nm + "pk", [1024, 2048], I32, kind="ExternalInput").ap()
        scs[nm] = nc.dram_tensor(pfx + nm + "scT", [64, 2048], F32, kind="ExternalInput").ap()
        if fz:
            outs[nm] = fz[nm + "dec"]
        else:
            outs[nm] = nc.dram_tensor(nm + "dec", [2048, 2048], BF16, kind="ExternalOutput").ap()
    G_np = np.zeros((8, 128), dtype=np.float32)
    for a in range(8):
        G_np[a, a * 16:(a + 1) * 16] = 1
    G_i = nc.inline_tensor(G_np, name=pfx + "G_c").ap()

    es = contextlib.ExitStack()
    with es:
        pk_sb = sbufT(es, nc, "pk_sb", [128, 2 * 2048], I32)
        m_sb = sbufT(es, nc, "m_sb", [128, 2048], I32)
        t_sb = sbufT(es, nc, "t_sb", [128, 2048], I32)
        u_sb = sbufT(es, nc, "u_sb", [128, 2048], I32)
        m2_sb = sbufT(es, nc, "m2_sb", [128, 2048], I32)
        t2_sb = sbufT(es, nc, "t2_sb", [128, 2048], I32)
        u2_sb = sbufT(es, nc, "u2_sb", [128, 2048], I32)
        dlo_sb = sbufT(es, nc, "dlo_sb", [128, 2048], BF16)
        dhi_sb = sbufT(es, nc, "dhi_sb", [128, 2048], BF16)
        olo_sb = sbufT(es, nc, "olo_sb", [128, 2 * 2048], BF16)
        ohi_sb = sbufT(es, nc, "ohi_sb", [128, 2 * 2048], BF16)
        sc_sb = sbufT(es, nc, "sc_sb", [8, 2 * 2048], F32)
        G_sb = sbufT(es, nc, "G_sb", [8, 128], F32)
        ps_sc = psumT(es, nc, "ps_sc", [128, 512], F32)
        ps_s2 = psumT(es, nc, "ps_s2", [128, 512], F32)

        if fz:
            sm = fz["sems"]
            ld = sm["ld"]
            e.mm = mm = sm["mm"]; e.ev = ev = sm["ev"]; e.so = so = sm["so"]
        else:
            ld = Sem(es, nc, "ld")
            e.mm = mm = Sem(es, nc, "mm")
            e.ev = ev = Sem(es, nc, "ev")
            e.so = so = Sem(es, nc, "so")
        sy, gg, ve, te = nc.sync, nc.gpsimd, nc.vector, nc.tensor

        ld.inc(sy.dma_start(out=G_sb[:], in_=G_i[:, :]), 16)
        its = [(nm, rt) for nm in ("g", "u", "d") for rt in range(8)]
        pk_ev = [0, 0]
        sc_mm = [0, 0]
        ld_mark = [0, 0]
        out_so = [0, 0]
        psc_ev = [0, 0]

        def emit_loads(it):
            nm, rt = its[it]
            h = it % 2
            sy.wait_ge(ev.h, pk_ev[h])
            sy.wait_ge(mm.h, sc_mm[h])
            ld.inc(sy.dma_start(out=pk_sb[:, h * 2048:(h + 1) * 2048],
                                in_=ins[nm][rt * 128:(rt + 1) * 128, :]), 16)
            ld.inc(sy.dma_start(out=sc_sb[0:8, h * 2048:(h + 1) * 2048],
                                in_=scs[nm][rt * 8:(rt + 1) * 8, :]), 16)
            ld_mark[h] = ld.n

        emit_loads(0)
        for it in range(24):
            nm, rt = its[it]
            h = it % 2
            if it + 1 < 24:
                emit_loads(it + 1)
            pk = pk_sb[:, h * 2048:(h + 1) * 2048]
            ve.wait_ge(ld.h, ld_mark[h])
            ve.wait_ge(so.h, out_so[h])
            emit_decode(e, pk, dlo_sb, dhi_sb,
                        (m_sb, t_sb, u_sb, m2_sb, t2_sb, u2_sb))
            pk_ev[h] = e.pk_done
            te.wait_ge(ld.h, ld_mark[h])
            for cb in range(4):
                psc = (ps_sc, ps_s2)[cb % 2]
                te.wait_ge(ev.h, psc_ev[cb % 2])
                mm.inc(te.matmul(psc[:, :], G_sb[0:8, :],
                                 sc_sb[0:8, h * 2048 + cb * 512:
                                       h * 2048 + cb * 512 + 512],
                                 start=True, stop=True), 1)
                ve.wait_ge(mm.h, mm.n)
                cbs = slice(h * 2048 + cb * 512, h * 2048 + (cb + 1) * 512)
                dbs = slice(cb * 512, (cb + 1) * 512)
                ev.inc(ve.tensor_tensor(olo_sb[:, cbs], dlo_sb[:, dbs],
                                        psc[:, :], ALU.mult), 1)
                ev.inc(ve.tensor_tensor(ohi_sb[:, cbs], dhi_sb[:, dbs],
                                        psc[:, :], ALU.mult), 1)
                psc_ev[cb % 2] = ev.n
            sc_mm[h] = mm.n
            gg.wait_ge(ev.h, ev.n)
            ool = olo_sb[:, h * 2048:(h + 1) * 2048]
            ooh = ohi_sb[:, h * 2048:(h + 1) * 2048]
            if nm == "d":
                so.inc(gg.dma_start(out=outs[nm][(2 * rt) * 128:(2 * rt + 1) * 128, :],
                                    in_=ool), 16)
                so.inc(gg.dma_start(out=outs[nm][(2 * rt + 1) * 128:(2 * rt + 2) * 128, :],
                                    in_=ooh), 16)
            else:
                so.inc(gg.dma_start(out=outs[nm][256 * rt:256 * rt + 256:2, :],
                                    in_=ool), 16)
                so.inc(gg.dma_start(out=outs[nm][256 * rt + 1:256 * rt + 256:2, :],
                                    in_=ooh), 16)
            out_so[h] = so.n
        gg.wait_ge(so.h, so.n)
    return nc


def l1_perm(c):
    perm = np.zeros(2048, dtype=np.int64)
    for ci in range(16):
        for p in range(128):
            perm[ci * 128 + p] = 2048 * c + 256 * (ci // 2) + 2 * p + (ci % 2)
    return perm


def host_decode_inputs(inp):
    gp = np.asarray(inp["L1_ffn_gate_W_packed"])
    up = np.asarray(inp["L1_ffn_up_W_packed"])
    dp = np.asarray(inp["L1_ffn_down_W_packed"])
    gs = np.asarray(inp["L1_ffn_gate_scales"])
    us = np.asarray(inp["L1_ffn_up_scales"])
    ds = np.asarray(inp["L1_ffn_down_scales"])
    gpT = gp.T
    upT = up.T
    dpT = dp.T
    maps = []
    for c in range(W):
        perm = l1_perm(c)
        m = {
            "gpk": np.ascontiguousarray(gpT[:, perm]),
            "upk": np.ascontiguousarray(upT[:, perm]),
            "dpk": np.ascontiguousarray(dpT[1024 * c:1024 * (c + 1), :]),
            "gscT": np.ascontiguousarray(gs[perm, :].T),
            "uscT": np.ascontiguousarray(us[perm, :].T),
            "dscT": np.ascontiguousarray(ds[:, 64 * c:64 * (c + 1)].T),
        }
        maps.append(m)
    return maps


def host_oln_inputs(layer, inp, xcur, ys):
    x = np.asarray(xcur, dtype=np.float32).reshape(T, D)
    wo = np.asarray(inp["L0_attn_Wo"]).T if layer == 0 else np.asarray(inp["L1_attn_Wo"])
    wo = np.ascontiguousarray(wo.astype(np.float32))
    maps = []
    for c in range(W):
        yfm = np.ascontiguousarray(
            np.concatenate([np.asarray(ys[r])[:, c * TOK:(c + 1) * TOK]
                            for r in range(W)], axis=0))
        maps.append({"y": yfm, "wo": wo,
                     "x_own": np.ascontiguousarray(x[c * TOK:(c + 1) * TOK].T)})
    return maps


def host_ffn_inputs(layer, inp, xcur, decoded=None):
    x = np.asarray(xcur, dtype=np.float32).reshape(T, D)
    maps = []
    for c in range(W):
        m = {"x_own": np.ascontiguousarray(x[c * TOK:(c + 1) * TOK].T)}
        if layer == 0:
            s = slice(1024 * c, 1024 * (c + 1))
            m["wg"] = np.ascontiguousarray(np.asarray(inp["L0_ffn_gate_w"])[s].T)
            m["wu"] = np.ascontiguousarray(np.asarray(inp["L0_ffn_up_w"])[s].T)
            m["wd"] = np.ascontiguousarray(np.asarray(inp["L0_ffn_down_w"])[:, s].T)
            m["gsc"] = np.ascontiguousarray(
                np.asarray(inp["L0_ffn_gate_scale"])[s].reshape(8, 128).T)
            m["usc"] = np.ascontiguousarray(
                np.asarray(inp["L0_ffn_up_scale"])[s].reshape(8, 128).T)
            m["dsc"] = np.ascontiguousarray(
                np.asarray(inp["L0_ffn_down_scale"]).reshape(16, 128).T)
        else:
            m["wg"] = decoded[c]["gdec"]
            m["wu"] = decoded[c]["udec"]
            m["wd"] = decoded[c]["ddec"]
        maps.append(m)
    return maps




def build_fused():
    nc = bass.Bass("TRN2", target_bir_lowering=False, num_devices=W)
    es_top = contextlib.ExitStack()
    nc._fused_es = es_top
    sems = {nm: Sem(es_top, nc, nm) for nm in
            ("ld", "cv", "ag", "xt", "wl", "mm", "ev", "ex", "so")}

    x0 = nc.dram_tensor("x_own", [D, TOK], F32, kind="ExternalInput").ap()
    out_h = nc.dram_tensor("xn", [D, TOK], F32, kind="ExternalOutput").ap()
    agi = nc.dram_tensor("agi", [D, TOK], BF16, kind="Internal").ap()
    xg = nc.dram_tensor("xg", [W * D, TOK], BF16, kind="Internal").ap()
    part = nc.dram_tensor("part", [W * D, TOK], F32, kind="Internal").ap()
    rso = nc.dram_tensor("rso", [D, TOK], F32, kind="Internal").ap()
    ytm = nc.dram_tensor("ytm", [T, 256], BF16, kind="Internal").ap()
    ygath = nc.dram_tensor("ygath", [W * T, 256], BF16, kind="Internal").ap()
    x1 = nc.dram_tensor("x1", [D, TOK], F32, kind="Internal").ap()
    x2 = nc.dram_tensor("x2", [D, TOK], F32, kind="Internal").ap()
    x3 = nc.dram_tensor("x3", [D, TOK], F32, kind="Internal").ap()
    gdec = nc.dram_tensor("gdec", [2048, 2048], BF16, kind="Internal").ap()
    udec = nc.dram_tensor("udec", [2048, 2048], BF16, kind="Internal").ap()
    ddec = nc.dram_tensor("ddec", [2048, 2048], BF16, kind="Internal").ap()

    base = dict(nc=nc, sems=sems, agi=agi, xg=xg, part=part, rso=rso,
                ytm=ytm, ygath=ygath)

    with nc.named_scope("dec"):
        build_decode(fz={**base, "pfx": "dc_", "gdec": gdec, "udec": udec, "ddec": ddec})
        nc.all_engine_barrier()
    with nc.named_scope("a0"):
        build_attn(0, fz={**base, "pfx": "a0_", "x_ap": x0})
        nc.all_engine_barrier()
    with nc.named_scope("o0"):
        build_o_ln(fz={**base, "pfx": "o0_", "x_ap": x0, "out_ap": x1})
        nc.all_engine_barrier()
    with nc.named_scope("f0"):
        build_ffn(0, fz={**base, "pfx": "f0_", "x_ap": x1, "out_ap": x2})
        nc.all_engine_barrier()
    with nc.named_scope("a1"):
        build_attn(1, fz={**base, "pfx": "a1_", "x_ap": x2})
        nc.all_engine_barrier()
    with nc.named_scope("o1"):
        build_o_ln(fz={**base, "pfx": "o1_", "x_ap": x2, "out_ap": x3})
        nc.all_engine_barrier()
    with nc.named_scope("f1"):
        build_ffn(1, fz={**base, "pfx": "f1_", "x_ap": x3,
                         "wg_ap": gdec, "wu_ap": udec, "wd_ap": ddec,
                         "out_ap": out_h})
    es_top.close()
    return nc


def host_fused_inputs(inp):
    x = np.asarray(inp["L0_attn_x"], dtype=np.float32).reshape(T, D)
    a0 = host_attn_inputs(0, inp)
    a1 = host_attn_inputs(1, inp)
    dc = host_decode_inputs(inp)
    f0 = host_ffn_inputs(0, inp, x)
    wo0 = np.ascontiguousarray(np.asarray(inp["L0_attn_Wo"]).T.astype(np.float32))
    wo1 = np.ascontiguousarray(np.asarray(inp["L1_attn_Wo"]).astype(np.float32))
    maps = []
    for c in range(W):
        yidx = np.zeros((128, 64), dtype=np.int32)
        for r in range(W):
            for tt in range(8):
                yidx[:, r * 8 + tt] = r * T + c * TOK + tt * 128 + np.arange(128)
        m = {"x_own": np.ascontiguousarray(x[c * TOK:(c + 1) * TOK].T)}
        for k, v in a0[c].items():
            if k != "x_own":
                m["a0_" + k] = v
        for k, v in a1[c].items():
            if k != "x_own":
                m["a1_" + k] = v
        for k, v in dc[c].items():
            m["dc_" + k] = v
        for k, v in f0[c].items():
            if k != "x_own":
                m["f0_" + k] = v
        m["o0_wo"] = wo0
        m["o1_wo"] = wo1
        m["o0_yidx"] = yidx
        m["o1_yidx"] = yidx
        maps.append(m)
    return maps


def kernel_fused(**inputs):
    nc = build_fused()
    maps = host_fused_inputs(inputs)
    res = run_bass_kernel_spmd(nc, maps, core_ids=list(range(W)))
    x4 = _gather_xn(res.results)
    return np.ascontiguousarray(x4.reshape(B, TB, D).astype(np.float32))


def _run(nc, maps):
    res = run_bass_kernel_spmd(nc, maps, core_ids=list(range(W)))
    return res.results


def _gather_xn(res):
    return np.concatenate([np.asarray(res[c]["xn"]).T for c in range(W)], axis=0)


def kernel_multi(**inputs):
    inp = inputs
    x0 = np.asarray(inp["L0_attn_x"], dtype=np.float32)

    res = _run(build_attn(0), host_attn_inputs(0, {**inp, "L0_attn_x": x0}))
    ys = [np.asarray(res[c]["y"]) for c in range(W)]
    res = _run(build_o_ln(), host_oln_inputs(0, inp, x0, ys))
    x1 = _gather_xn(res)

    res = _run(build_ffn(0), host_ffn_inputs(0, inp, x1))
    x2 = _gather_xn(res)

    res = _run(build_attn(1), host_attn_inputs(1, {**inp, "L0_attn_x": x2}))
    ys = [np.asarray(res[c]["y"]) for c in range(W)]
    res = _run(build_o_ln(), host_oln_inputs(1, inp, x2, ys))
    x3 = _gather_xn(res)

    res = _run(build_decode(), host_decode_inputs(inp))
    decoded = [{k: np.asarray(res[c][k]) for k in ("gdec", "udec", "ddec")}
               for c in range(W)]
    res = _run(build_ffn(1), host_ffn_inputs(1, inp, x3, decoded))
    x4 = _gather_xn(res)

    return np.ascontiguousarray(x4.reshape(B, TB, D).astype(np.float32))


def kernel(**inputs):
    return kernel_fused(**inputs)
